# version 02
# baseline (speedup 1.0000x reference)
import numpy as np
import ml_dtypes

import concourse.bass as bass
import concourse.mybir as mybir
from concourse.bass_utils import run_bass_kernel_spmd

F32 = mybir.dt.float32
BF16 = mybir.dt.bfloat16
I32 = mybir.dt.int32
AF = mybir.ActivationFunctionType
ALU = mybir.AluOpType
AX = mybir.AxisListType

P = 128
NBATCH = 2
S = 2048
D = 1024
NB = S // P
EPS = 1e-5
S127 = 1.0 / 127.0
S2_FFN = S127 * S127

CONSTS = [EPS, -0.5, 0.125, 1.0 / D, 1.0, -1.0, S127, S2_FFN,
          1.0 / 32768.0, 0.0, 4.0, -4.0, 0.5,
          S2_FFN / D, (S2_FFN * S2_FFN) / D]

ENGS = ("tensor", "vector", "scalar", "pool", "sp")
DMA_QUEUES = {"sp": "sp", "pool": "pool"}
DMA_RING = 12


class Prog:

    def __init__(self, nc):
        self.nc = nc
        from contextlib import ExitStack
        self.ctx = ExitStack()
        self.eng = {"tensor": nc.tensor, "vector": nc.vector,
                    "scalar": nc.scalar, "pool": nc.gpsimd, "sp": nc.sync}
        self.esem = {e: self.ctx.enter_context(nc.semaphore(f"es_{e}")) for e in ENGS}
        self.ecnt = {e: 0 for e in ENGS}
        self.qsem = {q: [self.ctx.enter_context(nc.semaphore(f"q{q}{i}"))
                         for i in range(DMA_RING)] for q in DMA_QUEUES}
        self.qcnt = {q: 0 for q in DMA_QUEUES}
        self.quses = {q: [0] * DMA_RING for q in DMA_QUEUES}
        self.waited = {}
        self.res = {}
        self.n_waits = 0
        self.consts_tile = None

    def _stream_of(self, tok):
        return ("eng", tok[1]) if tok[0] == "eng" else ("dmaq", tok[1])

    def _wait(self, stream, tok):
        if tok[0] == "eng":
            sem, val = self.esem[tok[1]], tok[2]
        else:
            _, q, slot, uses = tok
            sem, val = self.qsem[q][slot], 16 * uses
        key = (stream, sem.name)
        if self.waited.get(key, 0) >= val:
            return
        eng = stream[1] if stream[0] == "eng" else DMA_QUEUES[stream[1]]
        self.eng[eng].wait_ge(sem, val)
        self.waited[key] = val
        self.n_waits += 1

    def _pre(self, stream, reads, writes):
        is_dma = stream[0] == "dmaq"
        for r in reads:
            e = self.res.get(r)
            if e and e["w"] is not None:
                w = e["w"]
                same = self._stream_of(w) == stream
                if (not same) or is_dma:
                    self._wait(stream, w)
                elif stream != ("eng", "tensor") and w[0] == "eng":
                    self._wait(stream, w)
        for wname in writes:
            e = self.res.get(wname)
            if not e:
                continue
            if e["w"] is not None and (self._stream_of(e["w"]) != stream or is_dma):
                self._wait(stream, e["w"])
            for rt in e["r"]:
                if self._stream_of(rt) != stream or is_dma:
                    self._wait(stream, rt)

    def _post(self, tok, reads, writes):
        for r in reads:
            self.res.setdefault(r, {"w": None, "r": []})["r"].append(tok)
        for w in writes:
            e = self.res.setdefault(w, {"w": None, "r": []})
            e["w"], e["r"] = tok, []

    def op(self, eng_name, fn, reads=(), writes=()):
        stream = ("eng", eng_name)
        self._pre(stream, reads, writes)
        inst = fn(self.eng[eng_name])
        self.ecnt[eng_name] += 1
        tok = ("eng", eng_name, self.ecnt[eng_name])
        inst.then_inc(self.esem[eng_name], 1)
        self._post(tok, reads, writes)
        return tok

    def dma(self, q, out, in_, reads=(), writes=(), **kw):
        stream = ("dmaq", q)
        slot = self.qcnt[q] % DMA_RING
        prev = self.quses[q][slot]
        if prev > 0:
            self._wait(stream, ("dma", q, slot, prev))
        self._pre(stream, reads, writes)
        inst = self.eng[DMA_QUEUES[q]].dma_start(out=out, in_=in_, **kw)
        self.qcnt[q] += 1
        self.quses[q][slot] += 1
        tok = ("dma", q, slot, self.quses[q][slot])
        inst.then_inc(self.qsem[q][slot], 16)
        self._post(tok, reads, writes)
        return tok

    def finish(self):
        stream = ("eng", "pool")
        for e in ENGS:
            if self.ecnt[e]:
                self._wait(stream, ("eng", e, self.ecnt[e]))
        for q in DMA_QUEUES:
            for s in range(DMA_RING):
                if self.quses[q][s]:
                    self._wait(stream, ("dma", q, s, self.quses[q][s]))
        self.eng["pool"].nop()

    def sbuf(self, name, shape, dtype):
        return self.ctx.enter_context(self.nc.sbuf_tensor(name, list(shape), dtype))

    def psum(self, name, shape, dtype=F32):
        return self.ctx.enter_context(self.nc.psum_tensor(name, list(shape), dtype))

    def setup_consts(self):
        cst = self.sbuf("consts", (P, len(CONSTS)), F32)
        for j, v in enumerate(CONSTS):
            self.nc.const_aps.register(F32, float(v), cst[:, j:j + 1])
        self.consts_tile = cst
        return cst

    def cst(self, v):
        return self.consts_tile[:, CONSTS.index(v):CONSTS.index(v) + 1]

    def cstn(self, v, n):
        j = CONSTS.index(v)
        return self.consts_tile[0:n, j:j + 1]

    def cstb(self, v, base, n=64):
        j = CONSTS.index(v)
        return self.consts_tile[base:base + n, j:j + 1]


def bcastF(col_ap, n):
    return bass.AP(col_ap.tensor, col_ap.offset, [list(col_ap.ap[0]), [0, n]])



def _rope_tables():
    dim, base = 64, 10000.0
    inv = 1.0 / base ** (np.arange(0, dim, 2, dtype=np.float32) / dim)
    t = np.arange(S, dtype=np.float32)
    fr = np.outer(t, inv)
    cos = np.concatenate([np.cos(fr), np.cos(fr)], -1)[:, ::2]
    sin = np.concatenate([np.sin(fr), np.sin(fr)], -1)[:, ::2]
    return cos.astype(np.float32), sin.astype(np.float32)


def _deint_perm():
    p = []
    for h in range(16):
        p.extend(64 * h + np.arange(0, 64, 2))
        p.extend(64 * h + np.arange(1, 64, 2))
    return np.array(p)


def prep_shared(inputs):
    f32 = np.float32
    C = np.ascontiguousarray
    g = {}
    g["wdqT"] = C(inputs["L0_attn_w_down_q"].T)
    g["wdkvT"] = C(inputs["L0_attn_w_down_kv"].T)
    perm = _deint_perm()
    g["wupqT"] = C(inputs["L0_attn_w_up_q"][perm].T)
    g["wupkT"] = C(inputs["L0_attn_w_up_k"][perm].T)
    g["wupvT"] = C(inputs["L0_attn_w_up_v"].T)
    g["woutT"] = C(inputs["L0_attn_w_out"].T)
    g["lnsB"] = np.tile(np.asarray(inputs["L0_attn_ln_scale"], f32)[None, :], (P, 1))
    g["lnbB"] = np.tile(np.asarray(inputs["L0_attn_ln_bias"], f32)[None, :], (P, 1))
    cos, sin = _rope_tables()
    g["ropeC"] = C(cos.reshape(NB, P, 32).transpose(1, 0, 2))
    g["ropeS"] = C(sin.reshape(NB, P, 32).transpose(1, 0, 2))
    g["WrT"] = C(inputs["L0_ffn_Wr"].T)
    g["W1"] = C(np.asarray(inputs["L0_ffn_W1"])[7])
    g["b1c"] = C(np.asarray(inputs["L0_ffn_b1"], f32)[7].reshape(32, P).T)
    g["W2"] = C(np.asarray(inputs["L0_ffn_W2"])[7])
    g["b2B"] = np.tile(np.asarray(inputs["L0_ffn_b2"], f32)[7][None, :], (P, 1))
    g["WqT"] = C(inputs["L1_attn_Wq"].T)
    g["WkT"] = C(inputs["L1_attn_Wk"].T)
    g["WvT"] = C(inputs["L1_attn_Wv"].T)
    g["WpT"] = C(inputs["L1_attn_Wp"].T)
    g["bpB"] = np.tile(np.asarray(inputs["L1_attn_bp"], f32)[None, :], (P, 1))
    gam = np.stack([np.asarray(inputs["L1_attn_g" + b], f32) for b in "qkv"])
    bet = np.stack([np.asarray(inputs["L1_attn_b" + b], f32) for b in "qkv"])
    g["bn_gam"] = C(gam.reshape(3, 8, P).transpose(2, 0, 1))
    g["bn_bet"] = C(bet.reshape(3, 8, P).transpose(2, 0, 1))
    g["wgT"] = C(np.asarray(inputs["L1_ffn_w_gate"]).T)
    g["wuT"] = C(np.asarray(inputs["L1_ffn_w_up"]).T)
    g["wdT"] = C(np.asarray(inputs["L1_ffn_w_down"]).T)
    g["ident"] = np.eye(P, dtype=ml_dtypes.bfloat16)
    g["constv"] = np.tile(np.array(CONSTS, f32), (P, 1))
    return g



class TinyStats:

    def __init__(self, p, name, ncols=8):
        self.t = p.sbuf(name, (P, ncols), F32)
        self.name = name

    def col(self, j):
        return self.t[:, j:j + 1]


def emit_boundary_chunk(p, cfg, b, tc, y_psum, y_psum_res):
    scr = cfg["scr"]
    u = scr["u"]; xold = scr["xold"][tc % 2]; xnew = scr["xnew"]
    sqj = scr["sqj"]; st = cfg["st"]
    uref = "bscr_u"
    xo_res = f"xold{tc % 2}"
    parts = y_psum if isinstance(y_psum, list) else [(y_psum, y_psum_res, 0, D)]
    if cfg.get("_u_ready"):
        pass
    elif cfg.get("trec") is not None:
        r1, nb1 = cfg["trec"]
        tr = scr["trec"]
        p.op("scalar", lambda s: s.activation(
            tr[:], xold[:], AF.Identity,
            scale=r1[:, tc:tc + 1], bias=nb1[:, tc:tc + 1]),
            reads=[xo_res, cfg["trec_res"]], writes=["bscr_trec"])
        for ap, res, c0, c1 in parts:
            p.op("vector", lambda v, ap=ap, c0=c0, c1=c1: v.tensor_add(
                u[:, c0:c1], ap, tr[:, c0:c1]),
                reads=[res, "bscr_trec"], writes=[uref])
    elif cfg.get("pre_add") is not None:
        pap, pres = cfg["pre_add"]
        for ap, res, c0, c1 in parts:
            p.op("vector", lambda v, ap=ap, c0=c0, c1=c1: v.tensor_add(
                u[:, c0:c1], ap, pap[:, c0:c1]),
                reads=[res, pres], writes=[uref])
    else:
        for ap, res, c0, c1 in parts:
            p.op("vector", lambda v, ap=ap, c0=c0, c1=c1: v.tensor_copy(
                u[:, c0:c1], ap),
                reads=[res], writes=[uref])
    su, sq = st.col(0), st.col(1)
    p.op("vector", lambda v: v.reduce_sum(su, u[:], axis=AX.X),
         reads=[uref], writes=["st_su"])
    p.op("scalar", lambda s: s.activation(sqj[:], u[:], AF.Square, accum_out=sq),
         reads=[uref], writes=["st_sq", "bscr_sqj"])
    alpha = cfg.get("alpha")
    m_y, e2_y, var, r, scaleA, nb_ = (st.col(2), st.col(3), st.col(4),
                                      st.col(5), st.col(6), st.col(7))
    cD = p.cst(1.0 / D)
    if alpha is None:
        p.op("vector", lambda v: v.tensor_scalar(m_y, su, cD, None, ALU.mult),
             reads=["st_su"], writes=["st_m"])
        p.op("vector", lambda v: v.tensor_scalar(e2_y, sq, cD, None, ALU.mult),
             reads=["st_sq"], writes=["st_e2"])
    elif isinstance(alpha, tuple) and alpha[0] == "col":
        acol, ares = alpha[1], alpha[2]
        p.op("vector", lambda v: v.tensor_scalar(m_y, su, cD, None, ALU.mult),
             reads=["st_su"], writes=["st_m"])
        p.op("vector", lambda v: v.tensor_mul(m_y, m_y, acol),
             reads=["st_m", ares], writes=["st_m"])
        p.op("vector", lambda v: v.tensor_scalar(e2_y, sq, cD, None, ALU.mult),
             reads=["st_sq"], writes=["st_e2"])
        p.op("vector", lambda v: v.tensor_mul(e2_y, e2_y, acol),
             reads=["st_e2", ares], writes=["st_e2"])
        p.op("vector", lambda v: v.tensor_mul(e2_y, e2_y, acol),
             reads=["st_e2", ares], writes=["st_e2"])
    else:
        _, ca, ca_m, ca_e = alpha
        p.op("vector", lambda v: v.tensor_scalar(m_y, su, p.cst(ca_m), None, ALU.mult),
             reads=["st_su"], writes=["st_m"])
        p.op("vector", lambda v: v.tensor_scalar(e2_y, sq, p.cst(ca_e), None, ALU.mult),
             reads=["st_sq"], writes=["st_e2"])
    p.op("vector", lambda v: v.tensor_mul(var, m_y, m_y),
         reads=["st_m"], writes=["st_var"])
    p.op("vector", lambda v: v.tensor_sub(var, e2_y, var),
         reads=["st_e2", "st_var"], writes=["st_var"])
    p.op("scalar", lambda s: s.activation(r, var, AF.Ln, bias=p.cst(EPS)),
         reads=["st_var"], writes=["st_r"])
    p.op("scalar", lambda s: s.activation(r, r, AF.Exp, scale=p.cst(-0.5)),
         reads=["st_r"], writes=["st_r"])
    if alpha is None:
        p.op("vector", lambda v: v.tensor_copy(scaleA, r),
             reads=["st_r"], writes=["st_sa"])
    elif isinstance(alpha, tuple) and alpha[0] == "col":
        p.op("vector", lambda v: v.tensor_mul(scaleA, r, alpha[1]),
             reads=["st_r", alpha[2]], writes=["st_sa"])
    else:
        p.op("vector", lambda v: v.tensor_scalar(scaleA, r, p.cst(alpha[1]), None,
                                                 ALU.mult),
             reads=["st_r"], writes=["st_sa"])
    p.op("vector", lambda v: v.tensor_mul(nb_, m_y, r),
         reads=["st_m", "st_r"], writes=["st_nb"])
    p.op("vector", lambda v: v.tensor_scalar(nb_, nb_, p.cst(-1.0), None, ALU.mult),
         reads=["st_nb"], writes=["st_nb"])
    p.op("vector", lambda v: v.tensor_scalar(xnew[:], u[:], scaleA, nb_,
                                             ALU.mult, ALU.add),
         reads=[uref, "st_sa", "st_nb"], writes=["bscr_xnew"])
    p.op("vector", lambda v: v.tensor_add(xnew[:], xnew[:], xold[:]),
         reads=["bscr_xnew", xo_res], writes=["bscr_xnew"])
    p.dma(cfg.get("out_q", "pool"), cfg["xnew_dram"](b, tc), xnew[:],
          reads=["bscr_xnew"], writes=[cfg.get("xnew_res", "d_xnew")])
    nxt = cfg.get("next")
    if nxt is None:
        return
    Rarr, NBarr, Rres, tT_store = nxt
    su2, sq2 = st.col(0), st.col(1)
    p.op("vector", lambda v: v.reduce_sum(su2, xnew[:], axis=AX.X),
         reads=["bscr_xnew"], writes=["st_su"])
    p.op("scalar", lambda s: s.activation(sqj[:], xnew[:], AF.Square, accum_out=sq2),
         reads=["bscr_xnew"], writes=["st_sq", "bscr_sqj"])
    m2, e22, v2, r2 = st.col(2), st.col(3), st.col(4), st.col(5)
    p.op("vector", lambda v: v.tensor_scalar(m2, su2, cD, None, ALU.mult),
         reads=["st_su"], writes=["st_m"])
    p.op("vector", lambda v: v.tensor_scalar(e22, sq2, cD, None, ALU.mult),
         reads=["st_sq"], writes=["st_e2"])
    p.op("vector", lambda v: v.tensor_mul(v2, m2, m2), reads=["st_m"], writes=["st_var"])
    p.op("vector", lambda v: v.tensor_sub(v2, e22, v2),
         reads=["st_e2", "st_var"], writes=["st_var"])
    p.op("scalar", lambda s: s.activation(r2, v2, AF.Ln, bias=p.cst(EPS)),
         reads=["st_var"], writes=["st_r"])
    p.op("scalar", lambda s: s.activation(r2, r2, AF.Exp, scale=p.cst(-0.5)),
         reads=["st_r"], writes=["st_r"])
    p.op("vector", lambda v: v.tensor_copy(Rarr[:, tc:tc + 1], r2),
         reads=["st_r"], writes=[Rres])
    p.op("vector", lambda v: v.tensor_mul(NBarr[:, tc:tc + 1], m2, r2),
         reads=["st_m", "st_r"], writes=[Rres])
    p.op("vector", lambda v: v.tensor_scalar(NBarr[:, tc:tc + 1],
                                             NBarr[:, tc:tc + 1],
                                             p.cst(-1.0), None, ALU.mult),
         reads=[Rres], writes=[Rres])
    tn = scr["tn"]
    p.op("scalar", lambda s: s.activation(tn[:], xnew[:], AF.Identity,
                                          scale=Rarr[:, tc:tc + 1],
                                          bias=NBarr[:, tc:tc + 1]),
         reads=["bscr_xnew", Rres], writes=["bscr_tn"])
    tpp = cfg["tr_psum"]
    for dc in range(8):
        p.op("tensor", lambda t, dc=dc: t.transpose(
            tpp[:, dc * P:(dc + 1) * P], tn[:, dc * P:(dc + 1) * P],
            scr["ident"][:]),
            reads=["bscr_tn", "c_ident"], writes=["ptr"])
    kind = tT_store[0]
    if kind == "sbuf":
        tT = tT_store[1]
        p.op("vector", lambda v: v.tensor_copy(
            tT[:, :, tc * P:(tc + 1) * P],
            tpp[:].rearrange("p (c n) -> p c n", c=8)),
            reads=["ptr"], writes=[tT_store[2]])
    else:
        stg = scr["tT_stage"]
        p.op("vector", lambda v: v.tensor_copy(stg[:], tpp[:]),
             reads=["ptr"], writes=["bscr_tstage"])
        p.dma("pool", tT_store[1](b, tc), stg[:].rearrange("p (c n) -> p c n", c=8),
              reads=["bscr_tstage"], writes=["d_tT"])


def emit_pre_ln_from_dram(p, cfg, b, x_dram_fn, Rarr, NBarr, Rres, tT_store,
                          inner=None):
    scr = cfg["scr"]; st = cfg["st"]
    cD = p.cst(1.0 / D)
    for tc in range(NB):
        xold = scr["xold"][tc % 2]
        xo_res = f"xold{tc % 2}"
        p.dma("sp", xold[:], x_dram_fn(b, tc), writes=[xo_res])
        su, sq = st.col(0), st.col(1)
        p.op("vector", lambda v: v.reduce_sum(su, xold[:], axis=AX.X),
             reads=[xo_res], writes=["st_su"])
        p.op("scalar", lambda s: s.activation(scr["sqj"][:], xold[:], AF.Square,
                                              accum_out=sq),
             reads=[xo_res], writes=["st_sq", "bscr_sqj"])
        m2, e22, v2, r2 = st.col(2), st.col(3), st.col(4), st.col(5)
        p.op("vector", lambda v: v.tensor_scalar(m2, su, cD, None, ALU.mult),
             reads=["st_su"], writes=["st_m"])
        p.op("vector", lambda v: v.tensor_scalar(e22, sq, cD, None, ALU.mult),
             reads=["st_sq"], writes=["st_e2"])
        p.op("vector", lambda v: v.tensor_mul(v2, m2, m2),
             reads=["st_m"], writes=["st_var"])
        p.op("vector", lambda v: v.tensor_sub(v2, e22, v2),
             reads=["st_e2", "st_var"], writes=["st_var"])
        p.op("scalar", lambda s: s.activation(r2, v2, AF.Ln, bias=p.cst(EPS)),
             reads=["st_var"], writes=["st_r"])
        p.op("scalar", lambda s: s.activation(r2, r2, AF.Exp, scale=p.cst(-0.5)),
             reads=["st_r"], writes=["st_r"])
        p.op("vector", lambda v: v.tensor_copy(Rarr[:, tc:tc + 1], r2),
             reads=["st_r"], writes=[Rres])
        p.op("vector", lambda v: v.tensor_mul(NBarr[:, tc:tc + 1], m2, r2),
             reads=["st_m", "st_r"], writes=[Rres])
        p.op("vector", lambda v: v.tensor_scalar(NBarr[:, tc:tc + 1],
                                                 NBarr[:, tc:tc + 1],
                                                 p.cst(-1.0), None, ALU.mult),
             reads=[Rres], writes=[Rres])
        if inner is None:
            tn = scr["tn"]
            p.op("scalar", lambda s: s.activation(tn[:], xold[:], AF.Identity,
                                                  scale=Rarr[:, tc:tc + 1],
                                                  bias=NBarr[:, tc:tc + 1]),
                 reads=[xo_res, Rres], writes=["bscr_tn"])
            src_res = "bscr_tn"
            src = tn
        else:
            lnsB, lnbB = inner
            tf = scr["trec"]
            p.op("scalar", lambda s: s.activation(tf[:], xold[:], AF.Identity,
                                                  scale=Rarr[:, tc:tc + 1],
                                                  bias=NBarr[:, tc:tc + 1]),
                 reads=[xo_res, Rres], writes=["bscr_trec"])
            su3, sq3 = st.col(0), st.col(1)
            p.op("vector", lambda v: v.reduce_sum(su3, tf[:], axis=AX.X),
                 reads=["bscr_trec"], writes=["st_su"])
            p.op("scalar", lambda s: s.activation(scr["sqj"][:], tf[:], AF.Square,
                                                  accum_out=sq3),
                 reads=["bscr_trec"], writes=["st_sq", "bscr_sqj"])
            m3, e3, v3, r3, nb3 = (st.col(2), st.col(3), st.col(4), st.col(5),
                                   st.col(6))
            p.op("vector", lambda v: v.tensor_scalar(m3, su3, cD, None, ALU.mult),
                 reads=["st_su"], writes=["st_m"])
            p.op("vector", lambda v: v.tensor_scalar(e3, sq3, cD, None, ALU.mult),
                 reads=["st_sq"], writes=["st_e2"])
            p.op("vector", lambda v: v.tensor_mul(v3, m3, m3),
                 reads=["st_m"], writes=["st_var"])
            p.op("vector", lambda v: v.tensor_sub(v3, e3, v3),
                 reads=["st_e2", "st_var"], writes=["st_var"])
            p.op("scalar", lambda s: s.activation(r3, v3, AF.Ln, bias=p.cst(EPS)),
                 reads=["st_var"], writes=["st_r"])
            p.op("scalar", lambda s: s.activation(r3, r3, AF.Exp, scale=p.cst(-0.5)),
                 reads=["st_r"], writes=["st_r"])
            p.op("vector", lambda v: v.tensor_mul(nb3, m3, r3),
                 reads=["st_m", "st_r"], writes=["st_nb"])
            p.op("vector", lambda v: v.tensor_scalar(nb3, nb3, p.cst(-1.0), None,
                                                     ALU.mult),
                 reads=["st_nb"], writes=["st_nb"])
            un = scr["u"]
            p.op("scalar", lambda s: s.activation(un[:], tf[:], AF.Identity,
                                                  scale=r3, bias=nb3),
                 reads=["bscr_trec", "st_r", "st_nb"], writes=["bscr_u"])
            tn = scr["tn"]
            p.op("vector", lambda v: v.tensor_mul(un[:], un[:], cfg["lnsB"][:]),
                 reads=["bscr_u", "c_lnsB"], writes=["bscr_u"])
            p.op("vector", lambda v: v.tensor_add(tn[:], un[:], cfg["lnbB"][:]),
                 reads=["bscr_u", "c_lnbB"], writes=["bscr_tn"])
            src_res = "bscr_tn"
            src = tn
        tpp = cfg["tr_psum"]
        for dc in range(8):
            p.op("tensor", lambda t, dc=dc: t.transpose(
                tpp[:, dc * P:(dc + 1) * P], src[:, dc * P:(dc + 1) * P],
                scr["ident"][:]),
                reads=[src_res, "c_ident"], writes=["ptr"])
        kind = tT_store[0]
        if kind == "sbuf":
            p.op("vector", lambda v, tc=tc: v.tensor_copy(
                tT_store[1][:, :, tc * P:(tc + 1) * P],
                tpp[:].rearrange("p (c n) -> p c n", c=8)),
                reads=["ptr"], writes=[tT_store[2]])
        else:
            stg = scr["tT_stage"]
            p.op("vector", lambda v: v.tensor_copy(stg[:], tpp[:]),
                 reads=["ptr"], writes=["bscr_tstage"])
            p.dma("pool", tT_store[1](b, tc),
                  stg[:].rearrange("p (c n) -> p c n", c=8),
                  reads=["bscr_tstage"], writes=["d_tT"])



def _evac_copy(p, eng, dst, src, reads, writes):
    if eng == "scalar":
        p.op("scalar", lambda s: s.copy(dst, src), reads=reads, writes=writes)
    else:
        p.op("vector", lambda v: v.tensor_copy(dst, src), reads=reads, writes=writes)


def _takeover(p, news, olds):
    toks = []
    for o in olds:
        e = p.res.pop(o, None)
        if e:
            if e["w"] is not None:
                toks.append(e["w"])
            toks.extend(e["r"])
    if isinstance(news, str):
        news = [news]
    for new in news:
        ent = p.res.setdefault(new, {"w": None, "r": []})
        ent["r"].extend(list(toks))


def emit_bn_finalize(p, io):
    scr, st = io["scr"], io["st"]
    SSin = scr["SSin"]
    SS = scr["SS"]
    p.dma("sp", SSin[:], io["allstats"], writes=["SSin"])
    p.op("vector", lambda v: v.tensor_reduce(SS[:], SSin[:], AX.X, ALU.add),
         reads=["SSin"], writes=["SS"])
    mu, e2, var, r = scr["mu24"], scr["e224"], scr["var24"], scr["r24"]
    su_v = bass.AP(SS[:].tensor, SS[:].offset, [list(SS[:].ap[0]), [2, 24]])
    sq_v = bass.AP(SS[:].tensor, SS[:].offset + 1, [list(SS[:].ap[0]), [2, 24]])
    c32k = p.cst(1.0 / 32768.0)
    p.op("vector", lambda v: v.tensor_scalar(mu[:], su_v, c32k, None, ALU.mult),
         reads=["SS"], writes=["mu24"])
    p.op("vector", lambda v: v.tensor_scalar(e2[:], sq_v, c32k, None, ALU.mult),
         reads=["SS"], writes=["e224"])
    p.op("vector", lambda v: v.tensor_mul(var[:], mu[:], mu[:]),
         reads=["mu24"], writes=["var24"])
    p.op("vector", lambda v: v.tensor_sub(var[:], e2[:], var[:]),
         reads=["e224", "var24"], writes=["var24"])
    p.op("scalar", lambda s: s.activation(r[:], var[:], AF.Ln, bias=p.cst(EPS)),
         reads=["var24"], writes=["r24"])
    p.op("scalar", lambda s: s.activation(r[:], r[:], AF.Exp, scale=p.cst(-0.5)),
         reads=["r24"], writes=["r24"])
    gamB, betB = scr["gamB"], scr["betB"]
    scB, biB = io["scB"], io["biB"]
    p.dma("sp", gamB[:], io["bn_gam"].rearrange("p a b -> p (a b)"), writes=["gamB"])
    p.dma("sp", betB[:], io["bn_bet"].rearrange("p a b -> p (a b)"), writes=["betB"])
    gr = scr["gr24"]
    p.op("vector", lambda v: v.tensor_mul(gr[:], gamB[:], r[:]),
         reads=["gamB", "r24"], writes=["gr24"])
    p.op("vector", lambda v: v.tensor_scalar(scB[:], gr[:], p.cst(4.0), None,
                                             ALU.mult),
         reads=["gr24"], writes=["scB"])
    p.op("vector", lambda v: v.tensor_mul(var[:], mu[:], gr[:]),
         reads=["mu24", "gr24"], writes=["var24"])
    p.op("vector", lambda v: v.tensor_sub(var[:], betB[:], var[:]),
         reads=["betB", "var24"], writes=["var24"])
    p.op("vector", lambda v: v.tensor_scalar(biB[:], var[:], p.cst(4.0),
                                             p.cst(-4.0), ALU.mult, ALU.add),
         reads=["var24"], writes=["biB"])


def emit_l1_post(p, b, io):
    A = io["A4"]
    hq, bq, hv, vn = A["hq"], A["bq"], A["hv"], A["vn"]
    outT, WpT = A["outT"], A["WpT"]
    q0, q1, q2, y0, y1 = io["q0"], io["q1"], io["q2"], io["y0"], io["y1"]
    scr, st = io["scr"], io["st"]
    scB, biB = io["scB"], io["biB"]
    stg2 = scr["stg2"]
    _takeover(p, ["zq", "zbq", "zv", "zvn"],
              ["zwg", "zt3", "zHg", "wut0", "wut1", "wut2", "wut3",
               "wdt0", "wdt1", "wdt2", "wdt3", "castg", "zWpT"])
    p.dma("sp", hq[:], io["h_in"][0, b].rearrange("c p n -> p c n"), writes=["zq"])
    p.dma("sp", hv[:], io["h_in"][2, b].rearrange("c p n -> p c n"), writes=["zv"])
    for ec in range(8):
        p.op("scalar", lambda s, ec=ec: s.activation(
            bq[:, ec, :], hq[:, ec, :], AF.Sigmoid,
            scale=scB[:, ec:ec + 1], bias=biB[:, ec:ec + 1]),
            reads=["zq", "scB", "biB"], writes=["zbq"])
    for ec in range(8):
        p.dma("sp", stg2[:], io["h_in"][1, b, ec], writes=["stg2"])
        p.op("scalar", lambda s, ec=ec: s.activation(
            stg2[:], stg2[:], AF.Sigmoid,
            scale=scB[:, 8 + ec:9 + ec], bias=biB[:, 8 + ec:9 + ec]),
            reads=["stg2", "scB", "biB"], writes=["stg2"])
        p.op("vector", lambda v, ec=ec: v.tensor_mul(
            hq[:, ec, :], stg2[:], bq[:, ec, :]),
            reads=["stg2", "zbq", "zq"], writes=["zq"])
    qk = hq
    for ec in range(8):
        p.op("scalar", lambda s, ec=ec: s.activation(
            stg2[:], hv[:, ec, :], AF.Sigmoid,
            scale=scB[:, 16 + ec:17 + ec], bias=biB[:, 16 + ec:17 + ec]),
            reads=["zv", "scB", "biB"], writes=["stg2"])
        for tc in range(NB):
            p.op("tensor", lambda t, tc=tc: t.transpose(
                io["tr"][:, 0:P], stg2[:, tc * P:(tc + 1) * P], scr["ident"][:]),
                reads=["stg2", "c_ident"], writes=["ptr"])
            p.op("vector", lambda v, tc=tc, ec=ec: v.tensor_copy(
                vn[:, tc, ec * P:(ec + 1) * P], io["tr"][:, 0:P]),
                reads=["ptr"], writes=["zvn"])
    vv_sb = scr["vv_sb"]
    for h in range(16):
        base = 64 * (h % 2)
        for tc in range(NB):
            p.op("tensor", lambda t, h=h, tc=tc, base=base: t.matmul(
                q2[base:base + 64, 0:64], vn[:, tc, h * 64:(h + 1) * 64],
                vn[:, tc, h * 64:(h + 1) * 64],
                start=(tc == 0), stop=(tc == NB - 1),
                tile_position=(0, base)),
                reads=["zvn"], writes=["pq2"])
        p.op("scalar", lambda s, h=h, base=base: s.activation(
            vv_sb[base:base + 64, h // 2, :], q2[base:base + 64, 0:64],
            AF.Identity, scale=p.cstb(0.125, base)),
            reads=["pq2"], writes=["vv_sb"])
    ostg = scr["aostg"]
    for h in range(16):
        base = 64 * (h % 2)
        for qt in range(4):
            p.op("tensor", lambda t, h=h, qt=qt, base=base: t.matmul(
                q0[0:64, :], vv_sb[base:base + 64, h // 2, :],
                qk[base:base + 64, h // 2, qt * 512:(qt + 1) * 512],
                start=True, stop=True),
                reads=["vv_sb", "zq"], writes=["pq0"])
            p.op("vector", lambda v: v.tensor_copy(ostg[:], q0[0:64, :]),
                 reads=["pq0"], writes=["aostg"])
            p.dma("pool", io["outTbuf"][base:base + 64, h // 2,
                                      qt * 512:(qt + 1) * 512],
                  ostg[:], reads=["aostg"], writes=["d_oT"])
    _takeover(p, ["zWpT"], ["zv"])
    p.dma("pool", WpT[:], io["WpT"].rearrange("(c p) m -> p c m", p=P),
          writes=["zWpT"])
    for tc in range(NB):
        otile = scr["otile"][tc % 2]
        p.dma("sp", otile[:], io["outTbuf"][:, :, tc * P:(tc + 1) * P],
              reads=["d_oT"], writes=[f"otile{tc % 2}"])
        p.dma("sp", scr["xold"][tc % 2][:], io["X2in"][b, tc * P:(tc + 1) * P, :],
              writes=[f"xold{tc % 2}"])
        yb, ybres = (y0, "py0") if tc % 2 == 0 else (y1, "py1")
        for k in range(8):
            for h2 in range(2):
                p.op("tensor", lambda t, k=k, tc=tc, h2=h2, yb=yb,
                     otile=otile: t.matmul(
                    yb[:, h2 * 512:(h2 + 1) * 512],
                    otile[:, k, :],
                    WpT[:, k, h2 * 512:(h2 + 1) * 512],
                    start=(k == 0), stop=(k == 7)),
                    reads=[f"otile{tc % 2}", "zWpT"], writes=[ybres])
        cfg = dict(io["bcfg"])
        cfg["pre_add"] = (io["bpB_sb"][:], "c_bpB")
        cfg["xnew_dram"] = lambda b_, tc_: io["X3buf"][b_, tc_ * P:(tc_ + 1) * P, :]
        cfg["xnew_res"] = "d_X3"
        cfg["next"] = (io["R3"], io["NB3"], "R3",
                       ("dram", lambda b_, tc_: io["tTbuf"][:, :, tc_ * P:(tc_ + 1) * P]))
        emit_boundary_chunk(p, cfg, b, tc, yb[:], ybres)


def emit_l1_ffn(p, b, io):
    A = io["A5"]
    wg, t3T, Hg, wut, wdt = A["wg"], A["t3T"], A["Hg"], A["wut"], A["wdt"]
    q0, q1, q2, y0, y1 = io["q0"], io["q1"], io["q2"], io["y0"], io["y1"]
    scr, st = io["scr"], io["st"]
    sstg, ustg = scr["sstg"], scr["ustg"]
    _takeover(p, ["zwg", "zt3", "zHg", "wut0", "wut1", "wut2", "wut3",
               "wdt0", "wdt1", "wdt2", "wdt3", "castg"],
              ["zq", "zbq", "zv", "zvn", "zWpT"])
    if b == 0:
        stg = scr["castg"]
        for wname, dst, nrc in (("wgT", io["wg_bf"], 8), ("wuT", io["wu_bf"], 8),
                                ("wdT", io["wd_bf"], 24)):
            src = io[wname]
            for rc in range(nrc):
                w = src.shape[1]
                p.dma("pool", stg[:, 0:w], src[rc * P:(rc + 1) * P, :],
                      writes=["castg"])
                p.dma("pool", dst[rc * P:(rc + 1) * P, :], stg[:, 0:w],
                      reads=["castg"], writes=["d_wbf"])
    p.dma("sp", wg[:], io["wg_bf"].rearrange("(c p) m -> p c m", p=P),
          reads=["d_wbf"], writes=["zwg"])
    p.dma("sp", t3T[:], io["tTbuf"], reads=["d_tT"], writes=["zt3"])
    groups = [list(range(g * 3, min(g * 3 + 3, NB))) for g in range(6)]
    for grp in groups:
        glen = len(grp) * P
        t0 = grp[0] * P
        for hc in range(24):
            wt = wut[hc % 4]
            p.dma("sp", wt[:], io["wu_bf"][:, hc * P:(hc + 1) * P]
                  .rearrange("(c p) m -> p c m", p=P), reads=["d_wbf"],
                  writes=[f"wut{hc % 4}"])
            for k in range(8):
                p.op("tensor", lambda t, k=k, hc=hc: t.matmul(
                    q0[:, 0:glen], wg[:, k, hc * P:(hc + 1) * P],
                    t3T[:, k, t0:t0 + glen], start=(k == 0), stop=(k == 7)),
                    reads=["zwg", "zt3"], writes=["pq0"])
            for k in range(8):
                p.op("tensor", lambda t, k=k, wt=wt: t.matmul(
                    q1[:, 0:glen], wt[:, k, :],
                    t3T[:, k, t0:t0 + glen], start=(k == 0), stop=(k == 7)),
                    reads=[f"wut{hc % 4}", "zt3"], writes=["pq1"])
            p.op("scalar", lambda s: s.activation(sstg[:, 0:glen], q0[:, 0:glen],
                                                  AF.Silu, scale=p.cst(S127)),
                 reads=["pq0"], writes=["sstg"])
            p.op("scalar", lambda s: s.copy(ustg[:, 0:glen], q1[:, 0:glen]),
                 reads=["pq1"], writes=["ustg"])
            p.op("vector", lambda v, hc=hc: v.tensor_mul(
                Hg[:, hc, 0:glen], sstg[:, 0:glen], ustg[:, 0:glen]),
                reads=["sstg", "ustg"], writes=["zHg"])
        banks = [(y0, "py0"), (y1, "py1")]
        for hc in range(24):
            wt = wdt[hc % 4]
            p.dma("sp", wt[:], io["wd_bf"][hc * P:(hc + 1) * P, :],
                  reads=["d_wbf"], writes=[f"wdt{hc % 4}"])
            for mi, tc in enumerate(grp):
                m0 = (tc - grp[0]) * P
                if mi < 2:
                    ps, pres = banks[mi]
                    for h2 in range(2):
                        p.op("tensor", lambda t, hc=hc, m0=m0, h2=h2, ps=ps, wt=wt:
                             t.matmul(ps[:, h2 * 512:(h2 + 1) * 512],
                                      Hg[:, hc, m0:m0 + P],
                                      wt[:, h2 * 512:(h2 + 1) * 512],
                                      start=(hc == 0), stop=(hc == 23)),
                             reads=["zHg", f"wdt{hc % 4}"], writes=[pres])
                else:
                    for h2, (psh, presh) in enumerate(((q0, "pq0"), (q1, "pq1"))):
                        p.op("tensor", lambda t, hc=hc, m0=m0, h2=h2, psh=psh, wt=wt:
                             t.matmul(psh[:],
                                      Hg[:, hc, m0:m0 + P],
                                      wt[:, h2 * 512:(h2 + 1) * 512],
                                      start=(hc == 0), stop=(hc == 23)),
                             reads=["zHg", f"wdt{hc % 4}"], writes=[presh])
        for mi, tc in enumerate(grp):
            p.dma("sp", scr["xold"][tc % 2][:],
                  io["X3buf"][b, tc * P:(tc + 1) * P, :], reads=["d_X3"],
                  writes=[f"xold{tc % 2}"])
            cfg = dict(io["bcfg"])
            cfg["alpha"] = ("const", S2_FFN, S2_FFN / D, (S2_FFN * S2_FFN) / D)
            cfg["xnew_dram"] = lambda b_, tc_: io["OUT"][b_, tc_ * P:(tc_ + 1) * P, :]
            cfg["xnew_res"] = "d_OUT"
            cfg["next"] = None
            if mi < 2:
                ps, pres = banks[mi]
                emit_boundary_chunk(p, cfg, b, tc, ps[:], pres)
            else:
                emit_boundary_chunk(p, cfg, b, tc,
                                    [(q0[:], "pq0", 0, 512),
                                     (q1[:], "pq1", 512, D)], None)


def emit_boundary_chunk_2part(p, cfg, b, tc, parts):
    scr = cfg["scr"]
    u = scr["u"]
    (a0, r0), (a1, r1) = parts
    p.op("vector", lambda v: v.tensor_copy(u[:, 0:512], a0), reads=[r0],
         writes=["bscr_u"])
    p.op("vector", lambda v: v.tensor_copy(u[:, 512:1024], a1), reads=[r1],
         writes=["bscr_u"])
    cfg = dict(cfg)
    cfg["pre_add"] = None
    cfg["_u_ready"] = True
    emit_boundary_chunk(p, cfg, b, tc, u[:], "bscr_u")



def emit_l0_attn(p, b, io):
    A = io["A1"]
    zA, cqT, ckvT = A["zA"], A["cqT"], A["ckvT"]
    wdq, wdkv, wtile = A["wdq"], A["wdkv"], A["wtile"]
    ropeC, ropeS = A["ropeC"], A["ropeS"]
    vaug, qTt = A["vaug"], A["qTt"]
    q0, q1, q2, y0, y1 = io["q0"], io["q1"], io["q2"], io["y0"], io["y1"]
    scr, st = io["scr"], io["st"]

    _takeover(p, ["zB1", "zB2", "cqT", "ckvT", "vaug", "qTt", "aoTt"],
              ["w1t0", "w1t1", "w1t2", "w1t3", "w2t0", "w2t1", "w2t2", "w2t3",
               "Hg", "ropeCS", "P01a", "P01b", "WbrT", "wtile", "woutT"])
    p.dma("pool", wdq[:], io["wdqT"].rearrange("(c p) m -> p c m", p=P),
          writes=["zB1"])
    p.dma("pool", wdkv[:], io["wdkvT"].rearrange("(c p) m -> p c m", p=P),
          writes=["zB2"])
    for wres, wsb, dst, dres in (("zB1", wdq, cqT, "cqT"),
                                 ("zB2", wdkv, ckvT, "ckvT")):
        for oc in range(4):
            for tt in range(4):
                ps, pres = ((q0, "pq0") if (oc * 4 + tt) % 2 == 0 else (q1, "pq1"))
                for k in range(8):
                    p.op("tensor", lambda t, k=k, oc=oc, tt=tt, ps=ps, wsb=wsb:
                         t.matmul(ps[:], wsb[:, k, oc * P:(oc + 1) * P],
                                  zA[:, k, tt * 512:(tt + 1) * 512],
                                  start=(k == 0), stop=(k == 7)),
                         reads=[wres, "zA"], writes=[pres])
                p.op("scalar", lambda s, oc=oc, tt=tt, ps=ps, dst=dst:
                     s.copy(dst[:, oc, tt * 512:(tt + 1) * 512], ps[:]),
                     reads=[pres], writes=[dres])

    _takeover(p, ["wtile"], ["zB1"])
    _takeover(p, ["ropeCS", "P01a", "P01b"], ["zB2"])
    p.op("pool", lambda g: g.memset(vaug[:, :, :, 64:65], 1.0), writes=["vaug"])
    p.dma("pool", wtile[:], io["wupvT"].rearrange("(c p) m -> p c m", p=P),
          writes=["wtile"])
    for tc in range(NB):
        for k in range(4):
            for h2 in range(2):
                p.op("tensor", lambda t, k=k, tc=tc, h2=h2:
                     t.matmul(y0[:, h2 * 512:(h2 + 1) * 512],
                              ckvT[:, k, tc * P:(tc + 1) * P],
                              wtile[:, k, h2 * 512:(h2 + 1) * 512],
                              start=(k == 0), stop=(k == 3)),
                     reads=["ckvT", "wtile"], writes=["py0"])
        p.op("scalar", lambda s, tc=tc: s.copy(
            vaug[:, tc, :, 0:64],
            y0[:].rearrange("p (h d) -> p h d", h=16)),
            reads=["py0"], writes=["vaug"])

    p.dma("sp", ropeC[:], io["ropeC"], writes=["ropeCS"])
    p.dma("sp", ropeS[:], io["ropeS"], writes=["ropeCS"])

    def qk_produce(cT, cres, dst, dres, tcs, qtile_mode):
        for tc in tcs:
            for k in range(4):
                for h2 in range(2):
                    p.op("tensor", lambda t, k=k, tc=tc, h2=h2, cT=cT:
                         t.matmul(y0[:, h2 * 512:(h2 + 1) * 512],
                                  cT[:, k, tc * P:(tc + 1) * P],
                                  wtile[:, k, h2 * 512:(h2 + 1) * 512],
                                  start=(k == 0), stop=(k == 3)),
                         reads=[cres, "wtile"], writes=["py0"])
            yv = y0[:].rearrange("p (h e i) -> p h e i", h=16, e=2)
            E, O = yv[:, :, 0, :], yv[:, :, 1, :]
            cb = ropeC[:, tc, :]
            sb_ = ropeS[:, tc, :]
            Cb = bass.AP(cb.tensor, cb.offset, [list(cb.ap[0]), [0, 16], [1, 32]])
            Sb = bass.AP(sb_.tensor, sb_.offset, [list(sb_.ap[0]), [0, 16], [1, 32]])
            r1v = scr["rope1"][:].rearrange("p (h i) -> p h i", h=16)
            r2v = scr["rope2"][:].rearrange("p (h i) -> p h i", h=16)
            stg = scr["qkstage"]
            sv = stg[:].rearrange("p (h e i) -> p h e i", h=16, e=2)
            p.op("vector", lambda v: v.tensor_mul(r1v, E, Cb),
                 reads=["py0", "ropeCS"], writes=["rope1"])
            p.op("vector", lambda v: v.tensor_mul(r2v, O, Sb),
                 reads=["py0", "ropeCS"], writes=["rope2"])
            p.op("vector", lambda v: v.tensor_sub(sv[:, :, 0, :], r1v, r2v),
                 reads=["rope1", "rope2"], writes=["qkstage"])
            p.op("vector", lambda v: v.tensor_mul(r1v, E, Sb),
                 reads=["py0", "ropeCS"], writes=["rope1"])
            p.op("vector", lambda v: v.tensor_mul(r2v, O, Cb),
                 reads=["py0", "ropeCS"], writes=["rope2"])
            p.op("vector", lambda v: v.tensor_add(sv[:, :, 1, :], r1v, r2v),
                 reads=["rope1", "rope2"], writes=["qkstage"])
            for dc in range(8):
                p.op("tensor", lambda t, dc=dc: t.transpose(
                    io["tr"][:, dc * P:(dc + 1) * P],
                    stg[:, dc * P:(dc + 1) * P], scr["ident"][:]),
                    reads=["qkstage", "c_ident"], writes=["ptr"])
            ti = (tc % 4) if qtile_mode else tc
            p.op("vector", lambda v, ti=ti: v.tensor_copy(
                dst[:, :, ti * P:(ti + 1) * P],
                io["tr"][:].rearrange("p (c n) -> p c n", c=8)),
                reads=["ptr"], writes=[dres])

    p.dma("pool", wtile[:], io["wupkT"].rearrange("(c p) m -> p c m", p=P),
          writes=["wtile"])
    qk_produce(ckvT, "ckvT", zA, "zA", range(NB), False)

    p.dma("pool", wtile[:], io["wupqT"].rearrange("(c p) m -> p c m", p=P),
          writes=["wtile"])
    ones64 = scr["ones64"]
    for qt in range(4):
        qk_produce(cqT, "cqT", qTt, "qTt", range(4 * qt, 4 * qt + 4), True)
        for hp in range(8):
            for kc in range(NB):
                Sp, Sres = (y0, "py0") if kc % 2 == 0 else (y1, "py1")
                Pt, Pres = (scr["P01a"], "P01a") if kc % 2 == 0 else                     (scr["P01b"], "P01b")
                for hloc in range(2):
                    base = 64 * hloc
                    p.op("tensor", lambda t, kc=kc, hp=hp, base=base, Sp=Sp,
                         hloc=hloc: t.matmul(
                             Sp[:, hloc * 512:(hloc + 1) * 512],
                             zA[base:base + 64, hp, kc * P:(kc + 1) * P],
                             qTt[base:base + 64, hp, :],
                             start=True, stop=True),
                         reads=["zA", "qTt"], writes=[Sres])
                p.op("scalar", lambda sE, Pt=Pt, Sp=Sp: sE.activation(
                    Pt[:], Sp[:], AF.Exp, scale=p.cst(0.125)),
                    reads=[Sres], writes=[Pres])
                for hloc, (avp, avres) in enumerate(((q0, "pq0"), (q1, "pq1"))):
                    p.op("tensor", lambda t, kc=kc, hp=hp, hloc=hloc, Pt=Pt,
                         avp=avp: t.matmul(
                             avp[0:65, :], vaug[:, kc, 2 * hp + hloc, :],
                             Pt[:, hloc * 512:(hloc + 1) * 512],
                             start=(kc == 0), stop=(kc == NB - 1)),
                         reads=["vaug", Pres], writes=[avres])
            for hloc, (avp, avres) in enumerate(((q0, "pq0"), (q1, "pq1"))):
                srow, rrow, rb = scr["srow"], scr["rrow"], scr["rbsb"]
                aostg = scr["aostg"]
                p.op("vector", lambda v, avp=avp: v.tensor_copy(
                    srow[:], avp[64:65, :]),
                    reads=[avres], writes=["srow"])
                p.op("vector", lambda v: v.reciprocal(rrow[:], srow[:]),
                     reads=["srow"], writes=["rrow"])
                p.op("tensor", lambda t: t.matmul(q2[0:64, :], ones64[:],
                                                  rrow[:], start=True, stop=True),
                     reads=["rrow", "ones64"], writes=["pq2"])
                p.op("vector", lambda v: v.tensor_copy(rb[:], q2[0:64, :]),
                     reads=["pq2"], writes=["rbsb"])
                p.op("vector", lambda v, avp=avp: v.tensor_mul(
                    aostg[:], avp[0:64, :], rb[:]),
                    reads=[avres, "rbsb"], writes=["aostg"])
                p.dma("pool", io["aoTbuf"][hloc * 64:(hloc + 1) * 64, hp,
                                         qt * 512:(qt + 1) * 512],
                      aostg[:], reads=["aostg"], writes=["d_ao"])
    _takeover(p, ["woutT"], ["wtile", "ropeCS", "P01a", "P01b"])
    woutT = A["woutT"]
    aotile = scr["aotile"]
    p.dma("pool", woutT[:], io["woutT"].rearrange("(c p) m -> p c m", p=P),
          writes=["woutT"])
    for tc in range(NB):
        aot = aotile[tc % 2]
        p.dma("sp", aot[:], io["aoTbuf"][:, :, tc * P:(tc + 1) * P],
              reads=["d_ao"], writes=[f"aotile{tc % 2}"])
        p.dma("sp", scr["xold"][tc % 2][:], io["x_in"][b, tc * P:(tc + 1) * P, :],
              writes=[f"xold{tc % 2}"])
        yb, ybres = (y0, "py0") if tc % 2 == 0 else (y1, "py1")
        for k in range(8):
            for h2 in range(2):
                p.op("tensor", lambda t, k=k, h2=h2, yb=yb, aot=aot: t.matmul(
                    yb[:, h2 * 512:(h2 + 1) * 512], aot[:, k, :],
                    woutT[:, k, h2 * 512:(h2 + 1) * 512],
                    start=(k == 0), stop=(k == 7)),
                    reads=[f"aotile{tc % 2}", "woutT"], writes=[ybres])
        cfg = dict(io["bcfg"])
        cfg["trec"] = (io["R0"], io["NB0"])
        cfg["trec_res"] = "R0"
        cfg["xnew_dram"] = lambda b_, tc_: io["X1buf"][b_, tc_ * P:(tc_ + 1) * P, :]
        cfg["xnew_res"] = "d_X1"
        cfg["next"] = (io["R1"], io["NB1"], "R1",
                       ("dram", lambda b_, tc_: io["tTbuf"][:, :, tc_ * P:(tc_ + 1) * P]))
        emit_boundary_chunk(p, cfg, b, tc, yb[:], ybres)


def emit_l0_ffn(p, b, io):
    A = io["A2"]
    zA, Hg = A["zA"], A["Hg"]
    w1t, w2t, WrT = A["w1t"], A["w2t"], A["WrT"]
    q0, q1, q2, y0, y1 = io["q0"], io["q1"], io["q2"], io["y0"], io["y1"]
    scr, st = io["scr"], io["st"]
    _takeover(p, ["w1t0", "w1t1", "w1t2", "w1t3"], ["zB1", "woutT"])
    _takeover(p, ["w2t0", "w2t1", "w2t2", "w2t3", "Hg"],
              ["zB2", "cqT", "ckvT", "vaug", "P01a", "P01b", "ropeCS",
               "woutT", "qTt", "aoTt", "wtile"])
    p.dma("sp", zA[:], io["tTbuf"], reads=["d_tT"], writes=["zA"])
    if b == 0:
        p.dma("pool", WrT[:], io["WrT"].rearrange("(c p) m -> p c m", p=P),
              writes=["WrT"])
        for rc in range(32):
            p.dma("pool", scr["qkstage"][:], io["W2"][rc * P:(rc + 1) * P, :],
                  writes=["qkstage"])
            p.dma("pool", io["W2bf"][rc * P:(rc + 1) * P, :], scr["qkstage"][:],
                  reads=["qkstage"], writes=["d_W2bf"])
    p7 = io["p7"]
    for tc in range(NB):
        for k in range(8):
            p.op("tensor", lambda t, k=k, tc=tc: t.matmul(
                q2[:, 0:8], zA[:, k, tc * P:(tc + 1) * P], WrT[:, k, :],
                start=(k == 0), stop=(k == 7)),
                reads=["zA", "WrT"], writes=["pq2"])
        nmx, e8, ssum, rs = st.col(2), scr["e8"], st.col(3), st.col(4)
        p.op("vector", lambda v: v.reduce_max(nmx, q2[:, 0:8], axis=AX.X,
                                              negate=True),
             reads=["pq2"], writes=["st_m"])
        p.op("scalar", lambda s: s.activation(e8[:], q2[:, 0:8], AF.Exp, bias=nmx),
             reads=["pq2", "st_m"], writes=["e8"])
        p.op("vector", lambda v: v.reduce_sum(ssum, e8[:], axis=AX.X),
             reads=["e8"], writes=["st_e2"])
        p.op("vector", lambda v: v.reciprocal(rs, ssum),
             reads=["st_e2"], writes=["st_var"])
        p.op("vector", lambda v, tc=tc: v.tensor_mul(p7[:, tc:tc + 1],
                                                     e8[:, 7:8], rs),
             reads=["e8", "st_var"], writes=["p7"])
    groups = [list(range(g * 3, min(g * 3 + 3, NB))) for g in range(6)]
    for grp in groups:
        glen = len(grp) * P
        t0 = grp[0] * P
        for hc in range(32):
            wt = w1t[hc % 4]
            gb, gres = (q0, "pq0") if hc % 2 == 0 else (q1, "pq1")
            p.dma("pool", wt[:], io["W1"][:, hc * P:(hc + 1) * P]
                  .rearrange("(c p) m -> p c m", p=P), writes=[f"w1t{hc % 4}"])
            for k in range(8):
                p.op("tensor", lambda t, k=k, wt=wt, gb=gb: t.matmul(
                    gb[:, 0:glen], wt[:, k, :], zA[:, k, t0:t0 + glen],
                    start=(k == 0), stop=(k == 7)),
                    reads=[f"w1t{hc % 4}", "zA"], writes=[gres])
            p.op("scalar", lambda s, hc=hc, gb=gb: s.activation(
                Hg[:, hc, 0:glen], gb[:, 0:glen], AF.Gelu,
                bias=io["b1c_sb"][:, hc:hc + 1]),
                reads=[gres, "c_b1c"], writes=["Hg"])
        banks = [(y0, "py0"), (y1, "py1")]
        for hc in range(32):
            wt = w2t[hc % 4]
            p.dma("sp", wt[:], io["W2bf"][hc * P:(hc + 1) * P, :],
                  reads=["d_W2bf"], writes=[f"w2t{hc % 4}"])
            for mi, tc in enumerate(grp):
                m0 = (tc - grp[0]) * P
                if mi < 2:
                    ps, pres = banks[mi]
                    for h2 in range(2):
                        p.op("tensor", lambda t, hc=hc, m0=m0, h2=h2, ps=ps, wt=wt:
                             t.matmul(ps[:, h2 * 512:(h2 + 1) * 512],
                                      Hg[:, hc, m0:m0 + P],
                                      wt[:, h2 * 512:(h2 + 1) * 512],
                                      start=(hc == 0), stop=(hc == 31)),
                             reads=["Hg", f"w2t{hc % 4}"], writes=[pres])
                else:
                    for h2, (psh, presh) in enumerate(((q0, "pq0"), (q1, "pq1"))):
                        p.op("tensor", lambda t, hc=hc, m0=m0, psh=psh, wt=wt,
                             h2=h2: t.matmul(psh[:], Hg[:, hc, m0:m0 + P],
                                             wt[:, h2 * 512:(h2 + 1) * 512],
                                             start=(hc == 0), stop=(hc == 31)),
                             reads=["Hg", f"w2t{hc % 4}"], writes=[presh])
        for mi, tc in enumerate(grp):
            p.dma("sp", scr["xold"][tc % 2][:],
                  io["X1buf"][b, tc * P:(tc + 1) * P, :], reads=["d_X1"],
                  writes=[f"xold{tc % 2}"])
            cfg = dict(io["bcfg"])
            cfg["pre_add"] = (io["b2B_sb"][:], "c_b2B")
            cfg["alpha"] = ("col", p7[:, tc:tc + 1], "p7")
            cfg["xnew_dram"] = lambda b_, tc_: io["X2out"][b_, tc_ * P:(tc_ + 1) * P, :]
            cfg["xnew_res"] = "d_X2o"
            cfg["next"] = (io["R2"], io["NB2"], "R2",
                           ("dram", lambda b_, tc_: io["tTbuf"][:, :, tc_ * P:(tc_ + 1) * P]))
            if mi < 2:
                ps, pres = banks[mi]
                emit_boundary_chunk(p, cfg, b, tc, ps[:], pres)
            else:
                emit_boundary_chunk(p, cfg, b, tc,
                                    [(q0[:], "pq0", 0, 512),
                                     (q1[:], "pq1", 512, D)], None)


def emit_l1_branches(p, b, io):
    A = io["A3"]
    zA, WbrT = A["zA"], A["WbrT"]
    q0, q1 = io["q0"], io["q1"]
    scr, st = io["scr"], io["st"]
    statsA = io["statsA"]
    hstage = scr["hstage"]
    _takeover(p, ["WbrT"], ["w1t0", "w1t1", "w1t2", "w1t3", "w2t0", "w2t1", "w2t2", "w2t3"])
    p.dma("sp", zA[:], io["tTbuf"], reads=["d_tT"], writes=["zA"])
    for ibr, wname in enumerate(("WqT", "WkT", "WvT")):
        p.dma("pool", WbrT[:], io[wname].rearrange("(c p) m -> p c m", p=P),
              writes=["WbrT"])
        for ec in range(8):
            for tt in range(4):
                ps, pres = ((q0, "pq0") if tt % 2 == 0 else (q1, "pq1"))
                for k in range(8):
                    p.op("tensor", lambda t, k=k, ec=ec, tt=tt, ps=ps: t.matmul(
                        ps[:], WbrT[:, k, ec * P:(ec + 1) * P],
                        zA[:, k, tt * 512:(tt + 1) * 512],
                        start=(k == 0), stop=(k == 7)),
                        reads=["WbrT", "zA"], writes=[pres])
                su_t, sq_t = st.col(2), st.col(3)
                p.op("scalar", lambda s, ps=ps: s.activation(
                    hstage[:], ps[:], AF.Identity, accum_out=su_t),
                    reads=[pres], writes=["hstage", "st_m"])
                p.op("scalar", lambda s, ps=ps: s.activation(
                    scr["sqj512"][:], ps[:], AF.Square, accum_out=sq_t),
                    reads=[pres], writes=["sqj512", "st_e2"])
                p.op("vector", lambda v, ibr=ibr, ec=ec: v.tensor_add(
                    statsA[:, ibr, ec, 0:1], statsA[:, ibr, ec, 0:1], su_t),
                    reads=["st_m"], writes=["statsA"])
                p.op("vector", lambda v, ibr=ibr, ec=ec: v.tensor_add(
                    statsA[:, ibr, ec, 1:2], statsA[:, ibr, ec, 1:2], sq_t),
                    reads=["st_e2"], writes=["statsA"])
                p.dma("pool", io["h_out"][ibr, b, ec, :, tt * 512:(tt + 1) * 512],
                      hstage[:], reads=["hstage"])



def _mk_scr_common(p):
    scr = {}
    scr["xold"] = [p.sbuf("xold0", (P, D), F32), p.sbuf("xold1", (P, D), F32)]
    scr["u"] = p.sbuf("scr_u", (P, D), F32)
    scr["xnew"] = p.sbuf("scr_xnew", (P, D), F32)
    scr["tn"] = p.sbuf("scr_tn", (P, D), BF16)
    scr["sqj"] = p.sbuf("scr_sqj", (P, D), BF16)
    scr["tT_stage"] = p.sbuf("scr_tstage", (P, D), BF16)
    scr["ident"] = p.sbuf("identsb", (P, P), BF16)
    return scr


def _load_setup(p, scr, io):
    tok1 = p.dma("sp", p.consts_tile[:], io["constv"], writes=["c_const"])
    tok2 = p.dma("sp", scr["ident"][:], io["ident"], writes=["c_ident"])
    for e in ENGS:
        p._wait(("eng", e), tok1)
        p._wait(("eng", e), tok2)


def build_launchA(debug=False):
    nc = bass.Bass("TRN2", target_bir_lowering=False)
    io = {}
    di = nc.dram_tensor
    io["x_in"] = di("x", [NBATCH, S, D], F32, kind="ExternalInput").ap()
    for nm, shp, dt in (("wdqT", [D, 512], F32), ("wdkvT", [D, 512], F32),
                        ("wupqT", [512, D], F32), ("wupkT", [512, D], F32),
                        ("wupvT", [512, D], F32), ("woutT", [D, D], F32),
                        ("lnsB", [P, D], F32), ("lnbB", [P, D], F32),
                        ("ropeC", [P, NB, 32], F32), ("ropeS", [P, NB, 32], F32),
                        ("WrT", [D, 8], F32), ("W1", [D, 4096], F32),
                        ("b1c", [P, 32], F32), ("W2", [4096, D], F32),
                        ("b2B", [P, D], F32), ("WqT", [D, D], F32),
                        ("WkT", [D, D], F32), ("WvT", [D, D], F32),
                        ("ident", [P, P], BF16),
                        ("constv", [P, len(CONSTS)], F32)):
        io[nm] = di(nm, shp, dt, kind="ExternalInput").ap()
    io["X2out"] = di("X2out", [NBATCH, S, D], F32, kind="ExternalOutput").ap()
    io["h_out"] = di("h_out", [3, NBATCH, 8, P, S], BF16, kind="ExternalOutput").ap()
    io["stats_out"] = di("stats_out", [P, 48], F32, kind="ExternalOutput").ap()
    io["X1buf"] = di("X1buf", [NBATCH, S, D], F32, kind="Internal").ap()
    if debug:
        io["dbg_X1"] = di("dbg_X1", [NBATCH, S, D], F32, kind="ExternalOutput").ap()
        io["dbg_ao"] = di("dbg_ao", [P, 8, S], BF16, kind="ExternalOutput").ap()
        io["dbg_t1"] = di("dbg_t1", [P, 8, S], BF16, kind="ExternalOutput").ap()
        io["dbg_p7"] = di("dbg_p7", [P, NB], F32, kind="ExternalOutput").ap()
        io["dbg_W2bf"] = di("dbg_W2bf", [4096, D], BF16, kind="ExternalOutput").ap()
    io["tTbuf"] = di("tTbuf", [P, 8, S], BF16, kind="Internal").ap()
    io["aoTbuf"] = di("aoTbuf", [P, 8, S], BF16, kind="Internal").ap()
    io["W2bf"] = di("W2bf", [4096, D], BF16, kind="Internal").ap()

    p = Prog(nc)
    p.setup_consts()
    scr = _mk_scr_common(p)
    scr["trec"] = p.sbuf("scr_trec", (P, D), F32)
    scr["qkstage"] = p.sbuf("scr_qkst", (P, D), BF16)
    scr["rope1"] = p.sbuf("scr_rope1", (P, 512), F32)
    scr["rope2"] = p.sbuf("scr_rope2", (P, 512), F32)
    scr["e8"] = p.sbuf("scr_e8", (P, 8), F32)
    scr["srow"] = p.sbuf("scr_srow", (1, 512), F32)
    scr["rrow"] = p.sbuf("scr_rrow", (1, 512), F32)
    scr["rbsb"] = p.sbuf("scr_rbsb", (64, 512), F32)
    scr["ones64"] = p.sbuf("scr_ones64", (1, 64), F32)
    scr["aotile"] = [p.sbuf("scr_aotile0", (P, 8, P), BF16),
                     p.sbuf("scr_aotile1", (P, 8, P), BF16)]
    scr["aostg"] = p.sbuf("scr_aostg", (64, 512), BF16)
    scr["hstage"] = p.sbuf("scr_hstage", (P, 512), BF16)
    scr["sqj512"] = p.sbuf("scr_sqj512", (P, 512), BF16)
    io["lnsB_sb"] = p.sbuf("lnsB_sb", (P, D), F32)
    io["lnbB_sb"] = p.sbuf("lnbB_sb", (P, D), F32)
    io["b1c_sb"] = p.sbuf("b1c_sb", (P, 32), F32)
    io["b2B_sb"] = p.sbuf("b2B_sb", (P, D), F32)
    st = TinyStats(p, "tinyst", 8)
    for nm in ("R0", "NB0", "R1", "NB1", "R2", "NB2"):
        io[nm] = p.sbuf(nm, (P, NB), F32)
    io["p7"] = p.sbuf("p7arr", (P, NB), F32)
    io["statsA"] = p.sbuf("statsA", (P, 3, 8, 2), F32)

    AR = 131584
    arena = nc.alloc_sbuf_tensor("arena", [P, AR], mybir.dt.uint8)
    base = nc.lookup_mloc(arena).addr
    at = lambda nm, off, shp, dt: nc.alloc_sbuf_tensor_at(nm, list(shp), dt,
                                                          offset=base + off)
    A1 = {}
    A1["zA"] = at("zA_t", 0, (P, 8, S), BF16)
    A1["wdq"] = at("wdq_t", 32768, (P, 8, 512), BF16)
    A1["woutT"] = at("woutT_t", 32768, (P, 8, D), BF16)
    A1["wtile"] = at("wtile_t", 32768, (P, 4, D), BF16)
    A1["wdkv"] = at("wdkv_t", 40960, (P, 8, 512), BF16)
    A1["ropeC"] = at("ropeC_t", 40960, (P, NB, 32), F32)
    A1["ropeS"] = at("ropeS_t", 43008, (P, NB, 32), F32)
    scr["P01a"] = at("P01a_t", 45056, (P, D), BF16)
    scr["P01b"] = at("P01b_t", 47104, (P, D), BF16)
    A1["cqT"] = at("cqT_t", 49152, (P, 4, S), BF16)
    A1["ckvT"] = at("ckvT_t", 65536, (P, 4, S), BF16)
    A1["vaug"] = at("vaug_t", 81920, (P, NB, 16, 65), BF16)
    A1["qTt"] = at("qTt_t", 115200, (P, 8, 512), BF16)
    A1["aoTt"] = at("aoTt_t", 123392, (P, 8, 512), BF16)
    A2 = {"zA": A1["zA"]}
    A2["w1t"] = [at(f"w1t{i}_t", 32768 + 2048 * i, (P, 8, P), BF16)
                 for i in range(4)]
    A2["w2t"] = [at(f"w2t{i}_t", 40960 + 2048 * i, (P, D), BF16)
                 for i in range(4)]
    A2["Hg"] = at("Hg_t", 49152, (P, 32, 384), BF16)
    A2["WrT"] = p.sbuf("WrT_sb", (P, 8, 8), BF16)
    A3 = {"zA": A1["zA"], "WbrT": at("WbrT_t", 32768, (P, 8, D), BF16)}
    io["A1"], io["A2"], io["A3"] = A1, A2, A3

    q0 = p.psum("q0", (P, 512)); q1 = p.psum("q1", (P, 512))
    q2 = p.psum("q2", (P, 512))
    y0 = p.psum("y0", (P, D)); y1 = p.psum("y1", (P, D))
    tr = p.psum("trp", (P, D), BF16)
    io.update(q0=q0, q1=q1, q2=q2, y0=y0, y1=y1, tr=tr)
    io["scr"], io["st"] = scr, st
    io["bcfg"] = {"scr": scr, "st": st, "out_q": "pool", "tr_psum": tr}

    with nc.Block():
        _load_setup(p, scr, io)
        p.dma("sp", io["lnsB_sb"][:], io["lnsB"], writes=["c_lnsB"])
        p.dma("sp", io["lnbB_sb"][:], io["lnbB"], writes=["c_lnbB"])
        p.dma("sp", io["b1c_sb"][:], io["b1c"], writes=["c_b1c"])
        p.dma("sp", io["b2B_sb"][:], io["b2B"], writes=["c_b2B"])
        p.op("pool", lambda g: g.memset(scr["ones64"][:], 1.0), writes=["ones64"])
        p.op("pool", lambda g: g.memset(io["statsA"][:], 0.0), writes=["statsA"])
        for b in range(NBATCH):
            cfg_pre = {"scr": scr, "st": st, "lnsB": io["lnsB_sb"],
                       "lnbB": io["lnbB_sb"], "tr_psum": tr}
            emit_pre_ln_from_dram(
                p, cfg_pre, b,
                lambda b_, tc: io["x_in"][b_, tc * P:(tc + 1) * P, :],
                io["R0"], io["NB0"], "R0", ("sbuf", A1["zA"], "zA"),
                inner=(io["lnsB_sb"], io["lnbB_sb"]))
            emit_l0_attn(p, b, io)
            if debug and b == 0:
                p.dma("sp", io["dbg_ao"], io["aoTbuf"], reads=["d_ao"])
                p.dma("sp", io["dbg_t1"], io["tTbuf"], reads=["d_tT"])
                p.dma("sp", io["dbg_X1"][0], io["X1buf"][0], reads=["d_X1"])
            emit_l0_ffn(p, b, io)
            if debug and b == 0:
                p.dma("sp", io["dbg_p7"], io["p7"][:], reads=["p7"])
                p.dma("sp", io["dbg_W2bf"], io["W2bf"], reads=["d_W2bf"])
            emit_l1_branches(p, b, io)
        p.dma("sp", io["stats_out"],
              io["statsA"][:].rearrange("p a b c -> p (a b c)"),
              reads=["statsA"])
        p.finish()
    return nc, p


def build_launchB(debug=False):
    nc = bass.Bass("TRN2", target_bir_lowering=False)
    io = {}
    di = nc.dram_tensor
    io["X2in"] = di("X2in", [NBATCH, S, D], F32, kind="ExternalInput").ap()
    io["h_in"] = di("h_in", [3, NBATCH, 8, P, S], BF16, kind="ExternalInput").ap()
    io["allstats"] = di("allstats", [P, 48, 8], F32, kind="ExternalInput").ap()
    for nm, shp, dt in (("bn_gam", [P, 3, 8], F32), ("bn_bet", [P, 3, 8], F32),
                        ("WpT", [D, D], F32), ("bpB", [P, D], F32),
                        ("wgT", [D, 3072], I32), ("wuT", [D, 3072], I32),
                        ("wdT", [3072, D], I32), ("ident", [P, P], BF16),
                        ("constv", [P, len(CONSTS)], F32)):
        io[nm] = di(nm, shp, dt, kind="ExternalInput").ap()
    io["OUT"] = di("OUT", [NBATCH, S, D], F32, kind="ExternalOutput").ap()
    if debug:
        io["dbg_sc"] = di("dbg_sc", [P, 48], F32, kind="ExternalOutput").ap()
        io["dbg_X3"] = di("dbg_X3", [NBATCH, S, D], F32, kind="ExternalOutput").ap()
        io["dbg_oT"] = di("dbg_oT", [P, 8, S], BF16, kind="ExternalOutput").ap()
        io["dbg_t3"] = di("dbg_t3", [P, 8, S], BF16, kind="ExternalOutput").ap()
    io["X3buf"] = di("X3buf", [NBATCH, S, D], F32, kind="Internal").ap()
    io["tTbuf"] = di("tTbufB", [P, 8, S], BF16, kind="Internal").ap()
    io["outTbuf"] = di("outTbuf", [P, 8, S], BF16, kind="Internal").ap()
    io["wg_bf"] = di("wg_bf", [D, 3072], BF16, kind="Internal").ap()
    io["wu_bf"] = di("wu_bf", [D, 3072], BF16, kind="Internal").ap()
    io["wd_bf"] = di("wd_bf", [3072, D], BF16, kind="Internal").ap()

    p = Prog(nc)
    p.setup_consts()
    scr = _mk_scr_common(p)
    scr["stg2"] = p.sbuf("scr_stg2", (P, S), BF16)
    scr["sstg"] = p.sbuf("scr_sstg", (P, 512), BF16)
    scr["ustg"] = p.sbuf("scr_ustg", (P, 512), BF16)
    scr["vv_sb"] = p.sbuf("scr_vv", (P, 8, 64), BF16)
    scr["aostg"] = p.sbuf("scr_aostgB", (64, 512), BF16)
    scr["otile"] = [p.sbuf("scr_otile0", (P, 8, P), BF16),
                    p.sbuf("scr_otile1", (P, 8, P), BF16)]
    scr["SSin"] = p.sbuf("scr_ssin", (P, 48, 8), F32)
    scr["SS"] = p.sbuf("scr_ss", (P, 48), F32)
    for nm in ("mu24", "e224", "var24", "r24", "gr24", "gamB", "betB"):
        scr[nm] = p.sbuf("scr_" + nm, (P, 24), F32)
    io["scB"] = p.sbuf("scB", (P, 24), F32)
    io["biB"] = p.sbuf("biB", (P, 24), F32)
    st = TinyStats(p, "tinystB", 8)
    for nm in ("R3", "NB3"):
        io[nm] = p.sbuf(nm, (P, NB), F32)
    io["bpB_sb"] = p.sbuf("bpB_sb", (P, D), F32)

    AR = 131072
    arena = nc.alloc_sbuf_tensor("arenaB", [P, AR], mybir.dt.uint8)
    base = nc.lookup_mloc(arena).addr
    at = lambda nm, off, shp, dt: nc.alloc_sbuf_tensor_at(nm, list(shp), dt,
                                                          offset=base + off)
    A4 = {}
    A4["hq"] = at("hq_t", 0, (P, 8, S), BF16)
    A4["bq"] = at("bq_t", 32768, (P, 8, S), BF16)
    A4["outT"] = A4["bq"]
    A4["hv"] = at("hv_t", 65536, (P, 8, S), BF16)
    A4["WpT"] = at("WpT_t", 65536, (P, 8, D), BF16)
    A4["vn"] = at("vn_t", 98304, (P, NB, D), BF16)
    A5 = {}
    A5["wg"] = at("wg_t", 0, (P, 8, 3072), BF16)
    A5["t3T"] = at("t3T_t", 49152, (P, 8, S), BF16)
    A5["Hg"] = at("HgB_t", 81920, (P, 24, 384), BF16)
    A5["wut"] = [at(f"wut{i}_t", 100352 + 2048 * i, (P, 8, P), BF16)
                 for i in range(4)]
    A5["wdt"] = [at(f"wdt{i}_t", 108544 + 2048 * i, (P, D), BF16)
                 for i in range(4)]
    scr["castg"] = at("castg_t", 116736, (P, 3072), BF16)
    io["A4"], io["A5"] = A4, A5

    q0 = p.psum("q0", (P, 512)); q1 = p.psum("q1", (P, 512))
    q2 = p.psum("q2", (P, 512))
    y0 = p.psum("y0", (P, D)); y1 = p.psum("y1", (P, D))
    tr = p.psum("trp", (P, D), BF16)
    io.update(q0=q0, q1=q1, q2=q2, y0=y0, y1=y1, tr=tr)
    io["scr"], io["st"] = scr, st
    io["bcfg"] = {"scr": scr, "st": st, "out_q": "pool", "tr_psum": tr}

    with nc.Block():
        _load_setup(p, scr, io)
        p.dma("sp", io["bpB_sb"][:], io["bpB"], writes=["c_bpB"])
        emit_bn_finalize(p, io)
        if debug:
            p.dma("sp", io["dbg_sc"][:, 0:24], io["scB"][:], reads=["scB"])
            p.dma("sp", io["dbg_sc"][:, 24:48], io["biB"][:], reads=["biB"])
        for b in range(NBATCH):
            emit_l1_post(p, b, io)
            if debug and b == 0:
                p.dma("sp", io["dbg_oT"], io["outTbuf"], reads=["d_oT"])
                p.dma("sp", io["dbg_t3"], io["tTbuf"], reads=["d_tT"])
                p.dma("sp", io["dbg_X3"][0], io["X3buf"][0], reads=["d_X3"])
            emit_l1_ffn(p, b, io)
        if debug:
            p.dma("sp", io["dbg_X3"][1], io["X3buf"][1], reads=["d_X3"])
        p.finish()
    return nc, p


_CACHE = {}


def kernel(**inputs):
    g = prep_shared(inputs)
    x = np.asarray(inputs["L0_attn_x"], np.float32)
    if "A" not in _CACHE:
        _CACHE["A"] = build_launchA()
        _CACHE["B"] = build_launchB()
    ncA, _ = _CACHE["A"]
    ncB, _ = _CACHE["B"]
    xsh = x.reshape(8, NBATCH, S, D)
    keysA = ["wdqT", "wdkvT", "wupqT", "wupkT", "wupvT", "woutT", "lnsB",
             "lnbB", "ropeC", "ropeS", "WrT", "W1", "b1c", "W2", "b2B",
             "WqT", "WkT", "WvT", "ident", "constv"]
    in_mapsA = [dict({k: g[k] for k in keysA}, x=np.ascontiguousarray(xsh[c]))
                for c in range(8)]
    resA = run_bass_kernel_spmd(ncA, in_mapsA, core_ids=list(range(8)))
    allstats = np.ascontiguousarray(
        np.stack([resA.results[c]["stats_out"] for c in range(8)], axis=-1))
    keysB = ["bn_gam", "bn_bet", "WpT", "bpB", "wgT", "wuT", "wdT", "ident",
             "constv"]
    in_mapsB = [dict({k: g[k] for k in keysB},
                     X2in=resA.results[c]["X2out"],
                     h_in=resA.results[c]["h_out"],
                     allstats=allstats)
                for c in range(8)]
    resB = run_bass_kernel_spmd(ncB, in_mapsB, core_ids=list(range(8)))
    out = np.concatenate([resB.results[c]["OUT"] for c in range(8)], axis=0)
    return np.ascontiguousarray(out.reshape(16, S, D).astype(np.float32))


# version 03
# speedup vs baseline: 1.0908x; 1.0908x over previous
import numpy as np
import ml_dtypes

import concourse.bass as bass
import concourse.mybir as mybir
from concourse.bass_utils import run_bass_kernel_spmd

F32 = mybir.dt.float32
BF16 = mybir.dt.bfloat16
I32 = mybir.dt.int32
AF = mybir.ActivationFunctionType
ALU = mybir.AluOpType
AX = mybir.AxisListType

P = 128
NBATCH = 2
S = 2048
D = 1024
NB = S // P
EPS = 1e-5
S127 = 1.0 / 127.0
S2_FFN = S127 * S127

CONSTS = [EPS, -0.5, 0.125, 1.0 / D, 1.0, -1.0, S127, S2_FFN,
          1.0 / 32768.0, 0.0, 4.0, -4.0, 0.5,
          S2_FFN / D, (S2_FFN * S2_FFN) / D]

ENGS = ("tensor", "vector", "scalar", "pool", "sp")
DMA_QUEUES = {"sp": "sp", "pool": "pool"}
DMA_RING = 12


class Prog:

    def __init__(self, nc):
        self.nc = nc
        from contextlib import ExitStack
        self.ctx = ExitStack()
        self.eng = {"tensor": nc.tensor, "vector": nc.vector,
                    "scalar": nc.scalar, "pool": nc.gpsimd, "sp": nc.sync}
        self.esem = {e: self.ctx.enter_context(nc.semaphore(f"es_{e}")) for e in ENGS}
        self.ecnt = {e: 0 for e in ENGS}
        self.qsem = {q: [self.ctx.enter_context(nc.semaphore(f"q{q}{i}"))
                         for i in range(DMA_RING)] for q in DMA_QUEUES}
        self.qcnt = {q: 0 for q in DMA_QUEUES}
        self.quses = {q: [0] * DMA_RING for q in DMA_QUEUES}
        self.waited = {}
        self.pending = {}
        self.res = {}
        self.n_waits = 0
        self.consts_tile = None

    def _stream_of(self, tok):
        return ("eng", tok[1]) if tok[0] == "eng" else ("dmaq", tok[1])

    def _wait(self, stream, tok):
        if tok[0] == "eng":
            sem, val = self.esem[tok[1]], tok[2]
        else:
            _, q, slot, uses = tok
            sem, val = self.qsem[q][slot], 16 * uses
        key = (stream, sem.name)
        if self.waited.get(key, 0) >= val:
            return
        self.waited[key] = val
        pend = self.pending.setdefault(stream, {})
        if pend.get(sem, 0) < val:
            pend[sem] = val

    def _flush_pre(self, stream):
        pend = self.pending.pop(stream, None)
        if not pend:
            return None
        eng = stream[1] if stream[0] == "eng" else DMA_QUEUES[stream[1]]
        items = list(pend.items())
        for sem, val in items[:-1]:
            self.eng[eng].wait_ge(sem, val)
            self.n_waits += 1
        return items[-1]

    def _pre(self, stream, reads, writes):
        is_dma = stream[0] == "dmaq"
        for r in reads:
            e = self.res.get(r)
            if e and e["w"] is not None:
                w = e["w"]
                same = self._stream_of(w) == stream
                if (not same) or is_dma:
                    self._wait(stream, w)
                elif stream != ("eng", "tensor") and w[0] == "eng":
                    self._wait(stream, w)
        for wname in writes:
            e = self.res.get(wname)
            if not e:
                continue
            if e["w"] is not None and (self._stream_of(e["w"]) != stream or is_dma):
                self._wait(stream, e["w"])
            for rt in e["r"]:
                if self._stream_of(rt) != stream or is_dma:
                    self._wait(stream, rt)

    def _post(self, tok, reads, writes):
        for r in reads:
            self.res.setdefault(r, {"w": None, "r": []})["r"].append(tok)
        for w in writes:
            e = self.res.setdefault(w, {"w": None, "r": []})
            e["w"], e["r"] = tok, []

    def op(self, eng_name, fn, reads=(), writes=()):
        stream = ("eng", eng_name)
        self._pre(stream, reads, writes)
        last = self._flush_pre(stream)
        inst = fn(self.eng[eng_name])
        if last is not None:
            inst._wait_ge(last[0], last[1])
        self.ecnt[eng_name] += 1
        tok = ("eng", eng_name, self.ecnt[eng_name])
        inst.then_inc(self.esem[eng_name], 1)
        self._post(tok, reads, writes)
        return tok

    def dma(self, q, out, in_, reads=(), writes=(), **kw):
        stream = ("dmaq", q)
        slot = self.qcnt[q] % DMA_RING
        prev = self.quses[q][slot]
        if prev > 0:
            self._wait(stream, ("dma", q, slot, prev))
        self._pre(stream, reads, writes)
        last = self._flush_pre(stream)
        inst = self.eng[DMA_QUEUES[q]].dma_start(out=out, in_=in_, **kw)
        if last is not None:
            inst._wait_ge(last[0], last[1])
        self.qcnt[q] += 1
        self.quses[q][slot] += 1
        tok = ("dma", q, slot, self.quses[q][slot])
        inst.then_inc(self.qsem[q][slot], 16)
        self._post(tok, reads, writes)
        return tok

    def finish(self):
        stream = ("eng", "pool")
        for e in ENGS:
            if self.ecnt[e]:
                self._wait(stream, ("eng", e, self.ecnt[e]))
        for q in DMA_QUEUES:
            for s in range(DMA_RING):
                if self.quses[q][s]:
                    self._wait(stream, ("dma", q, s, self.quses[q][s]))
        pend = self.pending.pop(stream, None)
        if pend:
            for sem, val in pend.items():
                self.eng["pool"].wait_ge(sem, val)
        self.eng["pool"].nop()

    def sbuf(self, name, shape, dtype):
        return self.ctx.enter_context(self.nc.sbuf_tensor(name, list(shape), dtype))

    def psum(self, name, shape, dtype=F32):
        return self.ctx.enter_context(self.nc.psum_tensor(name, list(shape), dtype))

    def setup_consts(self):
        cst = self.sbuf("consts", (P, len(CONSTS)), F32)
        for j, v in enumerate(CONSTS):
            self.nc.const_aps.register(F32, float(v), cst[:, j:j + 1])
        self.consts_tile = cst
        return cst

    def cst(self, v):
        return self.consts_tile[:, CONSTS.index(v):CONSTS.index(v) + 1]

    def cstn(self, v, n):
        j = CONSTS.index(v)
        return self.consts_tile[0:n, j:j + 1]

    def cstb(self, v, base, n=64):
        j = CONSTS.index(v)
        return self.consts_tile[base:base + n, j:j + 1]


def bcastF(col_ap, n):
    return bass.AP(col_ap.tensor, col_ap.offset, [list(col_ap.ap[0]), [0, n]])



def _rope_tables():
    dim, base = 64, 10000.0
    inv = 1.0 / base ** (np.arange(0, dim, 2, dtype=np.float32) / dim)
    t = np.arange(S, dtype=np.float32)
    fr = np.outer(t, inv)
    cos = np.concatenate([np.cos(fr), np.cos(fr)], -1)[:, ::2]
    sin = np.concatenate([np.sin(fr), np.sin(fr)], -1)[:, ::2]
    return cos.astype(np.float32), sin.astype(np.float32)


def _deint_perm():
    p = []
    for h in range(16):
        p.extend(64 * h + np.arange(0, 64, 2))
        p.extend(64 * h + np.arange(1, 64, 2))
    return np.array(p)


def prep_shared(inputs):
    f32 = np.float32
    C = np.ascontiguousarray
    g = {}
    g["wdqT"] = C(inputs["L0_attn_w_down_q"].T)
    g["wdkvT"] = C(inputs["L0_attn_w_down_kv"].T)
    perm = _deint_perm()
    g["wupqT"] = C(inputs["L0_attn_w_up_q"][perm].T)
    g["wupkT"] = C(inputs["L0_attn_w_up_k"][perm].T)
    g["wupvT"] = C(inputs["L0_attn_w_up_v"].T)
    g["woutT"] = C(inputs["L0_attn_w_out"].T)
    g["lnsB"] = np.tile(np.asarray(inputs["L0_attn_ln_scale"], f32)[None, :], (P, 1))
    g["lnbB"] = np.tile(np.asarray(inputs["L0_attn_ln_bias"], f32)[None, :], (P, 1))
    cos, sin = _rope_tables()
    g["ropeC"] = C(cos.reshape(NB, P, 32).transpose(1, 0, 2))
    g["ropeS"] = C(sin.reshape(NB, P, 32).transpose(1, 0, 2))
    g["WrT"] = C(inputs["L0_ffn_Wr"].T)
    g["W1"] = C(np.asarray(inputs["L0_ffn_W1"])[7])
    g["b1c"] = C(np.asarray(inputs["L0_ffn_b1"], f32)[7].reshape(32, P).T)
    g["W2"] = C(np.asarray(inputs["L0_ffn_W2"])[7])
    g["b2B"] = np.tile(np.asarray(inputs["L0_ffn_b2"], f32)[7][None, :], (P, 1))
    g["WqT"] = C(inputs["L1_attn_Wq"].T)
    g["WkT"] = C(inputs["L1_attn_Wk"].T)
    g["WvT"] = C(inputs["L1_attn_Wv"].T)
    g["WpT"] = C(inputs["L1_attn_Wp"].T)
    g["bpB"] = np.tile(np.asarray(inputs["L1_attn_bp"], f32)[None, :], (P, 1))
    gam = np.stack([np.asarray(inputs["L1_attn_g" + b], f32) for b in "qkv"])
    bet = np.stack([np.asarray(inputs["L1_attn_b" + b], f32) for b in "qkv"])
    g["bn_gam"] = C(gam.reshape(3, 8, P).transpose(2, 0, 1))
    g["bn_bet"] = C(bet.reshape(3, 8, P).transpose(2, 0, 1))
    g["wgT"] = C(np.asarray(inputs["L1_ffn_w_gate"]).T)
    g["wuT"] = C(np.asarray(inputs["L1_ffn_w_up"]).T)
    g["wdT"] = C(np.asarray(inputs["L1_ffn_w_down"]).T)
    g["ident"] = np.eye(P, dtype=ml_dtypes.bfloat16)
    g["constv"] = np.tile(np.array(CONSTS, f32), (P, 1))
    return g



class TinyStats:

    def __init__(self, p, name, ncols=8):
        self.t = p.sbuf(name, (P, ncols), F32)
        self.name = name

    def col(self, j):
        return self.t[:, j:j + 1]


def emit_boundary_chunk(p, cfg, b, tc, y_psum, y_psum_res):
    scr = cfg["scr"]
    u = scr["u"]; xold = scr["xold"][tc % 2]; xnew = scr["xnew"]
    sqj = scr["sqj"]; st = cfg["st"]
    uref = "bscr_u"
    xo_res = f"xold{tc % 2}"
    parts = y_psum if isinstance(y_psum, list) else [(y_psum, y_psum_res, 0, D)]
    if cfg.get("_u_ready"):
        pass
    elif cfg.get("trec") is not None:
        r1, nb1 = cfg["trec"]
        tr = scr["trec"]
        p.op("scalar", lambda s: s.activation(
            tr[:], xold[:], AF.Identity,
            scale=r1[:, tc:tc + 1], bias=nb1[:, tc:tc + 1]),
            reads=[xo_res, cfg["trec_res"]], writes=["bscr_trec"])
        for ap, res, c0, c1 in parts:
            p.op("vector", lambda v, ap=ap, c0=c0, c1=c1: v.tensor_add(
                u[:, c0:c1], ap, tr[:, c0:c1]),
                reads=[res, "bscr_trec"], writes=[uref])
    elif cfg.get("pre_add") is not None:
        pap, pres = cfg["pre_add"]
        for ap, res, c0, c1 in parts:
            p.op("vector", lambda v, ap=ap, c0=c0, c1=c1: v.tensor_add(
                u[:, c0:c1], ap, pap[:, c0:c1]),
                reads=[res, pres], writes=[uref])
    else:
        for ap, res, c0, c1 in parts:
            p.op("vector", lambda v, ap=ap, c0=c0, c1=c1: v.tensor_copy(
                u[:, c0:c1], ap),
                reads=[res], writes=[uref])
    su, sq = st.col(0), st.col(1)
    p.op("vector", lambda v: v.reduce_sum(su, u[:], axis=AX.X),
         reads=[uref], writes=["st_su"])
    p.op("scalar", lambda s: s.activation(sqj[:], u[:], AF.Square, accum_out=sq),
         reads=[uref], writes=["st_sq", "bscr_sqj"])
    alpha = cfg.get("alpha")
    m_y, e2_y, var, r, scaleA, nb_ = (st.col(2), st.col(3), st.col(4),
                                      st.col(5), st.col(6), st.col(7))
    cD = p.cst(1.0 / D)
    if alpha is None:
        p.op("vector", lambda v: v.tensor_scalar(m_y, su, cD, None, ALU.mult),
             reads=["st_su"], writes=["st_m"])
        p.op("vector", lambda v: v.tensor_scalar(e2_y, sq, cD, None, ALU.mult),
             reads=["st_sq"], writes=["st_e2"])
    elif isinstance(alpha, tuple) and alpha[0] == "col":
        acol, ares = alpha[1], alpha[2]
        p.op("vector", lambda v: v.tensor_scalar(m_y, su, cD, None, ALU.mult),
             reads=["st_su"], writes=["st_m"])
        p.op("vector", lambda v: v.tensor_mul(m_y, m_y, acol),
             reads=["st_m", ares], writes=["st_m"])
        p.op("vector", lambda v: v.tensor_scalar(e2_y, sq, cD, None, ALU.mult),
             reads=["st_sq"], writes=["st_e2"])
        p.op("vector", lambda v: v.tensor_mul(e2_y, e2_y, acol),
             reads=["st_e2", ares], writes=["st_e2"])
        p.op("vector", lambda v: v.tensor_mul(e2_y, e2_y, acol),
             reads=["st_e2", ares], writes=["st_e2"])
    else:
        _, ca, ca_m, ca_e = alpha
        p.op("vector", lambda v: v.tensor_scalar(m_y, su, p.cst(ca_m), None, ALU.mult),
             reads=["st_su"], writes=["st_m"])
        p.op("vector", lambda v: v.tensor_scalar(e2_y, sq, p.cst(ca_e), None, ALU.mult),
             reads=["st_sq"], writes=["st_e2"])
    p.op("vector", lambda v: v.tensor_mul(var, m_y, m_y),
         reads=["st_m"], writes=["st_var"])
    p.op("vector", lambda v: v.tensor_sub(var, e2_y, var),
         reads=["st_e2", "st_var"], writes=["st_var"])
    p.op("scalar", lambda s: s.activation(r, var, AF.Ln, bias=p.cst(EPS)),
         reads=["st_var"], writes=["st_r"])
    p.op("scalar", lambda s: s.activation(r, r, AF.Exp, scale=p.cst(-0.5)),
         reads=["st_r"], writes=["st_r"])
    if alpha is None:
        p.op("vector", lambda v: v.tensor_copy(scaleA, r),
             reads=["st_r"], writes=["st_sa"])
    elif isinstance(alpha, tuple) and alpha[0] == "col":
        p.op("vector", lambda v: v.tensor_mul(scaleA, r, alpha[1]),
             reads=["st_r", alpha[2]], writes=["st_sa"])
    else:
        p.op("vector", lambda v: v.tensor_scalar(scaleA, r, p.cst(alpha[1]), None,
                                                 ALU.mult),
             reads=["st_r"], writes=["st_sa"])
    p.op("vector", lambda v: v.tensor_mul(nb_, m_y, r),
         reads=["st_m", "st_r"], writes=["st_nb"])
    p.op("vector", lambda v: v.tensor_scalar(nb_, nb_, p.cst(-1.0), None, ALU.mult),
         reads=["st_nb"], writes=["st_nb"])
    p.op("vector", lambda v: v.tensor_scalar(xnew[:], u[:], scaleA, nb_,
                                             ALU.mult, ALU.add),
         reads=[uref, "st_sa", "st_nb"], writes=["bscr_xnew"])
    p.op("vector", lambda v: v.tensor_add(xnew[:], xnew[:], xold[:]),
         reads=["bscr_xnew", xo_res], writes=["bscr_xnew"])
    p.dma(cfg.get("out_q", "pool"), cfg["xnew_dram"](b, tc), xnew[:],
          reads=["bscr_xnew"], writes=[cfg.get("xnew_res", "d_xnew")])
    nxt = cfg.get("next")
    if nxt is None:
        return
    Rarr, NBarr, Rres, tT_store = nxt
    su2, sq2 = st.col(0), st.col(1)
    p.op("vector", lambda v: v.reduce_sum(su2, xnew[:], axis=AX.X),
         reads=["bscr_xnew"], writes=["st_su"])
    p.op("scalar", lambda s: s.activation(sqj[:], xnew[:], AF.Square, accum_out=sq2),
         reads=["bscr_xnew"], writes=["st_sq", "bscr_sqj"])
    m2, e22, v2, r2 = st.col(2), st.col(3), st.col(4), st.col(5)
    p.op("vector", lambda v: v.tensor_scalar(m2, su2, cD, None, ALU.mult),
         reads=["st_su"], writes=["st_m"])
    p.op("vector", lambda v: v.tensor_scalar(e22, sq2, cD, None, ALU.mult),
         reads=["st_sq"], writes=["st_e2"])
    p.op("vector", lambda v: v.tensor_mul(v2, m2, m2), reads=["st_m"], writes=["st_var"])
    p.op("vector", lambda v: v.tensor_sub(v2, e22, v2),
         reads=["st_e2", "st_var"], writes=["st_var"])
    p.op("scalar", lambda s: s.activation(r2, v2, AF.Ln, bias=p.cst(EPS)),
         reads=["st_var"], writes=["st_r"])
    p.op("scalar", lambda s: s.activation(r2, r2, AF.Exp, scale=p.cst(-0.5)),
         reads=["st_r"], writes=["st_r"])
    p.op("vector", lambda v: v.tensor_copy(Rarr[:, tc:tc + 1], r2),
         reads=["st_r"], writes=[Rres])
    p.op("vector", lambda v: v.tensor_mul(NBarr[:, tc:tc + 1], m2, r2),
         reads=["st_m", "st_r"], writes=[Rres])
    p.op("vector", lambda v: v.tensor_scalar(NBarr[:, tc:tc + 1],
                                             NBarr[:, tc:tc + 1],
                                             p.cst(-1.0), None, ALU.mult),
         reads=[Rres], writes=[Rres])
    tn = scr["tn"]
    p.op("scalar", lambda s: s.activation(tn[:], xnew[:], AF.Identity,
                                          scale=Rarr[:, tc:tc + 1],
                                          bias=NBarr[:, tc:tc + 1]),
         reads=["bscr_xnew", Rres], writes=["bscr_tn"])
    tpp = cfg["tr_psum"]
    for dc in range(8):
        p.op("tensor", lambda t, dc=dc: t.transpose(
            tpp[:, dc * P:(dc + 1) * P], tn[:, dc * P:(dc + 1) * P],
            scr["ident"][:]),
            reads=["bscr_tn", "c_ident"], writes=["ptr"])
    kind = tT_store[0]
    if kind == "sbuf":
        tT = tT_store[1]
        p.op("vector", lambda v: v.tensor_copy(
            tT[:, :, tc * P:(tc + 1) * P],
            tpp[:].rearrange("p (c n) -> p c n", c=8)),
            reads=["ptr"], writes=[tT_store[2]])
    else:
        stg = scr["tT_stage"]
        p.op("vector", lambda v: v.tensor_copy(stg[:], tpp[:]),
             reads=["ptr"], writes=["bscr_tstage"])
        p.dma("pool", tT_store[1](b, tc), stg[:].rearrange("p (c n) -> p c n", c=8),
              reads=["bscr_tstage"], writes=["d_tT"])


def emit_pre_ln_from_dram(p, cfg, b, x_dram_fn, Rarr, NBarr, Rres, tT_store,
                          inner=None):
    scr = cfg["scr"]; st = cfg["st"]
    cD = p.cst(1.0 / D)
    for tc in range(NB):
        xold = scr["xold"][tc % 2]
        xo_res = f"xold{tc % 2}"
        p.dma("sp", xold[:], x_dram_fn(b, tc), writes=[xo_res])
        su, sq = st.col(0), st.col(1)
        p.op("vector", lambda v: v.reduce_sum(su, xold[:], axis=AX.X),
             reads=[xo_res], writes=["st_su"])
        p.op("scalar", lambda s: s.activation(scr["sqj"][:], xold[:], AF.Square,
                                              accum_out=sq),
             reads=[xo_res], writes=["st_sq", "bscr_sqj"])
        m2, e22, v2, r2 = st.col(2), st.col(3), st.col(4), st.col(5)
        p.op("vector", lambda v: v.tensor_scalar(m2, su, cD, None, ALU.mult),
             reads=["st_su"], writes=["st_m"])
        p.op("vector", lambda v: v.tensor_scalar(e22, sq, cD, None, ALU.mult),
             reads=["st_sq"], writes=["st_e2"])
        p.op("vector", lambda v: v.tensor_mul(v2, m2, m2),
             reads=["st_m"], writes=["st_var"])
        p.op("vector", lambda v: v.tensor_sub(v2, e22, v2),
             reads=["st_e2", "st_var"], writes=["st_var"])
        p.op("scalar", lambda s: s.activation(r2, v2, AF.Ln, bias=p.cst(EPS)),
             reads=["st_var"], writes=["st_r"])
        p.op("scalar", lambda s: s.activation(r2, r2, AF.Exp, scale=p.cst(-0.5)),
             reads=["st_r"], writes=["st_r"])
        p.op("vector", lambda v: v.tensor_copy(Rarr[:, tc:tc + 1], r2),
             reads=["st_r"], writes=[Rres])
        p.op("vector", lambda v: v.tensor_mul(NBarr[:, tc:tc + 1], m2, r2),
             reads=["st_m", "st_r"], writes=[Rres])
        p.op("vector", lambda v: v.tensor_scalar(NBarr[:, tc:tc + 1],
                                                 NBarr[:, tc:tc + 1],
                                                 p.cst(-1.0), None, ALU.mult),
             reads=[Rres], writes=[Rres])
        if inner is None:
            tn = scr["tn"]
            p.op("scalar", lambda s: s.activation(tn[:], xold[:], AF.Identity,
                                                  scale=Rarr[:, tc:tc + 1],
                                                  bias=NBarr[:, tc:tc + 1]),
                 reads=[xo_res, Rres], writes=["bscr_tn"])
            src_res = "bscr_tn"
            src = tn
        else:
            lnsB, lnbB = inner
            tf = scr["trec"]
            p.op("scalar", lambda s: s.activation(tf[:], xold[:], AF.Identity,
                                                  scale=Rarr[:, tc:tc + 1],
                                                  bias=NBarr[:, tc:tc + 1]),
                 reads=[xo_res, Rres], writes=["bscr_trec"])
            su3, sq3 = st.col(0), st.col(1)
            p.op("vector", lambda v: v.reduce_sum(su3, tf[:], axis=AX.X),
                 reads=["bscr_trec"], writes=["st_su"])
            p.op("scalar", lambda s: s.activation(scr["sqj"][:], tf[:], AF.Square,
                                                  accum_out=sq3),
                 reads=["bscr_trec"], writes=["st_sq", "bscr_sqj"])
            m3, e3, v3, r3, nb3 = (st.col(2), st.col(3), st.col(4), st.col(5),
                                   st.col(6))
            p.op("vector", lambda v: v.tensor_scalar(m3, su3, cD, None, ALU.mult),
                 reads=["st_su"], writes=["st_m"])
            p.op("vector", lambda v: v.tensor_scalar(e3, sq3, cD, None, ALU.mult),
                 reads=["st_sq"], writes=["st_e2"])
            p.op("vector", lambda v: v.tensor_mul(v3, m3, m3),
                 reads=["st_m"], writes=["st_var"])
            p.op("vector", lambda v: v.tensor_sub(v3, e3, v3),
                 reads=["st_e2", "st_var"], writes=["st_var"])
            p.op("scalar", lambda s: s.activation(r3, v3, AF.Ln, bias=p.cst(EPS)),
                 reads=["st_var"], writes=["st_r"])
            p.op("scalar", lambda s: s.activation(r3, r3, AF.Exp, scale=p.cst(-0.5)),
                 reads=["st_r"], writes=["st_r"])
            p.op("vector", lambda v: v.tensor_mul(nb3, m3, r3),
                 reads=["st_m", "st_r"], writes=["st_nb"])
            p.op("vector", lambda v: v.tensor_scalar(nb3, nb3, p.cst(-1.0), None,
                                                     ALU.mult),
                 reads=["st_nb"], writes=["st_nb"])
            un = scr["u"]
            p.op("scalar", lambda s: s.activation(un[:], tf[:], AF.Identity,
                                                  scale=r3, bias=nb3),
                 reads=["bscr_trec", "st_r", "st_nb"], writes=["bscr_u"])
            tn = scr["tn"]
            p.op("vector", lambda v: v.tensor_mul(un[:], un[:], cfg["lnsB"][:]),
                 reads=["bscr_u", "c_lnsB"], writes=["bscr_u"])
            p.op("vector", lambda v: v.tensor_add(tn[:], un[:], cfg["lnbB"][:]),
                 reads=["bscr_u", "c_lnbB"], writes=["bscr_tn"])
            src_res = "bscr_tn"
            src = tn
        tpp = cfg["tr_psum"]
        for dc in range(8):
            p.op("tensor", lambda t, dc=dc: t.transpose(
                tpp[:, dc * P:(dc + 1) * P], src[:, dc * P:(dc + 1) * P],
                scr["ident"][:]),
                reads=[src_res, "c_ident"], writes=["ptr"])
        kind = tT_store[0]
        if kind == "sbuf":
            p.op("vector", lambda v, tc=tc: v.tensor_copy(
                tT_store[1][:, :, tc * P:(tc + 1) * P],
                tpp[:].rearrange("p (c n) -> p c n", c=8)),
                reads=["ptr"], writes=[tT_store[2]])
        else:
            stg = scr["tT_stage"]
            p.op("vector", lambda v: v.tensor_copy(stg[:], tpp[:]),
                 reads=["ptr"], writes=["bscr_tstage"])
            p.dma("pool", tT_store[1](b, tc),
                  stg[:].rearrange("p (c n) -> p c n", c=8),
                  reads=["bscr_tstage"], writes=["d_tT"])



def _evac_copy(p, eng, dst, src, reads, writes):
    if eng == "scalar":
        p.op("scalar", lambda s: s.copy(dst, src), reads=reads, writes=writes)
    else:
        p.op("vector", lambda v: v.tensor_copy(dst, src), reads=reads, writes=writes)


def _takeover(p, news, olds):
    toks = []
    for o in olds:
        e = p.res.pop(o, None)
        if e:
            if e["w"] is not None:
                toks.append(e["w"])
            toks.extend(e["r"])
    if isinstance(news, str):
        news = [news]
    for new in news:
        ent = p.res.setdefault(new, {"w": None, "r": []})
        ent["r"].extend(list(toks))


def emit_bn_finalize(p, io):
    scr, st = io["scr"], io["st"]
    SSin = scr["SSin"]
    SS = scr["SS"]
    p.dma("sp", SSin[:], io["allstats"], writes=["SSin"])
    p.op("vector", lambda v: v.tensor_reduce(SS[:], SSin[:], AX.X, ALU.add),
         reads=["SSin"], writes=["SS"])
    mu, e2, var, r = scr["mu24"], scr["e224"], scr["var24"], scr["r24"]
    su_v = bass.AP(SS[:].tensor, SS[:].offset, [list(SS[:].ap[0]), [2, 24]])
    sq_v = bass.AP(SS[:].tensor, SS[:].offset + 1, [list(SS[:].ap[0]), [2, 24]])
    c32k = p.cst(1.0 / 32768.0)
    p.op("vector", lambda v: v.tensor_scalar(mu[:], su_v, c32k, None, ALU.mult),
         reads=["SS"], writes=["mu24"])
    p.op("vector", lambda v: v.tensor_scalar(e2[:], sq_v, c32k, None, ALU.mult),
         reads=["SS"], writes=["e224"])
    p.op("vector", lambda v: v.tensor_mul(var[:], mu[:], mu[:]),
         reads=["mu24"], writes=["var24"])
    p.op("vector", lambda v: v.tensor_sub(var[:], e2[:], var[:]),
         reads=["e224", "var24"], writes=["var24"])
    p.op("scalar", lambda s: s.activation(r[:], var[:], AF.Ln, bias=p.cst(EPS)),
         reads=["var24"], writes=["r24"])
    p.op("scalar", lambda s: s.activation(r[:], r[:], AF.Exp, scale=p.cst(-0.5)),
         reads=["r24"], writes=["r24"])
    gamB, betB = scr["gamB"], scr["betB"]
    scB, biB = io["scB"], io["biB"]
    p.dma("sp", gamB[:], io["bn_gam"].rearrange("p a b -> p (a b)"), writes=["gamB"])
    p.dma("sp", betB[:], io["bn_bet"].rearrange("p a b -> p (a b)"), writes=["betB"])
    gr = scr["gr24"]
    p.op("vector", lambda v: v.tensor_mul(gr[:], gamB[:], r[:]),
         reads=["gamB", "r24"], writes=["gr24"])
    p.op("vector", lambda v: v.tensor_scalar(scB[:], gr[:], p.cst(4.0), None,
                                             ALU.mult),
         reads=["gr24"], writes=["scB"])
    p.op("vector", lambda v: v.tensor_mul(var[:], mu[:], gr[:]),
         reads=["mu24", "gr24"], writes=["var24"])
    p.op("vector", lambda v: v.tensor_sub(var[:], betB[:], var[:]),
         reads=["betB", "var24"], writes=["var24"])
    p.op("vector", lambda v: v.tensor_scalar(biB[:], var[:], p.cst(4.0),
                                             p.cst(-4.0), ALU.mult, ALU.add),
         reads=["var24"], writes=["biB"])


def emit_l1_post(p, b, io):
    A = io["A4"]
    hq, bq, hv, vn = A["hq"], A["bq"], A["hv"], A["vn"]
    outT, WpT = A["outT"], A["WpT"]
    q0, q1, q2, y0, y1 = io["q0"], io["q1"], io["q2"], io["y0"], io["y1"]
    scr, st = io["scr"], io["st"]
    scB, biB = io["scB"], io["biB"]
    stg2 = scr["stg2"]
    _takeover(p, ["zq", "zbq", "zv", "zvn"],
              ["zwg", "zt3", "zHg", "wut0", "wut1", "wut2", "wut3",
               "wdt0", "wdt1", "wdt2", "wdt3", "castg", "zWpT"])
    p.dma("sp", hq[:], io["h_in"][0, b].rearrange("c p n -> p c n"), writes=["zq"])
    p.dma("sp", hv[:], io["h_in"][2, b].rearrange("c p n -> p c n"), writes=["zv"])
    for ec in range(8):
        p.op("scalar", lambda s, ec=ec: s.activation(
            bq[:, ec, :], hq[:, ec, :], AF.Sigmoid,
            scale=scB[:, ec:ec + 1], bias=biB[:, ec:ec + 1]),
            reads=["zq", "scB", "biB"], writes=["zbq"])
    for ec in range(8):
        p.dma("sp", stg2[:], io["h_in"][1, b, ec], writes=["stg2"])
        p.op("scalar", lambda s, ec=ec: s.activation(
            stg2[:], stg2[:], AF.Sigmoid,
            scale=scB[:, 8 + ec:9 + ec], bias=biB[:, 8 + ec:9 + ec]),
            reads=["stg2", "scB", "biB"], writes=["stg2"])
        p.op("vector", lambda v, ec=ec: v.tensor_mul(
            hq[:, ec, :], stg2[:], bq[:, ec, :]),
            reads=["stg2", "zbq", "zq"], writes=["zq"])
    qk = hq
    for ec in range(8):
        p.op("scalar", lambda s, ec=ec: s.activation(
            stg2[:], hv[:, ec, :], AF.Sigmoid,
            scale=scB[:, 16 + ec:17 + ec], bias=biB[:, 16 + ec:17 + ec]),
            reads=["zv", "scB", "biB"], writes=["stg2"])
        for tc in range(NB):
            p.op("tensor", lambda t, tc=tc: t.transpose(
                io["tr"][:, 0:P], stg2[:, tc * P:(tc + 1) * P], scr["ident"][:]),
                reads=["stg2", "c_ident"], writes=["ptr"])
            p.op("vector", lambda v, tc=tc, ec=ec: v.tensor_copy(
                vn[:, tc, ec * P:(ec + 1) * P], io["tr"][:, 0:P]),
                reads=["ptr"], writes=["zvn"])
    vv_sb = scr["vv_sb"]
    for h in range(16):
        base = 64 * (h % 2)
        for tc in range(NB):
            p.op("tensor", lambda t, h=h, tc=tc, base=base: t.matmul(
                q2[base:base + 64, 0:64], vn[:, tc, h * 64:(h + 1) * 64],
                vn[:, tc, h * 64:(h + 1) * 64],
                start=(tc == 0), stop=(tc == NB - 1),
                tile_position=(0, base)),
                reads=["zvn"], writes=["pq2"])
        p.op("scalar", lambda s, h=h, base=base: s.activation(
            vv_sb[base:base + 64, h // 2, :], q2[base:base + 64, 0:64],
            AF.Identity, scale=p.cstb(0.125, base)),
            reads=["pq2"], writes=["vv_sb"])
    ostg = scr["aostg"]
    for h in range(16):
        base = 64 * (h % 2)
        for qt in range(4):
            p.op("tensor", lambda t, h=h, qt=qt, base=base: t.matmul(
                q0[0:64, :], vv_sb[base:base + 64, h // 2, :],
                qk[base:base + 64, h // 2, qt * 512:(qt + 1) * 512],
                start=True, stop=True),
                reads=["vv_sb", "zq"], writes=["pq0"])
            p.op("vector", lambda v: v.tensor_copy(ostg[:], q0[0:64, :]),
                 reads=["pq0"], writes=["aostg"])
            p.dma("pool", io["outTbuf"][base:base + 64, h // 2,
                                      qt * 512:(qt + 1) * 512],
                  ostg[:], reads=["aostg"], writes=["d_oT"])
    _takeover(p, ["zWpT"], ["zv"])
    p.dma("pool", WpT[:], io["WpT"].rearrange("(c p) m -> p c m", p=P),
          writes=["zWpT"])
    for tc in range(NB):
        otile = scr["otile"][tc % 2]
        p.dma("sp", otile[:], io["outTbuf"][:, :, tc * P:(tc + 1) * P],
              reads=["d_oT"], writes=[f"otile{tc % 2}"])
        p.dma("sp", scr["xold"][tc % 2][:], io["X2in"][b, tc * P:(tc + 1) * P, :],
              writes=[f"xold{tc % 2}"])
        yb, ybres = (y0, "py0") if tc % 2 == 0 else (y1, "py1")
        for k in range(8):
            for h2 in range(2):
                p.op("tensor", lambda t, k=k, tc=tc, h2=h2, yb=yb,
                     otile=otile: t.matmul(
                    yb[:, h2 * 512:(h2 + 1) * 512],
                    otile[:, k, :],
                    WpT[:, k, h2 * 512:(h2 + 1) * 512],
                    start=(k == 0), stop=(k == 7)),
                    reads=[f"otile{tc % 2}", "zWpT"], writes=[ybres])
        cfg = dict(io["bcfg"])
        cfg["pre_add"] = (io["bpB_sb"][:], "c_bpB")
        cfg["xnew_dram"] = lambda b_, tc_: io["X3buf"][b_, tc_ * P:(tc_ + 1) * P, :]
        cfg["xnew_res"] = "d_X3"
        cfg["next"] = (io["R3"], io["NB3"], "R3",
                       ("dram", lambda b_, tc_: io["tTbuf"][:, :, tc_ * P:(tc_ + 1) * P]))
        emit_boundary_chunk(p, cfg, b, tc, yb[:], ybres)


def emit_l1_ffn(p, b, io):
    A = io["A5"]
    wg, t3T, Hg, wut, wdt = A["wg"], A["t3T"], A["Hg"], A["wut"], A["wdt"]
    q0, q1, q2, y0, y1 = io["q0"], io["q1"], io["q2"], io["y0"], io["y1"]
    scr, st = io["scr"], io["st"]
    sstg, ustg = scr["sstg"], scr["ustg"]
    _takeover(p, ["zwg", "zt3", "zHg", "wut0", "wut1", "wut2", "wut3",
               "wdt0", "wdt1", "wdt2", "wdt3", "castg"],
              ["zq", "zbq", "zv", "zvn", "zWpT"])
    if b == 0:
        stg = scr["castg"]
        for wname, dst, nrc in (("wgT", io["wg_bf"], 8), ("wuT", io["wu_bf"], 8),
                                ("wdT", io["wd_bf"], 24)):
            src = io[wname]
            for rc in range(nrc):
                w = src.shape[1]
                p.dma("pool", stg[:, 0:w], src[rc * P:(rc + 1) * P, :],
                      writes=["castg"])
                p.dma("pool", dst[rc * P:(rc + 1) * P, :], stg[:, 0:w],
                      reads=["castg"], writes=["d_wbf"])
    p.dma("sp", wg[:], io["wg_bf"].rearrange("(c p) m -> p c m", p=P),
          reads=["d_wbf"], writes=["zwg"])
    p.dma("sp", t3T[:], io["tTbuf"], reads=["d_tT"], writes=["zt3"])
    groups = [list(range(g * 3, min(g * 3 + 3, NB))) for g in range(6)]
    for grp in groups:
        glen = len(grp) * P
        t0 = grp[0] * P
        for hc in range(24):
            wt = wut[hc % 4]
            p.dma("sp", wt[:], io["wu_bf"][:, hc * P:(hc + 1) * P]
                  .rearrange("(c p) m -> p c m", p=P), reads=["d_wbf"],
                  writes=[f"wut{hc % 4}"])
            for k in range(8):
                p.op("tensor", lambda t, k=k, hc=hc: t.matmul(
                    q0[:, 0:glen], wg[:, k, hc * P:(hc + 1) * P],
                    t3T[:, k, t0:t0 + glen], start=(k == 0), stop=(k == 7)),
                    reads=["zwg", "zt3"], writes=["pq0"])
            for k in range(8):
                p.op("tensor", lambda t, k=k, wt=wt: t.matmul(
                    q1[:, 0:glen], wt[:, k, :],
                    t3T[:, k, t0:t0 + glen], start=(k == 0), stop=(k == 7)),
                    reads=[f"wut{hc % 4}", "zt3"], writes=["pq1"])
            p.op("scalar", lambda s: s.activation(sstg[:, 0:glen], q0[:, 0:glen],
                                                  AF.Silu, scale=p.cst(S127)),
                 reads=["pq0"], writes=["sstg"])
            p.op("scalar", lambda s: s.copy(ustg[:, 0:glen], q1[:, 0:glen]),
                 reads=["pq1"], writes=["ustg"])
            p.op("vector", lambda v, hc=hc: v.tensor_mul(
                Hg[:, hc, 0:glen], sstg[:, 0:glen], ustg[:, 0:glen]),
                reads=["sstg", "ustg"], writes=["zHg"])
        banks = [(y0, "py0"), (y1, "py1")]
        for hc in range(24):
            wt = wdt[hc % 4]
            p.dma("sp", wt[:], io["wd_bf"][hc * P:(hc + 1) * P, :],
                  reads=["d_wbf"], writes=[f"wdt{hc % 4}"])
            for mi, tc in enumerate(grp):
                m0 = (tc - grp[0]) * P
                if mi < 2:
                    ps, pres = banks[mi]
                    for h2 in range(2):
                        p.op("tensor", lambda t, hc=hc, m0=m0, h2=h2, ps=ps, wt=wt:
                             t.matmul(ps[:, h2 * 512:(h2 + 1) * 512],
                                      Hg[:, hc, m0:m0 + P],
                                      wt[:, h2 * 512:(h2 + 1) * 512],
                                      start=(hc == 0), stop=(hc == 23)),
                             reads=["zHg", f"wdt{hc % 4}"], writes=[pres])
                else:
                    for h2, (psh, presh) in enumerate(((q0, "pq0"), (q1, "pq1"))):
                        p.op("tensor", lambda t, hc=hc, m0=m0, h2=h2, psh=psh, wt=wt:
                             t.matmul(psh[:],
                                      Hg[:, hc, m0:m0 + P],
                                      wt[:, h2 * 512:(h2 + 1) * 512],
                                      start=(hc == 0), stop=(hc == 23)),
                             reads=["zHg", f"wdt{hc % 4}"], writes=[presh])
        for mi, tc in enumerate(grp):
            p.dma("sp", scr["xold"][tc % 2][:],
                  io["X3buf"][b, tc * P:(tc + 1) * P, :], reads=["d_X3"],
                  writes=[f"xold{tc % 2}"])
            cfg = dict(io["bcfg"])
            cfg["alpha"] = ("const", S2_FFN, S2_FFN / D, (S2_FFN * S2_FFN) / D)
            cfg["xnew_dram"] = lambda b_, tc_: io["OUT"][b_, tc_ * P:(tc_ + 1) * P, :]
            cfg["xnew_res"] = "d_OUT"
            cfg["next"] = None
            if mi < 2:
                ps, pres = banks[mi]
                emit_boundary_chunk(p, cfg, b, tc, ps[:], pres)
            else:
                emit_boundary_chunk(p, cfg, b, tc,
                                    [(q0[:], "pq0", 0, 512),
                                     (q1[:], "pq1", 512, D)], None)


def emit_boundary_chunk_2part(p, cfg, b, tc, parts):
    scr = cfg["scr"]
    u = scr["u"]
    (a0, r0), (a1, r1) = parts
    p.op("vector", lambda v: v.tensor_copy(u[:, 0:512], a0), reads=[r0],
         writes=["bscr_u"])
    p.op("vector", lambda v: v.tensor_copy(u[:, 512:1024], a1), reads=[r1],
         writes=["bscr_u"])
    cfg = dict(cfg)
    cfg["pre_add"] = None
    cfg["_u_ready"] = True
    emit_boundary_chunk(p, cfg, b, tc, u[:], "bscr_u")



def emit_l0_attn(p, b, io):
    A = io["A1"]
    zA, cqT, ckvT = A["zA"], A["cqT"], A["ckvT"]
    wdq, wdkv, wtile = A["wdq"], A["wdkv"], A["wtile"]
    ropeC, ropeS = A["ropeC"], A["ropeS"]
    vaug, qTt = A["vaug"], A["qTt"]
    q0, q1, q2, y0, y1 = io["q0"], io["q1"], io["q2"], io["y0"], io["y1"]
    scr, st = io["scr"], io["st"]

    _takeover(p, ["zB1", "zB2", "cqT", "ckvT", "vaug", "qTt", "aoTt"],
              ["w1t0", "w1t1", "w1t2", "w1t3", "w2t0", "w2t1", "w2t2", "w2t3",
               "Hg", "ropeCS", "P01a", "P01b", "WbrT", "wtile", "woutT"])
    p.dma("pool", wdq[:], io["wdqT"].rearrange("(c p) m -> p c m", p=P),
          writes=["zB1"])
    p.dma("pool", wdkv[:], io["wdkvT"].rearrange("(c p) m -> p c m", p=P),
          writes=["zB2"])
    for wres, wsb, dst, dres in (("zB1", wdq, cqT, "cqT"),
                                 ("zB2", wdkv, ckvT, "ckvT")):
        for oc in range(4):
            for tt in range(4):
                ps, pres = ((q0, "pq0") if (oc * 4 + tt) % 2 == 0 else (q1, "pq1"))
                for k in range(8):
                    p.op("tensor", lambda t, k=k, oc=oc, tt=tt, ps=ps, wsb=wsb:
                         t.matmul(ps[:], wsb[:, k, oc * P:(oc + 1) * P],
                                  zA[:, k, tt * 512:(tt + 1) * 512],
                                  start=(k == 0), stop=(k == 7)),
                         reads=[wres, "zA"], writes=[pres])
                p.op("scalar", lambda s, oc=oc, tt=tt, ps=ps, dst=dst:
                     s.copy(dst[:, oc, tt * 512:(tt + 1) * 512], ps[:]),
                     reads=[pres], writes=[dres])

    _takeover(p, ["wtile"], ["zB1"])
    _takeover(p, ["ropeCS", "P01a", "P01b"], ["zB2"])
    p.op("pool", lambda g: g.memset(vaug[:, :, :, 64:65], 1.0), writes=["vaug"])
    p.dma("pool", wtile[:], io["wupvT"].rearrange("(c p) m -> p c m", p=P),
          writes=["wtile"])
    for tc in range(NB):
        for k in range(4):
            for h2 in range(2):
                p.op("tensor", lambda t, k=k, tc=tc, h2=h2:
                     t.matmul(y0[:, h2 * 512:(h2 + 1) * 512],
                              ckvT[:, k, tc * P:(tc + 1) * P],
                              wtile[:, k, h2 * 512:(h2 + 1) * 512],
                              start=(k == 0), stop=(k == 3)),
                     reads=["ckvT", "wtile"], writes=["py0"])
        p.op("scalar", lambda s, tc=tc: s.copy(
            vaug[:, tc, :, 0:64],
            y0[:].rearrange("p (h d) -> p h d", h=16)),
            reads=["py0"], writes=["vaug"])

    p.dma("sp", ropeC[:], io["ropeC"], writes=["ropeCS"])
    p.dma("sp", ropeS[:], io["ropeS"], writes=["ropeCS"])

    def qk_produce(cT, cres, dst, dres, tcs, qtile_mode):
        for tc in tcs:
            for k in range(4):
                for h2 in range(2):
                    p.op("tensor", lambda t, k=k, tc=tc, h2=h2, cT=cT:
                         t.matmul(y0[:, h2 * 512:(h2 + 1) * 512],
                                  cT[:, k, tc * P:(tc + 1) * P],
                                  wtile[:, k, h2 * 512:(h2 + 1) * 512],
                                  start=(k == 0), stop=(k == 3)),
                         reads=[cres, "wtile"], writes=["py0"])
            yv = y0[:].rearrange("p (h e i) -> p h e i", h=16, e=2)
            E, O = yv[:, :, 0, :], yv[:, :, 1, :]
            cb = ropeC[:, tc, :]
            sb_ = ropeS[:, tc, :]
            Cb = bass.AP(cb.tensor, cb.offset, [list(cb.ap[0]), [0, 16], [1, 32]])
            Sb = bass.AP(sb_.tensor, sb_.offset, [list(sb_.ap[0]), [0, 16], [1, 32]])
            r1v = scr["rope1"][:].rearrange("p (h i) -> p h i", h=16)
            r2v = scr["rope2"][:].rearrange("p (h i) -> p h i", h=16)
            stg = scr["qkstage"]
            sv = stg[:].rearrange("p (h e i) -> p h e i", h=16, e=2)
            p.op("vector", lambda v: v.tensor_mul(r1v, E, Cb),
                 reads=["py0", "ropeCS"], writes=["rope1"])
            p.op("vector", lambda v: v.tensor_mul(r2v, O, Sb),
                 reads=["py0", "ropeCS"], writes=["rope2"])
            p.op("vector", lambda v: v.tensor_sub(sv[:, :, 0, :], r1v, r2v),
                 reads=["rope1", "rope2"], writes=["qkstage"])
            p.op("vector", lambda v: v.tensor_mul(r1v, E, Sb),
                 reads=["py0", "ropeCS"], writes=["rope1"])
            p.op("vector", lambda v: v.tensor_mul(r2v, O, Cb),
                 reads=["py0", "ropeCS"], writes=["rope2"])
            p.op("vector", lambda v: v.tensor_add(sv[:, :, 1, :], r1v, r2v),
                 reads=["rope1", "rope2"], writes=["qkstage"])
            for dc in range(8):
                p.op("tensor", lambda t, dc=dc: t.transpose(
                    io["tr"][:, dc * P:(dc + 1) * P],
                    stg[:, dc * P:(dc + 1) * P], scr["ident"][:]),
                    reads=["qkstage", "c_ident"], writes=["ptr"])
            ti = (tc % 4) if qtile_mode else tc
            p.op("vector", lambda v, ti=ti: v.tensor_copy(
                dst[:, :, ti * P:(ti + 1) * P],
                io["tr"][:].rearrange("p (c n) -> p c n", c=8)),
                reads=["ptr"], writes=[dres])

    p.dma("pool", wtile[:], io["wupkT"].rearrange("(c p) m -> p c m", p=P),
          writes=["wtile"])
    qk_produce(ckvT, "ckvT", zA, "zA", range(NB), False)

    p.dma("pool", wtile[:], io["wupqT"].rearrange("(c p) m -> p c m", p=P),
          writes=["wtile"])
    ones64 = scr["ones64"]
    for qt in range(4):
        qk_produce(cqT, "cqT", qTt, "qTt", range(4 * qt, 4 * qt + 4), True)
        for hp in range(8):
            for kc in range(NB):
                Sp, Sres = (y0, "py0") if kc % 2 == 0 else (y1, "py1")
                Pt, Pres = (scr["P01a"], "P01a") if kc % 2 == 0 else                     (scr["P01b"], "P01b")
                for hloc in range(2):
                    base = 64 * hloc
                    p.op("tensor", lambda t, kc=kc, hp=hp, base=base, Sp=Sp,
                         hloc=hloc: t.matmul(
                             Sp[:, hloc * 512:(hloc + 1) * 512],
                             zA[base:base + 64, hp, kc * P:(kc + 1) * P],
                             qTt[base:base + 64, hp, :],
                             start=True, stop=True),
                         reads=["zA", "qTt"], writes=[Sres])
                p.op("scalar", lambda sE, Pt=Pt, Sp=Sp: sE.activation(
                    Pt[:], Sp[:], AF.Exp, scale=p.cst(0.125)),
                    reads=[Sres], writes=[Pres])
                for hloc, (avp, avres) in enumerate(((q0, "pq0"), (q1, "pq1"))):
                    p.op("tensor", lambda t, kc=kc, hp=hp, hloc=hloc, Pt=Pt,
                         avp=avp: t.matmul(
                             avp[0:65, :], vaug[:, kc, 2 * hp + hloc, :],
                             Pt[:, hloc * 512:(hloc + 1) * 512],
                             start=(kc == 0), stop=(kc == NB - 1)),
                         reads=["vaug", Pres], writes=[avres])
            for hloc, (avp, avres) in enumerate(((q0, "pq0"), (q1, "pq1"))):
                srow, rrow, rb = scr["srow"], scr["rrow"], scr["rbsb"]
                aostg = scr["aostg"]
                p.op("vector", lambda v, avp=avp: v.tensor_copy(
                    srow[:], avp[64:65, :]),
                    reads=[avres], writes=["srow"])
                p.op("vector", lambda v: v.reciprocal(rrow[:], srow[:]),
                     reads=["srow"], writes=["rrow"])
                p.op("tensor", lambda t: t.matmul(q2[0:64, :], ones64[:],
                                                  rrow[:], start=True, stop=True),
                     reads=["rrow", "ones64"], writes=["pq2"])
                p.op("vector", lambda v: v.tensor_copy(rb[:], q2[0:64, :]),
                     reads=["pq2"], writes=["rbsb"])
                p.op("vector", lambda v, avp=avp: v.tensor_mul(
                    aostg[:], avp[0:64, :], rb[:]),
                    reads=[avres, "rbsb"], writes=["aostg"])
                p.dma("pool", io["aoTbuf"][hloc * 64:(hloc + 1) * 64, hp,
                                         qt * 512:(qt + 1) * 512],
                      aostg[:], reads=["aostg"], writes=["d_ao"])
    _takeover(p, ["woutT"], ["wtile", "ropeCS", "P01a", "P01b"])
    woutT = A["woutT"]
    aotile = scr["aotile"]
    p.dma("pool", woutT[:], io["woutT"].rearrange("(c p) m -> p c m", p=P),
          writes=["woutT"])
    for tc in range(NB):
        aot = aotile[tc % 2]
        p.dma("sp", aot[:], io["aoTbuf"][:, :, tc * P:(tc + 1) * P],
              reads=["d_ao"], writes=[f"aotile{tc % 2}"])
        p.dma("sp", scr["xold"][tc % 2][:], io["x_in"][b, tc * P:(tc + 1) * P, :],
              writes=[f"xold{tc % 2}"])
        yb, ybres = (y0, "py0") if tc % 2 == 0 else (y1, "py1")
        for k in range(8):
            for h2 in range(2):
                p.op("tensor", lambda t, k=k, h2=h2, yb=yb, aot=aot: t.matmul(
                    yb[:, h2 * 512:(h2 + 1) * 512], aot[:, k, :],
                    woutT[:, k, h2 * 512:(h2 + 1) * 512],
                    start=(k == 0), stop=(k == 7)),
                    reads=[f"aotile{tc % 2}", "woutT"], writes=[ybres])
        cfg = dict(io["bcfg"])
        cfg["trec"] = (io["R0"], io["NB0"])
        cfg["trec_res"] = "R0"
        cfg["xnew_dram"] = lambda b_, tc_: io["X1buf"][b_, tc_ * P:(tc_ + 1) * P, :]
        cfg["xnew_res"] = "d_X1"
        cfg["next"] = (io["R1"], io["NB1"], "R1",
                       ("dram", lambda b_, tc_: io["tTbuf"][:, :, tc_ * P:(tc_ + 1) * P]))
        emit_boundary_chunk(p, cfg, b, tc, yb[:], ybres)


def emit_l0_ffn(p, b, io):
    A = io["A2"]
    zA, Hg = A["zA"], A["Hg"]
    w1t, w2t, WrT = A["w1t"], A["w2t"], A["WrT"]
    q0, q1, q2, y0, y1 = io["q0"], io["q1"], io["q2"], io["y0"], io["y1"]
    scr, st = io["scr"], io["st"]
    _takeover(p, ["w1t0", "w1t1", "w1t2", "w1t3"], ["zB1", "woutT"])
    _takeover(p, ["w2t0", "w2t1", "w2t2", "w2t3", "Hg"],
              ["zB2", "cqT", "ckvT", "vaug", "P01a", "P01b", "ropeCS",
               "woutT", "qTt", "aoTt", "wtile"])
    p.dma("sp", zA[:], io["tTbuf"], reads=["d_tT"], writes=["zA"])
    if b == 0:
        p.dma("pool", WrT[:], io["WrT"].rearrange("(c p) m -> p c m", p=P),
              writes=["WrT"])
        for rc in range(32):
            p.dma("pool", scr["qkstage"][:], io["W2"][rc * P:(rc + 1) * P, :],
                  writes=["qkstage"])
            p.dma("pool", io["W2bf"][rc * P:(rc + 1) * P, :], scr["qkstage"][:],
                  reads=["qkstage"], writes=["d_W2bf"])
    p7 = io["p7"]
    for tc in range(NB):
        for k in range(8):
            p.op("tensor", lambda t, k=k, tc=tc: t.matmul(
                q2[:, 0:8], zA[:, k, tc * P:(tc + 1) * P], WrT[:, k, :],
                start=(k == 0), stop=(k == 7)),
                reads=["zA", "WrT"], writes=["pq2"])
        nmx, e8, ssum, rs = st.col(2), scr["e8"], st.col(3), st.col(4)
        p.op("vector", lambda v: v.reduce_max(nmx, q2[:, 0:8], axis=AX.X,
                                              negate=True),
             reads=["pq2"], writes=["st_m"])
        p.op("scalar", lambda s: s.activation(e8[:], q2[:, 0:8], AF.Exp, bias=nmx),
             reads=["pq2", "st_m"], writes=["e8"])
        p.op("vector", lambda v: v.reduce_sum(ssum, e8[:], axis=AX.X),
             reads=["e8"], writes=["st_e2"])
        p.op("vector", lambda v: v.reciprocal(rs, ssum),
             reads=["st_e2"], writes=["st_var"])
        p.op("vector", lambda v, tc=tc: v.tensor_mul(p7[:, tc:tc + 1],
                                                     e8[:, 7:8], rs),
             reads=["e8", "st_var"], writes=["p7"])
    groups = [list(range(g * 3, min(g * 3 + 3, NB))) for g in range(6)]
    for grp in groups:
        glen = len(grp) * P
        t0 = grp[0] * P
        for hc in range(32):
            wt = w1t[hc % 4]
            gb, gres = (q0, "pq0") if hc % 2 == 0 else (q1, "pq1")
            p.dma("pool", wt[:], io["W1"][:, hc * P:(hc + 1) * P]
                  .rearrange("(c p) m -> p c m", p=P), writes=[f"w1t{hc % 4}"])
            for k in range(8):
                p.op("tensor", lambda t, k=k, wt=wt, gb=gb: t.matmul(
                    gb[:, 0:glen], wt[:, k, :], zA[:, k, t0:t0 + glen],
                    start=(k == 0), stop=(k == 7)),
                    reads=[f"w1t{hc % 4}", "zA"], writes=[gres])
            p.op("scalar", lambda s, hc=hc, gb=gb: s.activation(
                Hg[:, hc, 0:glen], gb[:, 0:glen], AF.Gelu,
                bias=io["b1c_sb"][:, hc:hc + 1]),
                reads=[gres, "c_b1c"], writes=["Hg"])
        banks = [(y0, "py0"), (y1, "py1")]
        for hc in range(32):
            wt = w2t[hc % 4]
            p.dma("sp", wt[:], io["W2bf"][hc * P:(hc + 1) * P, :],
                  reads=["d_W2bf"], writes=[f"w2t{hc % 4}"])
            for mi, tc in enumerate(grp):
                m0 = (tc - grp[0]) * P
                if mi < 2:
                    ps, pres = banks[mi]
                    for h2 in range(2):
                        p.op("tensor", lambda t, hc=hc, m0=m0, h2=h2, ps=ps, wt=wt:
                             t.matmul(ps[:, h2 * 512:(h2 + 1) * 512],
                                      Hg[:, hc, m0:m0 + P],
                                      wt[:, h2 * 512:(h2 + 1) * 512],
                                      start=(hc == 0), stop=(hc == 31)),
                             reads=["Hg", f"w2t{hc % 4}"], writes=[pres])
                else:
                    for h2, (psh, presh) in enumerate(((q0, "pq0"), (q1, "pq1"))):
                        p.op("tensor", lambda t, hc=hc, m0=m0, psh=psh, wt=wt,
                             h2=h2: t.matmul(psh[:], Hg[:, hc, m0:m0 + P],
                                             wt[:, h2 * 512:(h2 + 1) * 512],
                                             start=(hc == 0), stop=(hc == 31)),
                             reads=["Hg", f"w2t{hc % 4}"], writes=[presh])
        for mi, tc in enumerate(grp):
            p.dma("sp", scr["xold"][tc % 2][:],
                  io["X1buf"][b, tc * P:(tc + 1) * P, :], reads=["d_X1"],
                  writes=[f"xold{tc % 2}"])
            cfg = dict(io["bcfg"])
            cfg["pre_add"] = (io["b2B_sb"][:], "c_b2B")
            cfg["alpha"] = ("col", p7[:, tc:tc + 1], "p7")
            cfg["xnew_dram"] = lambda b_, tc_: io["X2out"][b_, tc_ * P:(tc_ + 1) * P, :]
            cfg["xnew_res"] = "d_X2o"
            cfg["next"] = (io["R2"], io["NB2"], "R2",
                           ("dram", lambda b_, tc_: io["tTbuf"][:, :, tc_ * P:(tc_ + 1) * P]))
            if mi < 2:
                ps, pres = banks[mi]
                emit_boundary_chunk(p, cfg, b, tc, ps[:], pres)
            else:
                emit_boundary_chunk(p, cfg, b, tc,
                                    [(q0[:], "pq0", 0, 512),
                                     (q1[:], "pq1", 512, D)], None)


def emit_l1_branches(p, b, io):
    A = io["A3"]
    zA, WbrT = A["zA"], A["WbrT"]
    q0, q1 = io["q0"], io["q1"]
    scr, st = io["scr"], io["st"]
    statsA = io["statsA"]
    hstage = scr["hstage"]
    _takeover(p, ["WbrT"], ["w1t0", "w1t1", "w1t2", "w1t3", "w2t0", "w2t1", "w2t2", "w2t3"])
    p.dma("sp", zA[:], io["tTbuf"], reads=["d_tT"], writes=["zA"])
    for ibr, wname in enumerate(("WqT", "WkT", "WvT")):
        p.dma("pool", WbrT[:], io[wname].rearrange("(c p) m -> p c m", p=P),
              writes=["WbrT"])
        for ec in range(8):
            for tt in range(4):
                ps, pres = ((q0, "pq0") if tt % 2 == 0 else (q1, "pq1"))
                for k in range(8):
                    p.op("tensor", lambda t, k=k, ec=ec, tt=tt, ps=ps: t.matmul(
                        ps[:], WbrT[:, k, ec * P:(ec + 1) * P],
                        zA[:, k, tt * 512:(tt + 1) * 512],
                        start=(k == 0), stop=(k == 7)),
                        reads=["WbrT", "zA"], writes=[pres])
                su_t, sq_t = st.col(2), st.col(3)
                p.op("scalar", lambda s, ps=ps: s.activation(
                    hstage[:], ps[:], AF.Identity, accum_out=su_t),
                    reads=[pres], writes=["hstage", "st_m"])
                p.op("scalar", lambda s, ps=ps: s.activation(
                    scr["sqj512"][:], ps[:], AF.Square, accum_out=sq_t),
                    reads=[pres], writes=["sqj512", "st_e2"])
                p.op("vector", lambda v, ibr=ibr, ec=ec: v.tensor_add(
                    statsA[:, ibr, ec, 0:1], statsA[:, ibr, ec, 0:1], su_t),
                    reads=["st_m"], writes=["statsA"])
                p.op("vector", lambda v, ibr=ibr, ec=ec: v.tensor_add(
                    statsA[:, ibr, ec, 1:2], statsA[:, ibr, ec, 1:2], sq_t),
                    reads=["st_e2"], writes=["statsA"])
                p.dma("pool", io["h_out"][ibr, b, ec, :, tt * 512:(tt + 1) * 512],
                      hstage[:], reads=["hstage"])



def _mk_scr_common(p):
    scr = {}
    scr["xold"] = [p.sbuf("xold0", (P, D), F32), p.sbuf("xold1", (P, D), F32)]
    scr["u"] = p.sbuf("scr_u", (P, D), F32)
    scr["xnew"] = p.sbuf("scr_xnew", (P, D), F32)
    scr["tn"] = p.sbuf("scr_tn", (P, D), BF16)
    scr["sqj"] = p.sbuf("scr_sqj", (P, D), BF16)
    scr["tT_stage"] = p.sbuf("scr_tstage", (P, D), BF16)
    scr["ident"] = p.sbuf("identsb", (P, P), BF16)
    return scr


def _load_setup(p, scr, io):
    tok1 = p.dma("sp", p.consts_tile[:], io["constv"], writes=["c_const"])
    tok2 = p.dma("sp", scr["ident"][:], io["ident"], writes=["c_ident"])
    for e in ENGS:
        p._wait(("eng", e), tok1)
        p._wait(("eng", e), tok2)


def build_launchA(debug=False):
    nc = bass.Bass("TRN2", target_bir_lowering=False)
    io = {}
    di = nc.dram_tensor
    io["x_in"] = di("x", [NBATCH, S, D], F32, kind="ExternalInput").ap()
    for nm, shp, dt in (("wdqT", [D, 512], F32), ("wdkvT", [D, 512], F32),
                        ("wupqT", [512, D], F32), ("wupkT", [512, D], F32),
                        ("wupvT", [512, D], F32), ("woutT", [D, D], F32),
                        ("lnsB", [P, D], F32), ("lnbB", [P, D], F32),
                        ("ropeC", [P, NB, 32], F32), ("ropeS", [P, NB, 32], F32),
                        ("WrT", [D, 8], F32), ("W1", [D, 4096], F32),
                        ("b1c", [P, 32], F32), ("W2", [4096, D], F32),
                        ("b2B", [P, D], F32), ("WqT", [D, D], F32),
                        ("WkT", [D, D], F32), ("WvT", [D, D], F32),
                        ("ident", [P, P], BF16),
                        ("constv", [P, len(CONSTS)], F32)):
        io[nm] = di(nm, shp, dt, kind="ExternalInput").ap()
    io["X2out"] = di("X2out", [NBATCH, S, D], F32, kind="ExternalOutput").ap()
    io["h_out"] = di("h_out", [3, NBATCH, 8, P, S], BF16, kind="ExternalOutput").ap()
    io["stats_out"] = di("stats_out", [P, 48], F32, kind="ExternalOutput").ap()
    io["X1buf"] = di("X1buf", [NBATCH, S, D], F32, kind="Internal").ap()
    if debug:
        io["dbg_X1"] = di("dbg_X1", [NBATCH, S, D], F32, kind="ExternalOutput").ap()
        io["dbg_ao"] = di("dbg_ao", [P, 8, S], BF16, kind="ExternalOutput").ap()
        io["dbg_t1"] = di("dbg_t1", [P, 8, S], BF16, kind="ExternalOutput").ap()
        io["dbg_p7"] = di("dbg_p7", [P, NB], F32, kind="ExternalOutput").ap()
        io["dbg_W2bf"] = di("dbg_W2bf", [4096, D], BF16, kind="ExternalOutput").ap()
    io["tTbuf"] = di("tTbuf", [P, 8, S], BF16, kind="Internal").ap()
    io["aoTbuf"] = di("aoTbuf", [P, 8, S], BF16, kind="Internal").ap()
    io["W2bf"] = di("W2bf", [4096, D], BF16, kind="Internal").ap()

    p = Prog(nc)
    p.setup_consts()
    scr = _mk_scr_common(p)
    scr["trec"] = p.sbuf("scr_trec", (P, D), F32)
    scr["qkstage"] = p.sbuf("scr_qkst", (P, D), BF16)
    scr["rope1"] = p.sbuf("scr_rope1", (P, 512), F32)
    scr["rope2"] = p.sbuf("scr_rope2", (P, 512), F32)
    scr["e8"] = p.sbuf("scr_e8", (P, 8), F32)
    scr["srow"] = p.sbuf("scr_srow", (1, 512), F32)
    scr["rrow"] = p.sbuf("scr_rrow", (1, 512), F32)
    scr["rbsb"] = p.sbuf("scr_rbsb", (64, 512), F32)
    scr["ones64"] = p.sbuf("scr_ones64", (1, 64), F32)
    scr["aotile"] = [p.sbuf("scr_aotile0", (P, 8, P), BF16),
                     p.sbuf("scr_aotile1", (P, 8, P), BF16)]
    scr["aostg"] = p.sbuf("scr_aostg", (64, 512), BF16)
    scr["hstage"] = p.sbuf("scr_hstage", (P, 512), BF16)
    scr["sqj512"] = p.sbuf("scr_sqj512", (P, 512), BF16)
    io["lnsB_sb"] = p.sbuf("lnsB_sb", (P, D), F32)
    io["lnbB_sb"] = p.sbuf("lnbB_sb", (P, D), F32)
    io["b1c_sb"] = p.sbuf("b1c_sb", (P, 32), F32)
    io["b2B_sb"] = p.sbuf("b2B_sb", (P, D), F32)
    st = TinyStats(p, "tinyst", 8)
    for nm in ("R0", "NB0", "R1", "NB1", "R2", "NB2"):
        io[nm] = p.sbuf(nm, (P, NB), F32)
    io["p7"] = p.sbuf("p7arr", (P, NB), F32)
    io["statsA"] = p.sbuf("statsA", (P, 3, 8, 2), F32)

    AR = 131584
    arena = nc.alloc_sbuf_tensor("arena", [P, AR], mybir.dt.uint8)
    base = nc.lookup_mloc(arena).addr
    at = lambda nm, off, shp, dt: nc.alloc_sbuf_tensor_at(nm, list(shp), dt,
                                                          offset=base + off)
    A1 = {}
    A1["zA"] = at("zA_t", 0, (P, 8, S), BF16)
    A1["wdq"] = at("wdq_t", 32768, (P, 8, 512), BF16)
    A1["woutT"] = at("woutT_t", 32768, (P, 8, D), BF16)
    A1["wtile"] = at("wtile_t", 32768, (P, 4, D), BF16)
    A1["wdkv"] = at("wdkv_t", 40960, (P, 8, 512), BF16)
    A1["ropeC"] = at("ropeC_t", 40960, (P, NB, 32), F32)
    A1["ropeS"] = at("ropeS_t", 43008, (P, NB, 32), F32)
    scr["P01a"] = at("P01a_t", 45056, (P, D), BF16)
    scr["P01b"] = at("P01b_t", 47104, (P, D), BF16)
    A1["cqT"] = at("cqT_t", 49152, (P, 4, S), BF16)
    A1["ckvT"] = at("ckvT_t", 65536, (P, 4, S), BF16)
    A1["vaug"] = at("vaug_t", 81920, (P, NB, 16, 65), BF16)
    A1["qTt"] = at("qTt_t", 115200, (P, 8, 512), BF16)
    A1["aoTt"] = at("aoTt_t", 123392, (P, 8, 512), BF16)
    A2 = {"zA": A1["zA"]}
    A2["w1t"] = [at(f"w1t{i}_t", 32768 + 2048 * i, (P, 8, P), BF16)
                 for i in range(4)]
    A2["w2t"] = [at(f"w2t{i}_t", 40960 + 2048 * i, (P, D), BF16)
                 for i in range(4)]
    A2["Hg"] = at("Hg_t", 49152, (P, 32, 384), BF16)
    A2["WrT"] = p.sbuf("WrT_sb", (P, 8, 8), BF16)
    A3 = {"zA": A1["zA"], "WbrT": at("WbrT_t", 32768, (P, 8, D), BF16)}
    io["A1"], io["A2"], io["A3"] = A1, A2, A3

    q0 = p.psum("q0", (P, 512)); q1 = p.psum("q1", (P, 512))
    q2 = p.psum("q2", (P, 512))
    y0 = p.psum("y0", (P, D)); y1 = p.psum("y1", (P, D))
    tr = p.psum("trp", (P, D), BF16)
    io.update(q0=q0, q1=q1, q2=q2, y0=y0, y1=y1, tr=tr)
    io["scr"], io["st"] = scr, st
    io["bcfg"] = {"scr": scr, "st": st, "out_q": "pool", "tr_psum": tr}

    with nc.Block():
        _load_setup(p, scr, io)
        p.dma("sp", io["lnsB_sb"][:], io["lnsB"], writes=["c_lnsB"])
        p.dma("sp", io["lnbB_sb"][:], io["lnbB"], writes=["c_lnbB"])
        p.dma("sp", io["b1c_sb"][:], io["b1c"], writes=["c_b1c"])
        p.dma("sp", io["b2B_sb"][:], io["b2B"], writes=["c_b2B"])
        p.op("pool", lambda g: g.memset(scr["ones64"][:], 1.0), writes=["ones64"])
        p.op("pool", lambda g: g.memset(io["statsA"][:], 0.0), writes=["statsA"])
        for b in range(NBATCH):
            cfg_pre = {"scr": scr, "st": st, "lnsB": io["lnsB_sb"],
                       "lnbB": io["lnbB_sb"], "tr_psum": tr}
            emit_pre_ln_from_dram(
                p, cfg_pre, b,
                lambda b_, tc: io["x_in"][b_, tc * P:(tc + 1) * P, :],
                io["R0"], io["NB0"], "R0", ("sbuf", A1["zA"], "zA"),
                inner=(io["lnsB_sb"], io["lnbB_sb"]))
            emit_l0_attn(p, b, io)
            if debug and b == 0:
                p.dma("sp", io["dbg_ao"], io["aoTbuf"], reads=["d_ao"])
                p.dma("sp", io["dbg_t1"], io["tTbuf"], reads=["d_tT"])
                p.dma("sp", io["dbg_X1"][0], io["X1buf"][0], reads=["d_X1"])
            emit_l0_ffn(p, b, io)
            if debug and b == 0:
                p.dma("sp", io["dbg_p7"], io["p7"][:], reads=["p7"])
                p.dma("sp", io["dbg_W2bf"], io["W2bf"], reads=["d_W2bf"])
            emit_l1_branches(p, b, io)
        p.dma("sp", io["stats_out"],
              io["statsA"][:].rearrange("p a b c -> p (a b c)"),
              reads=["statsA"])
        p.finish()
    return nc, p


def build_launchB(debug=False):
    nc = bass.Bass("TRN2", target_bir_lowering=False)
    io = {}
    di = nc.dram_tensor
    io["X2in"] = di("X2in", [NBATCH, S, D], F32, kind="ExternalInput").ap()
    io["h_in"] = di("h_in", [3, NBATCH, 8, P, S], BF16, kind="ExternalInput").ap()
    io["allstats"] = di("allstats", [P, 48, 8], F32, kind="ExternalInput").ap()
    for nm, shp, dt in (("bn_gam", [P, 3, 8], F32), ("bn_bet", [P, 3, 8], F32),
                        ("WpT", [D, D], F32), ("bpB", [P, D], F32),
                        ("wgT", [D, 3072], I32), ("wuT", [D, 3072], I32),
                        ("wdT", [3072, D], I32), ("ident", [P, P], BF16),
                        ("constv", [P, len(CONSTS)], F32)):
        io[nm] = di(nm, shp, dt, kind="ExternalInput").ap()
    io["OUT"] = di("OUT", [NBATCH, S, D], F32, kind="ExternalOutput").ap()
    if debug:
        io["dbg_sc"] = di("dbg_sc", [P, 48], F32, kind="ExternalOutput").ap()
        io["dbg_X3"] = di("dbg_X3", [NBATCH, S, D], F32, kind="ExternalOutput").ap()
        io["dbg_oT"] = di("dbg_oT", [P, 8, S], BF16, kind="ExternalOutput").ap()
        io["dbg_t3"] = di("dbg_t3", [P, 8, S], BF16, kind="ExternalOutput").ap()
    io["X3buf"] = di("X3buf", [NBATCH, S, D], F32, kind="Internal").ap()
    io["tTbuf"] = di("tTbufB", [P, 8, S], BF16, kind="Internal").ap()
    io["outTbuf"] = di("outTbuf", [P, 8, S], BF16, kind="Internal").ap()
    io["wg_bf"] = di("wg_bf", [D, 3072], BF16, kind="Internal").ap()
    io["wu_bf"] = di("wu_bf", [D, 3072], BF16, kind="Internal").ap()
    io["wd_bf"] = di("wd_bf", [3072, D], BF16, kind="Internal").ap()

    p = Prog(nc)
    p.setup_consts()
    scr = _mk_scr_common(p)
    scr["stg2"] = p.sbuf("scr_stg2", (P, S), BF16)
    scr["sstg"] = p.sbuf("scr_sstg", (P, 512), BF16)
    scr["ustg"] = p.sbuf("scr_ustg", (P, 512), BF16)
    scr["vv_sb"] = p.sbuf("scr_vv", (P, 8, 64), BF16)
    scr["aostg"] = p.sbuf("scr_aostgB", (64, 512), BF16)
    scr["otile"] = [p.sbuf("scr_otile0", (P, 8, P), BF16),
                    p.sbuf("scr_otile1", (P, 8, P), BF16)]
    scr["SSin"] = p.sbuf("scr_ssin", (P, 48, 8), F32)
    scr["SS"] = p.sbuf("scr_ss", (P, 48), F32)
    for nm in ("mu24", "e224", "var24", "r24", "gr24", "gamB", "betB"):
        scr[nm] = p.sbuf("scr_" + nm, (P, 24), F32)
    io["scB"] = p.sbuf("scB", (P, 24), F32)
    io["biB"] = p.sbuf("biB", (P, 24), F32)
    st = TinyStats(p, "tinystB", 8)
    for nm in ("R3", "NB3"):
        io[nm] = p.sbuf(nm, (P, NB), F32)
    io["bpB_sb"] = p.sbuf("bpB_sb", (P, D), F32)

    AR = 131072
    arena = nc.alloc_sbuf_tensor("arenaB", [P, AR], mybir.dt.uint8)
    base = nc.lookup_mloc(arena).addr
    at = lambda nm, off, shp, dt: nc.alloc_sbuf_tensor_at(nm, list(shp), dt,
                                                          offset=base + off)
    A4 = {}
    A4["hq"] = at("hq_t", 0, (P, 8, S), BF16)
    A4["bq"] = at("bq_t", 32768, (P, 8, S), BF16)
    A4["outT"] = A4["bq"]
    A4["hv"] = at("hv_t", 65536, (P, 8, S), BF16)
    A4["WpT"] = at("WpT_t", 65536, (P, 8, D), BF16)
    A4["vn"] = at("vn_t", 98304, (P, NB, D), BF16)
    A5 = {}
    A5["wg"] = at("wg_t", 0, (P, 8, 3072), BF16)
    A5["t3T"] = at("t3T_t", 49152, (P, 8, S), BF16)
    A5["Hg"] = at("HgB_t", 81920, (P, 24, 384), BF16)
    A5["wut"] = [at(f"wut{i}_t", 100352 + 2048 * i, (P, 8, P), BF16)
                 for i in range(4)]
    A5["wdt"] = [at(f"wdt{i}_t", 108544 + 2048 * i, (P, D), BF16)
                 for i in range(4)]
    scr["castg"] = at("castg_t", 116736, (P, 3072), BF16)
    io["A4"], io["A5"] = A4, A5

    q0 = p.psum("q0", (P, 512)); q1 = p.psum("q1", (P, 512))
    q2 = p.psum("q2", (P, 512))
    y0 = p.psum("y0", (P, D)); y1 = p.psum("y1", (P, D))
    tr = p.psum("trp", (P, D), BF16)
    io.update(q0=q0, q1=q1, q2=q2, y0=y0, y1=y1, tr=tr)
    io["scr"], io["st"] = scr, st
    io["bcfg"] = {"scr": scr, "st": st, "out_q": "pool", "tr_psum": tr}

    with nc.Block():
        _load_setup(p, scr, io)
        p.dma("sp", io["bpB_sb"][:], io["bpB"], writes=["c_bpB"])
        emit_bn_finalize(p, io)
        if debug:
            p.dma("sp", io["dbg_sc"][:, 0:24], io["scB"][:], reads=["scB"])
            p.dma("sp", io["dbg_sc"][:, 24:48], io["biB"][:], reads=["biB"])
        for b in range(NBATCH):
            emit_l1_post(p, b, io)
            if debug and b == 0:
                p.dma("sp", io["dbg_oT"], io["outTbuf"], reads=["d_oT"])
                p.dma("sp", io["dbg_t3"], io["tTbuf"], reads=["d_tT"])
                p.dma("sp", io["dbg_X3"][0], io["X3buf"][0], reads=["d_X3"])
            emit_l1_ffn(p, b, io)
        if debug:
            p.dma("sp", io["dbg_X3"][1], io["X3buf"][1], reads=["d_X3"])
        p.finish()
    return nc, p


_CACHE = {}


def kernel(**inputs):
    g = prep_shared(inputs)
    x = np.asarray(inputs["L0_attn_x"], np.float32)
    if "A" not in _CACHE:
        _CACHE["A"] = build_launchA()
        _CACHE["B"] = build_launchB()
    ncA, _ = _CACHE["A"]
    ncB, _ = _CACHE["B"]
    xsh = x.reshape(8, NBATCH, S, D)
    keysA = ["wdqT", "wdkvT", "wupqT", "wupkT", "wupvT", "woutT", "lnsB",
             "lnbB", "ropeC", "ropeS", "WrT", "W1", "b1c", "W2", "b2B",
             "WqT", "WkT", "WvT", "ident", "constv"]
    in_mapsA = [dict({k: g[k] for k in keysA}, x=np.ascontiguousarray(xsh[c]))
                for c in range(8)]
    resA = run_bass_kernel_spmd(ncA, in_mapsA, core_ids=list(range(8)))
    allstats = np.ascontiguousarray(
        np.stack([resA.results[c]["stats_out"] for c in range(8)], axis=-1))
    keysB = ["bn_gam", "bn_bet", "WpT", "bpB", "wgT", "wuT", "wdT", "ident",
             "constv"]
    in_mapsB = [dict({k: g[k] for k in keysB},
                     X2in=resA.results[c]["X2out"],
                     h_in=resA.results[c]["h_out"],
                     allstats=allstats)
                for c in range(8)]
    resB = run_bass_kernel_spmd(ncB, in_mapsB, core_ids=list(range(8)))
    out = np.concatenate([resB.results[c]["OUT"] for c in range(8)], axis=0)
    return np.ascontiguousarray(out.reshape(16, S, D).astype(np.float32))


# version 05
# speedup vs baseline: 1.1491x; 1.0535x over previous
import numpy as np
import ml_dtypes

import concourse.bass as bass
import concourse.mybir as mybir
from concourse.bass_utils import run_bass_kernel_spmd

F32 = mybir.dt.float32
BF16 = mybir.dt.bfloat16
I32 = mybir.dt.int32
AF = mybir.ActivationFunctionType
ALU = mybir.AluOpType
AX = mybir.AxisListType

P = 128
NBATCH = 2
S = 2048
D = 1024
NB = S // P
EPS = 1e-5
S127 = 1.0 / 127.0
S2_FFN = S127 * S127

CONSTS = [EPS, -0.5, 0.125, 1.0 / D, 1.0, -1.0, S127, S2_FFN,
          1.0 / 32768.0, 0.0, 4.0, -4.0, 0.5,
          S2_FFN / D, (S2_FFN * S2_FFN) / D]

ENGS = ("tensor", "vector", "scalar", "pool", "sp")
DMA_QUEUES = {"sp": "sp", "pool": "pool"}
DMA_RING = 12


class Prog:

    def __init__(self, nc):
        self.nc = nc
        from contextlib import ExitStack
        self.ctx = ExitStack()
        self.eng = {"tensor": nc.tensor, "vector": nc.vector,
                    "scalar": nc.scalar, "pool": nc.gpsimd, "sp": nc.sync}
        self.esem = {e: self.ctx.enter_context(nc.semaphore(f"es_{e}")) for e in ENGS}
        self.ecnt = {e: 0 for e in ENGS}
        self.qsem = {q: [self.ctx.enter_context(nc.semaphore(f"q{q}{i}"))
                         for i in range(DMA_RING)] for q in DMA_QUEUES}
        self.qcnt = {q: 0 for q in DMA_QUEUES}
        self.quses = {q: [0] * DMA_RING for q in DMA_QUEUES}
        self.waited = {}
        self.pending = {}
        self.res = {}
        self.n_waits = 0
        self.consts_tile = None

    def _stream_of(self, tok):
        return ("eng", tok[1]) if tok[0] == "eng" else ("dmaq", tok[1])

    def _wait(self, stream, tok):
        if tok[0] == "eng":
            sem, val = self.esem[tok[1]], tok[2]
        else:
            _, q, slot, uses = tok
            sem, val = self.qsem[q][slot], 16 * uses
        key = (stream, sem.name)
        if self.waited.get(key, 0) >= val:
            return
        self.waited[key] = val
        pend = self.pending.setdefault(stream, {})
        if pend.get(sem, 0) < val:
            pend[sem] = val

    def _flush_pre(self, stream):
        pend = self.pending.pop(stream, None)
        if not pend:
            return None
        eng = stream[1] if stream[0] == "eng" else DMA_QUEUES[stream[1]]
        items = list(pend.items())
        for sem, val in items[:-1]:
            self.eng[eng].wait_ge(sem, val)
            self.n_waits += 1
        return items[-1]

    def _pre(self, stream, reads, writes):
        is_dma = stream[0] == "dmaq"
        for r in reads:
            e = self.res.get(r)
            if e and e["w"] is not None:
                w = e["w"]
                same = self._stream_of(w) == stream
                if (not same) or is_dma:
                    self._wait(stream, w)
                elif stream != ("eng", "tensor") and w[0] == "eng":
                    self._wait(stream, w)
        for wname in writes:
            e = self.res.get(wname)
            if not e:
                continue
            if e["w"] is not None and (self._stream_of(e["w"]) != stream or is_dma):
                self._wait(stream, e["w"])
            for rt in e["r"]:
                if self._stream_of(rt) != stream or is_dma:
                    self._wait(stream, rt)

    def _post(self, tok, reads, writes):
        for r in reads:
            self.res.setdefault(r, {"w": None, "r": []})["r"].append(tok)
        for w in writes:
            e = self.res.setdefault(w, {"w": None, "r": []})
            e["w"], e["r"] = tok, []

    def op(self, eng_name, fn, reads=(), writes=()):
        stream = ("eng", eng_name)
        self._pre(stream, reads, writes)
        last = self._flush_pre(stream)
        inst = fn(self.eng[eng_name])
        if last is not None:
            inst._wait_ge(last[0], last[1])
        self.ecnt[eng_name] += 1
        tok = ("eng", eng_name, self.ecnt[eng_name])
        inst.then_inc(self.esem[eng_name], 1)
        self._post(tok, reads, writes)
        return tok

    def dma(self, q, out, in_, reads=(), writes=(), **kw):
        stream = ("dmaq", q)
        slot = self.qcnt[q] % DMA_RING
        prev = self.quses[q][slot]
        if prev > 0:
            self._wait(stream, ("dma", q, slot, prev))
        self._pre(stream, reads, writes)
        last = self._flush_pre(stream)
        inst = self.eng[DMA_QUEUES[q]].dma_start(out=out, in_=in_, **kw)
        if last is not None:
            inst._wait_ge(last[0], last[1])
        self.qcnt[q] += 1
        self.quses[q][slot] += 1
        tok = ("dma", q, slot, self.quses[q][slot])
        inst.then_inc(self.qsem[q][slot], 16)
        self._post(tok, reads, writes)
        return tok

    def finish(self):
        stream = ("eng", "pool")
        for e in ENGS:
            if self.ecnt[e]:
                self._wait(stream, ("eng", e, self.ecnt[e]))
        for q in DMA_QUEUES:
            for s in range(DMA_RING):
                if self.quses[q][s]:
                    self._wait(stream, ("dma", q, s, self.quses[q][s]))
        pend = self.pending.pop(stream, None)
        if pend:
            for sem, val in pend.items():
                self.eng["pool"].wait_ge(sem, val)
        self.eng["pool"].nop()

    def sbuf(self, name, shape, dtype):
        return self.ctx.enter_context(self.nc.sbuf_tensor(name, list(shape), dtype))

    def psum(self, name, shape, dtype=F32):
        return self.ctx.enter_context(self.nc.psum_tensor(name, list(shape), dtype))

    def setup_consts(self):
        cst = self.sbuf("consts", (P, len(CONSTS)), F32)
        for j, v in enumerate(CONSTS):
            self.nc.const_aps.register(F32, float(v), cst[:, j:j + 1])
        self.consts_tile = cst
        return cst

    def cst(self, v):
        return self.consts_tile[:, CONSTS.index(v):CONSTS.index(v) + 1]

    def cstn(self, v, n):
        j = CONSTS.index(v)
        return self.consts_tile[0:n, j:j + 1]

    def cstb(self, v, base, n=64):
        j = CONSTS.index(v)
        return self.consts_tile[base:base + n, j:j + 1]


class TView:

    def __init__(self, t, off):
        self.t, self.off = t, off

    def __getitem__(self, idx):
        if idx == slice(None):
            return self.t[:, self.off:self.off + 1024]
        p_, f_ = idx
        if f_ == slice(None):
            f_ = slice(0, 1024)
        return self.t[p_, self.off + f_.start:self.off + f_.stop]


def bcastF(col_ap, n):
    return bass.AP(col_ap.tensor, col_ap.offset, [list(col_ap.ap[0]), [0, n]])



def _rope_tables():
    dim, base = 64, 10000.0
    inv = 1.0 / base ** (np.arange(0, dim, 2, dtype=np.float32) / dim)
    t = np.arange(S, dtype=np.float32)
    fr = np.outer(t, inv)
    cos = np.concatenate([np.cos(fr), np.cos(fr)], -1)[:, ::2]
    sin = np.concatenate([np.sin(fr), np.sin(fr)], -1)[:, ::2]
    return cos.astype(np.float32), sin.astype(np.float32)


def _deint_perm():
    p = []
    for h in range(16):
        p.extend(64 * h + np.arange(0, 64, 2))
        p.extend(64 * h + np.arange(1, 64, 2))
    return np.array(p)


def prep_shared(inputs):
    f32 = np.float32
    C = np.ascontiguousarray
    g = {}
    g["wdqT"] = C(inputs["L0_attn_w_down_q"].T)
    g["wdkvT"] = C(inputs["L0_attn_w_down_kv"].T)
    perm = _deint_perm()
    g["wupqT"] = C(inputs["L0_attn_w_up_q"][perm].T)
    g["wupkT"] = C(inputs["L0_attn_w_up_k"][perm].T)
    g["wupvT"] = C(inputs["L0_attn_w_up_v"].T)
    g["woutT"] = C(inputs["L0_attn_w_out"].T)
    g["lnsB"] = np.tile(np.asarray(inputs["L0_attn_ln_scale"], f32)[None, :], (P, 1))
    g["lnbB"] = np.tile(np.asarray(inputs["L0_attn_ln_bias"], f32)[None, :], (P, 1))
    cos, sin = _rope_tables()
    g["ropeC"] = C(cos.reshape(NB, P, 32).transpose(1, 0, 2))
    g["ropeS"] = C(sin.reshape(NB, P, 32).transpose(1, 0, 2))
    g["WrT"] = C(inputs["L0_ffn_Wr"].T)
    g["W1"] = C(np.asarray(inputs["L0_ffn_W1"])[7])
    g["b1c"] = C(np.asarray(inputs["L0_ffn_b1"], f32)[7].reshape(32, P).T)
    g["W2"] = C(np.asarray(inputs["L0_ffn_W2"])[7])
    g["b2B"] = np.tile(np.asarray(inputs["L0_ffn_b2"], f32)[7][None, :], (P, 1))
    g["WqT"] = C(inputs["L1_attn_Wq"].T)
    g["WkT"] = C(inputs["L1_attn_Wk"].T)
    g["WvT"] = C(inputs["L1_attn_Wv"].T)
    g["WpT"] = C(inputs["L1_attn_Wp"].T)
    g["bpB"] = np.tile(np.asarray(inputs["L1_attn_bp"], f32)[None, :], (P, 1))
    gam = np.stack([np.asarray(inputs["L1_attn_g" + b], f32) for b in "qkv"])
    bet = np.stack([np.asarray(inputs["L1_attn_b" + b], f32) for b in "qkv"])
    g["bn_gam"] = C(gam.reshape(3, 8, P).transpose(2, 0, 1))
    g["bn_bet"] = C(bet.reshape(3, 8, P).transpose(2, 0, 1))
    g["wgT"] = C(np.asarray(inputs["L1_ffn_w_gate"]).T)
    g["wuT"] = C(np.asarray(inputs["L1_ffn_w_up"]).T)
    g["wdT"] = C(np.asarray(inputs["L1_ffn_w_down"]).T)
    g["ident"] = np.eye(P, dtype=ml_dtypes.bfloat16)
    g["constv"] = np.tile(np.array(CONSTS, f32), (P, 1))
    return g



class TinyStats:

    def __init__(self, p, name, ncols=8):
        self.t = p.sbuf(name, (P, ncols), F32)
        self.name = name

    def col(self, j):
        return self.t[:, j:j + 1]


def emit_boundary_chunk(p, cfg, b, tc, y_psum, y_psum_res):
    scr = cfg["scr"]
    u = scr["u"]; xold = scr["xold"][tc % 2]; xnew = scr["xnew"]
    sqj = scr["sqj"]; st = cfg["st"]
    uref = "bscr_u"
    xo_res = f"xold{tc % 2}"
    parts = y_psum if isinstance(y_psum, list) else [(y_psum, y_psum_res, 0, D)]
    if cfg.get("_u_ready"):
        pass
    elif cfg.get("trec") is not None:
        r1, nb1 = cfg["trec"]
        tr = scr["trec"]
        p.op("scalar", lambda s: s.activation(
            tr[:], xold[:], AF.Identity,
            scale=r1[:, tc:tc + 1], bias=nb1[:, tc:tc + 1]),
            reads=[xo_res, cfg["trec_res"]], writes=["bscr_trec"])
        for ap, res, c0, c1 in parts:
            p.op("vector", lambda v, ap=ap, c0=c0, c1=c1: v.tensor_add(
                u[:, c0:c1], ap, tr[:, c0:c1]),
                reads=[res, "bscr_trec"], writes=[uref])
    elif cfg.get("pre_add") is not None:
        pap, pres = cfg["pre_add"]
        for ap, res, c0, c1 in parts:
            p.op("vector", lambda v, ap=ap, c0=c0, c1=c1: v.tensor_add(
                u[:, c0:c1], ap, pap[:, c0:c1]),
                reads=[res, pres], writes=[uref])
    else:
        for ap, res, c0, c1 in parts:
            p.op("vector", lambda v, ap=ap, c0=c0, c1=c1: v.tensor_copy(
                u[:, c0:c1], ap),
                reads=[res], writes=[uref])
    su, sq = st.col(0), st.col(1)
    p.op("vector", lambda v: v.reduce_sum(su, u[:], axis=AX.X),
         reads=[uref], writes=["st_su"])
    p.op("scalar", lambda s: s.activation(sqj[:], u[:], AF.Square, accum_out=sq),
         reads=[uref], writes=["st_sq", "bscr_sqj"])
    alpha = cfg.get("alpha")
    m_y, e2_y, var, r, scaleA, nb_ = (st.col(2), st.col(3), st.col(4),
                                      st.col(5), st.col(6), st.col(7))
    cD = p.cst(1.0 / D)
    if alpha is None:
        p.op("vector", lambda v: v.tensor_scalar(m_y, su, cD, None, ALU.mult),
             reads=["st_su"], writes=["st_m"])
        p.op("vector", lambda v: v.tensor_scalar(e2_y, sq, cD, None, ALU.mult),
             reads=["st_sq"], writes=["st_e2"])
    elif isinstance(alpha, tuple) and alpha[0] == "col":
        acol, ares = alpha[1], alpha[2]
        p.op("vector", lambda v: v.tensor_scalar(m_y, su, cD, None, ALU.mult),
             reads=["st_su"], writes=["st_m"])
        p.op("vector", lambda v: v.tensor_mul(m_y, m_y, acol),
             reads=["st_m", ares], writes=["st_m"])
        p.op("vector", lambda v: v.tensor_scalar(e2_y, sq, cD, None, ALU.mult),
             reads=["st_sq"], writes=["st_e2"])
        p.op("vector", lambda v: v.tensor_mul(e2_y, e2_y, acol),
             reads=["st_e2", ares], writes=["st_e2"])
        p.op("vector", lambda v: v.tensor_mul(e2_y, e2_y, acol),
             reads=["st_e2", ares], writes=["st_e2"])
    else:
        _, ca, ca_m, ca_e = alpha
        p.op("vector", lambda v: v.tensor_scalar(m_y, su, p.cst(ca_m), None, ALU.mult),
             reads=["st_su"], writes=["st_m"])
        p.op("vector", lambda v: v.tensor_scalar(e2_y, sq, p.cst(ca_e), None, ALU.mult),
             reads=["st_sq"], writes=["st_e2"])
    p.op("vector", lambda v: v.tensor_mul(var, m_y, m_y),
         reads=["st_m"], writes=["st_var"])
    p.op("vector", lambda v: v.tensor_sub(var, e2_y, var),
         reads=["st_e2", "st_var"], writes=["st_var"])
    p.op("scalar", lambda s: s.activation(r, var, AF.Ln, bias=p.cst(EPS)),
         reads=["st_var"], writes=["st_r"])
    p.op("scalar", lambda s: s.activation(r, r, AF.Exp, scale=p.cst(-0.5)),
         reads=["st_r"], writes=["st_r"])
    if alpha is None:
        p.op("vector", lambda v: v.tensor_copy(scaleA, r),
             reads=["st_r"], writes=["st_sa"])
    elif isinstance(alpha, tuple) and alpha[0] == "col":
        p.op("vector", lambda v: v.tensor_mul(scaleA, r, alpha[1]),
             reads=["st_r", alpha[2]], writes=["st_sa"])
    else:
        p.op("vector", lambda v: v.tensor_scalar(scaleA, r, p.cst(alpha[1]), None,
                                                 ALU.mult),
             reads=["st_r"], writes=["st_sa"])
    p.op("vector", lambda v: v.tensor_mul(nb_, m_y, r),
         reads=["st_m", "st_r"], writes=["st_nb"])
    p.op("vector", lambda v: v.tensor_scalar(nb_, nb_, p.cst(-1.0), None, ALU.mult),
         reads=["st_nb"], writes=["st_nb"])
    p.op("vector", lambda v: v.tensor_scalar(xnew[:], u[:], scaleA, nb_,
                                             ALU.mult, ALU.add),
         reads=[uref, "st_sa", "st_nb"], writes=["bscr_xnew"])
    p.op("vector", lambda v: v.tensor_add(xnew[:], xnew[:], xold[:]),
         reads=["bscr_xnew", xo_res], writes=["bscr_xnew"])
    p.dma(cfg.get("out_q", "pool"), cfg["xnew_dram"](b, tc), xnew[:],
          reads=["bscr_xnew"], writes=[cfg.get("xnew_res", "d_xnew")])
    nxt = cfg.get("next")
    if nxt is None:
        return
    Rarr, NBarr, Rres, tT_store = nxt
    su2, sq2 = st.col(0), st.col(1)
    p.op("vector", lambda v: v.reduce_sum(su2, xnew[:], axis=AX.X),
         reads=["bscr_xnew"], writes=["st_su"])
    p.op("scalar", lambda s: s.activation(sqj[:], xnew[:], AF.Square, accum_out=sq2),
         reads=["bscr_xnew"], writes=["st_sq", "bscr_sqj"])
    m2, e22, v2, r2 = st.col(2), st.col(3), st.col(4), st.col(5)
    p.op("vector", lambda v: v.tensor_scalar(m2, su2, cD, None, ALU.mult),
         reads=["st_su"], writes=["st_m"])
    p.op("vector", lambda v: v.tensor_scalar(e22, sq2, cD, None, ALU.mult),
         reads=["st_sq"], writes=["st_e2"])
    p.op("vector", lambda v: v.tensor_mul(v2, m2, m2), reads=["st_m"], writes=["st_var"])
    p.op("vector", lambda v: v.tensor_sub(v2, e22, v2),
         reads=["st_e2", "st_var"], writes=["st_var"])
    p.op("scalar", lambda s: s.activation(r2, v2, AF.Ln, bias=p.cst(EPS)),
         reads=["st_var"], writes=["st_r"])
    p.op("scalar", lambda s: s.activation(r2, r2, AF.Exp, scale=p.cst(-0.5)),
         reads=["st_r"], writes=["st_r"])
    p.op("vector", lambda v: v.tensor_copy(Rarr[:, tc:tc + 1], r2),
         reads=["st_r"], writes=[Rres])
    p.op("vector", lambda v: v.tensor_mul(NBarr[:, tc:tc + 1], m2, r2),
         reads=["st_m", "st_r"], writes=[Rres])
    p.op("vector", lambda v: v.tensor_scalar(NBarr[:, tc:tc + 1],
                                             NBarr[:, tc:tc + 1],
                                             p.cst(-1.0), None, ALU.mult),
         reads=[Rres], writes=[Rres])
    tn = scr["tn"]
    p.op("scalar", lambda s: s.activation(tn[:], xnew[:], AF.Identity,
                                          scale=Rarr[:, tc:tc + 1],
                                          bias=NBarr[:, tc:tc + 1]),
         reads=["bscr_xnew", Rres], writes=["bscr_tn"])
    tpp = cfg["tr_psum"]
    for dc in range(8):
        p.op("tensor", lambda t, dc=dc: t.transpose(
            tpp[:, dc * P:(dc + 1) * P], tn[:, dc * P:(dc + 1) * P],
            scr["ident"][:]),
            reads=["bscr_tn", "c_ident"], writes=["ptr"])
    kind = tT_store[0]
    if kind == "sbuf":
        tT = tT_store[1]
        p.op("vector", lambda v: v.tensor_copy(
            tT[:, :, tc * P:(tc + 1) * P],
            tpp[:].rearrange("p (c n) -> p c n", c=8)),
            reads=["ptr"], writes=[tT_store[2]])
    else:
        stg = scr["tT_stage"]
        p.op("vector", lambda v: v.tensor_copy(stg[:], tpp[:]),
             reads=["ptr"], writes=["bscr_tstage"])
        p.dma("pool", tT_store[1](b, tc), stg[:].rearrange("p (c n) -> p c n", c=8),
              reads=["bscr_tstage"], writes=["d_tT"])


def emit_pre_ln_from_dram(p, cfg, b, x_dram_fn, Rarr, NBarr, Rres, tT_store,
                          inner=None):
    scr = cfg["scr"]; st = cfg["st"]
    cD = p.cst(1.0 / D)
    for tc in range(NB):
        xold = scr["xold"][tc % 2]
        xo_res = f"xold{tc % 2}"
        p.dma("sp", xold[:], x_dram_fn(b, tc), writes=[xo_res])
        su, sq = st.col(0), st.col(1)
        p.op("vector", lambda v: v.reduce_sum(su, xold[:], axis=AX.X),
             reads=[xo_res], writes=["st_su"])
        p.op("scalar", lambda s: s.activation(scr["sqj"][:], xold[:], AF.Square,
                                              accum_out=sq),
             reads=[xo_res], writes=["st_sq", "bscr_sqj"])
        m2, e22, v2, r2 = st.col(2), st.col(3), st.col(4), st.col(5)
        p.op("vector", lambda v: v.tensor_scalar(m2, su, cD, None, ALU.mult),
             reads=["st_su"], writes=["st_m"])
        p.op("vector", lambda v: v.tensor_scalar(e22, sq, cD, None, ALU.mult),
             reads=["st_sq"], writes=["st_e2"])
        p.op("vector", lambda v: v.tensor_mul(v2, m2, m2),
             reads=["st_m"], writes=["st_var"])
        p.op("vector", lambda v: v.tensor_sub(v2, e22, v2),
             reads=["st_e2", "st_var"], writes=["st_var"])
        p.op("scalar", lambda s: s.activation(r2, v2, AF.Ln, bias=p.cst(EPS)),
             reads=["st_var"], writes=["st_r"])
        p.op("scalar", lambda s: s.activation(r2, r2, AF.Exp, scale=p.cst(-0.5)),
             reads=["st_r"], writes=["st_r"])
        p.op("vector", lambda v: v.tensor_copy(Rarr[:, tc:tc + 1], r2),
             reads=["st_r"], writes=[Rres])
        p.op("vector", lambda v: v.tensor_mul(NBarr[:, tc:tc + 1], m2, r2),
             reads=["st_m", "st_r"], writes=[Rres])
        p.op("vector", lambda v: v.tensor_scalar(NBarr[:, tc:tc + 1],
                                                 NBarr[:, tc:tc + 1],
                                                 p.cst(-1.0), None, ALU.mult),
             reads=[Rres], writes=[Rres])
        if inner is None:
            tn = scr["tn"]
            p.op("scalar", lambda s: s.activation(tn[:], xold[:], AF.Identity,
                                                  scale=Rarr[:, tc:tc + 1],
                                                  bias=NBarr[:, tc:tc + 1]),
                 reads=[xo_res, Rres], writes=["bscr_tn"])
            src_res = "bscr_tn"
            src = tn
        else:
            lnsB, lnbB = inner
            tf = scr["trec"]
            p.op("scalar", lambda s: s.activation(tf[:], xold[:], AF.Identity,
                                                  scale=Rarr[:, tc:tc + 1],
                                                  bias=NBarr[:, tc:tc + 1]),
                 reads=[xo_res, Rres], writes=["bscr_trec"])
            su3, sq3 = st.col(0), st.col(1)
            p.op("vector", lambda v: v.reduce_sum(su3, tf[:], axis=AX.X),
                 reads=["bscr_trec"], writes=["st_su"])
            p.op("scalar", lambda s: s.activation(scr["sqj"][:], tf[:], AF.Square,
                                                  accum_out=sq3),
                 reads=["bscr_trec"], writes=["st_sq", "bscr_sqj"])
            m3, e3, v3, r3, nb3 = (st.col(2), st.col(3), st.col(4), st.col(5),
                                   st.col(6))
            p.op("vector", lambda v: v.tensor_scalar(m3, su3, cD, None, ALU.mult),
                 reads=["st_su"], writes=["st_m"])
            p.op("vector", lambda v: v.tensor_scalar(e3, sq3, cD, None, ALU.mult),
                 reads=["st_sq"], writes=["st_e2"])
            p.op("vector", lambda v: v.tensor_mul(v3, m3, m3),
                 reads=["st_m"], writes=["st_var"])
            p.op("vector", lambda v: v.tensor_sub(v3, e3, v3),
                 reads=["st_e2", "st_var"], writes=["st_var"])
            p.op("scalar", lambda s: s.activation(r3, v3, AF.Ln, bias=p.cst(EPS)),
                 reads=["st_var"], writes=["st_r"])
            p.op("scalar", lambda s: s.activation(r3, r3, AF.Exp, scale=p.cst(-0.5)),
                 reads=["st_r"], writes=["st_r"])
            p.op("vector", lambda v: v.tensor_mul(nb3, m3, r3),
                 reads=["st_m", "st_r"], writes=["st_nb"])
            p.op("vector", lambda v: v.tensor_scalar(nb3, nb3, p.cst(-1.0), None,
                                                     ALU.mult),
                 reads=["st_nb"], writes=["st_nb"])
            un = scr["u"]
            p.op("scalar", lambda s: s.activation(un[:], tf[:], AF.Identity,
                                                  scale=r3, bias=nb3),
                 reads=["bscr_trec", "st_r", "st_nb"], writes=["bscr_u"])
            tn = scr["tn"]
            p.op("vector", lambda v: v.tensor_mul(un[:], un[:], cfg["lnsB"][:]),
                 reads=["bscr_u", "c_lnsB"], writes=["bscr_u"])
            p.op("vector", lambda v: v.tensor_add(tn[:], un[:], cfg["lnbB"][:]),
                 reads=["bscr_u", "c_lnbB"], writes=["bscr_tn"])
            src_res = "bscr_tn"
            src = tn
        tpp = cfg["tr_psum"]
        for dc in range(8):
            p.op("tensor", lambda t, dc=dc: t.transpose(
                tpp[:, dc * P:(dc + 1) * P], src[:, dc * P:(dc + 1) * P],
                scr["ident"][:]),
                reads=[src_res, "c_ident"], writes=["ptr"])
        kind = tT_store[0]
        if kind == "sbuf":
            p.op("vector", lambda v, tc=tc: v.tensor_copy(
                tT_store[1][:, :, tc * P:(tc + 1) * P],
                tpp[:].rearrange("p (c n) -> p c n", c=8)),
                reads=["ptr"], writes=[tT_store[2]])
        else:
            stg = scr["tT_stage"]
            p.op("vector", lambda v: v.tensor_copy(stg[:], tpp[:]),
                 reads=["ptr"], writes=["bscr_tstage"])
            p.dma("pool", tT_store[1](b, tc),
                  stg[:].rearrange("p (c n) -> p c n", c=8),
                  reads=["bscr_tstage"], writes=["d_tT"])



def _evac_copy(p, eng, dst, src, reads, writes):
    if eng == "scalar":
        p.op("scalar", lambda s: s.copy(dst, src), reads=reads, writes=writes)
    else:
        p.op("vector", lambda v: v.tensor_copy(dst, src), reads=reads, writes=writes)


def _takeover(p, news, olds):
    toks = []
    for o in olds:
        e = p.res.pop(o, None)
        if e:
            if e["w"] is not None:
                toks.append(e["w"])
            toks.extend(e["r"])
    if isinstance(news, str):
        news = [news]
    for new in news:
        ent = p.res.setdefault(new, {"w": None, "r": []})
        ent["r"].extend(list(toks))


def emit_bn_finalize(p, io):
    scr, st = io["scr"], io["st"]
    SSin = scr["SSin"]
    SS = scr["SS"]
    p.dma("sp", SSin[:], io["allstats"], writes=["SSin"])
    p.op("vector", lambda v: v.tensor_reduce(SS[:], SSin[:], AX.X, ALU.add),
         reads=["SSin"], writes=["SS"])
    mu, e2, var, r = scr["mu24"], scr["e224"], scr["var24"], scr["r24"]
    su_v = bass.AP(SS[:].tensor, SS[:].offset, [list(SS[:].ap[0]), [2, 24]])
    sq_v = bass.AP(SS[:].tensor, SS[:].offset + 1, [list(SS[:].ap[0]), [2, 24]])
    c32k = p.cst(1.0 / 32768.0)
    p.op("vector", lambda v: v.tensor_scalar(mu[:], su_v, c32k, None, ALU.mult),
         reads=["SS"], writes=["mu24"])
    p.op("vector", lambda v: v.tensor_scalar(e2[:], sq_v, c32k, None, ALU.mult),
         reads=["SS"], writes=["e224"])
    p.op("vector", lambda v: v.tensor_mul(var[:], mu[:], mu[:]),
         reads=["mu24"], writes=["var24"])
    p.op("vector", lambda v: v.tensor_sub(var[:], e2[:], var[:]),
         reads=["e224", "var24"], writes=["var24"])
    p.op("scalar", lambda s: s.activation(r[:], var[:], AF.Ln, bias=p.cst(EPS)),
         reads=["var24"], writes=["r24"])
    p.op("scalar", lambda s: s.activation(r[:], r[:], AF.Exp, scale=p.cst(-0.5)),
         reads=["r24"], writes=["r24"])
    gamB, betB = scr["gamB"], scr["betB"]
    scB, biB = io["scB"], io["biB"]
    p.dma("sp", gamB[:], io["bn_gam"].rearrange("p a b -> p (a b)"), writes=["gamB"])
    p.dma("sp", betB[:], io["bn_bet"].rearrange("p a b -> p (a b)"), writes=["betB"])
    gr = scr["gr24"]
    p.op("vector", lambda v: v.tensor_mul(gr[:], gamB[:], r[:]),
         reads=["gamB", "r24"], writes=["gr24"])
    p.op("vector", lambda v: v.tensor_scalar(scB[:], gr[:], p.cst(4.0), None,
                                             ALU.mult),
         reads=["gr24"], writes=["scB"])
    p.op("vector", lambda v: v.tensor_mul(var[:], mu[:], gr[:]),
         reads=["mu24", "gr24"], writes=["var24"])
    p.op("vector", lambda v: v.tensor_sub(var[:], betB[:], var[:]),
         reads=["betB", "var24"], writes=["var24"])
    p.op("vector", lambda v: v.tensor_scalar(biB[:], var[:], p.cst(4.0),
                                             p.cst(-4.0), ALU.mult, ALU.add),
         reads=["var24"], writes=["biB"])


def emit_l1_post(p, b, io):
    A = io["A4"]
    hq, bq, hv, vn = A["hq"], A["bq"], A["hv"], A["vn"]
    outT, WpT = A["outT"], A["WpT"]
    q0, q1, q2, y0, y1 = io["q0"], io["q1"], io["q2"], io["y0"], io["y1"]
    scr, st = io["scr"], io["st"]
    scB, biB = io["scB"], io["biB"]
    stg2 = scr["stg2"]
    _takeover(p, ["zq", "zbq", "zv", "zvn"],
              ["zwg", "zt3", "zHg", "wut0", "wut1", "wut2", "wut3",
               "wdt0", "wdt1", "wdt2", "wdt3", "castg0", "castg1", "zWpT"])
    p.dma("sp", hq[:], io["h_in"][0, b].rearrange("c p n -> p c n"), writes=["zq"])
    p.dma("sp", hv[:], io["h_in"][2, b].rearrange("c p n -> p c n"), writes=["zv"])
    for ec in range(8):
        p.op("scalar", lambda s, ec=ec: s.activation(
            bq[:, ec, :], hq[:, ec, :], AF.Sigmoid,
            scale=scB[:, ec:ec + 1], bias=biB[:, ec:ec + 1]),
            reads=["zq", "scB", "biB"], writes=["zbq"])
    for ec in range(8):
        p.dma("sp", stg2[:], io["h_in"][1, b, ec], writes=["stg2"])
        p.op("scalar", lambda s, ec=ec: s.activation(
            stg2[:], stg2[:], AF.Sigmoid,
            scale=scB[:, 8 + ec:9 + ec], bias=biB[:, 8 + ec:9 + ec]),
            reads=["stg2", "scB", "biB"], writes=["stg2"])
        p.op("vector", lambda v, ec=ec: v.tensor_mul(
            hq[:, ec, :], stg2[:], bq[:, ec, :]),
            reads=["stg2", "zbq", "zq"], writes=["zq"])
    qk = hq
    for ec in range(8):
        p.op("scalar", lambda s, ec=ec: s.activation(
            stg2[:], hv[:, ec, :], AF.Sigmoid,
            scale=scB[:, 16 + ec:17 + ec], bias=biB[:, 16 + ec:17 + ec]),
            reads=["zv", "scB", "biB"], writes=["stg2"])
        for tc in range(NB):
            p.op("tensor", lambda t, tc=tc: t.transpose(
                io["tr"][:, 0:P], stg2[:, tc * P:(tc + 1) * P], scr["ident"][:]),
                reads=["stg2", "c_ident"], writes=["ptr"])
            p.op("vector", lambda v, tc=tc, ec=ec: v.tensor_copy(
                vn[:, tc, ec * P:(ec + 1) * P], io["tr"][:, 0:P]),
                reads=["ptr"], writes=["zvn"])
    vv_sb = scr["vv_sb"]
    for h in range(16):
        base = 64 * (h % 2)
        for tc in range(NB):
            p.op("tensor", lambda t, h=h, tc=tc, base=base: t.matmul(
                q2[base:base + 64, 0:64], vn[:, tc, h * 64:(h + 1) * 64],
                vn[:, tc, h * 64:(h + 1) * 64],
                start=(tc == 0), stop=(tc == NB - 1),
                tile_position=(0, base)),
                reads=["zvn"], writes=["pq2"])
        p.op("scalar", lambda s, h=h, base=base: s.activation(
            vv_sb[base:base + 64, h // 2, :], q2[base:base + 64, 0:64],
            AF.Identity, scale=p.cstb(0.125, base)),
            reads=["pq2"], writes=["vv_sb"])
    ostg = scr["aostg"]
    for h in range(16):
        base = 64 * (h % 2)
        for qt in range(4):
            p.op("tensor", lambda t, h=h, qt=qt, base=base: t.matmul(
                q0[0:64, :], vv_sb[base:base + 64, h // 2, :],
                qk[base:base + 64, h // 2, qt * 512:(qt + 1) * 512],
                start=True, stop=True),
                reads=["vv_sb", "zq"], writes=["pq0"])
            p.op("vector", lambda v: v.tensor_copy(ostg[:], q0[0:64, :]),
                 reads=["pq0"], writes=["aostg"])
            p.dma("pool", io["outTbuf"][base:base + 64, h // 2,
                                      qt * 512:(qt + 1) * 512],
                  ostg[:], reads=["aostg"], writes=["d_oT"])
    _takeover(p, ["zWpT"], ["zv"])
    p.dma("pool", WpT[:], io["WpT"].rearrange("(c p) m -> p c m", p=P),
          writes=["zWpT"])
    for tcg in range(0, NB, 2):
        items = []
        for tc in (tcg, tcg + 1):
            otile = scr["otile"][tc % 2]
            p.dma("sp", otile[:], io["outTbuf"][:, :, tc * P:(tc + 1) * P],
                  reads=["d_oT"], writes=[f"otile{tc % 2}"])
            p.dma("sp", scr["xold"][tc % 4][:],
                  io["X2in"][b, tc * P:(tc + 1) * P, :],
                  writes=[f"xold{tc % 4}"])
            yb, ybres = (y0, "py0") if tc % 2 == 0 else (y1, "py1")
            for k in range(8):
                for h2 in range(2):
                    p.op("tensor", lambda t, k=k, tc=tc, h2=h2, yb=yb,
                         otile=otile: t.matmul(
                        yb[:, h2 * 512:(h2 + 1) * 512],
                        otile[:, k, :],
                        WpT[:, k, h2 * 512:(h2 + 1) * 512],
                        start=(k == 0), stop=(k == 7)),
                        reads=[f"otile{tc % 2}", "zWpT"], writes=[ybres])
            items.append((tc, [(yb[:], ybres, 0, D)]))
        cfg = dict(io["bcfg"])
        cfg["pre_add"] = (io["bpB_sb"][:], "c_bpB")
        cfg["xnew_dram"] = lambda b_, tc_: io["X3buf"][b_, tc_ * P:(tc_ + 1) * P, :]
        cfg["xnew_res"] = "d_X3"
        cfg["next"] = (io["R3"], io["NB3"], "R3",
                       ("dram", lambda b_, tc_: io["tTbuf"][:, :, tc_ * P:(tc_ + 1) * P]))
        emit_boundary_group(p, cfg, b, items)


def emit_l1_ffn(p, b, io):
    A = io["A5"]
    wg, t3T, Hg, wut, wdt = A["wg"], A["t3T"], A["Hg"], A["wut"], A["wdt"]
    q0, q1, q2, y0, y1 = io["q0"], io["q1"], io["q2"], io["y0"], io["y1"]
    scr, st = io["scr"], io["st"]
    sstg, ustg = scr["sstg"], scr["ustg"]
    _takeover(p, ["zwg", "zt3", "zHg", "wut0", "wut1", "wut2", "wut3",
               "wdt0", "wdt1", "wdt2", "wdt3", "castg0", "castg1"],
              ["zq", "zbq", "zv", "zvn", "zWpT"])
    if b == 0:
        stg = scr["castg"]
        i = 0
        for wname, dst, nrc in (("wgT", io["wg_bf"], 8), ("wuT", io["wu_bf"], 8),
                                ("wdT", io["wd_bf"], 24)):
            src = io[wname]
            for rc in range(nrc):
                w = src.shape[1]
                sl, slr = (i % 2), f"castg{i % 2}"
                i += 1
                p.dma("pool", stg[:, sl, 0:w], src[rc * P:(rc + 1) * P, :],
                      writes=[slr])
                p.dma("pool", dst[rc * P:(rc + 1) * P, :], stg[:, sl, 0:w],
                      reads=[slr], writes=["d_wbf"])
    p.dma("sp", wg[:], io["wg_bf"].rearrange("(c p) m -> p c m", p=P),
          reads=["d_wbf"], writes=["zwg"])
    p.dma("sp", t3T[:], io["tTbuf"], reads=["d_tT"], writes=["zt3"])
    groups = [list(range(g * 3, min(g * 3 + 3, NB))) for g in range(6)]
    for grp in groups:
        glen = len(grp) * P
        t0 = grp[0] * P
        for hc in range(24):
            wt = wut[hc % 4]
            p.dma("sp", wt[:], io["wu_bf"][:, hc * P:(hc + 1) * P]
                  .rearrange("(c p) m -> p c m", p=P), reads=["d_wbf"],
                  writes=[f"wut{hc % 4}"])
            for k in range(8):
                p.op("tensor", lambda t, k=k, hc=hc: t.matmul(
                    q0[:, 0:glen], wg[:, k, hc * P:(hc + 1) * P],
                    t3T[:, k, t0:t0 + glen], start=(k == 0), stop=(k == 7)),
                    reads=["zwg", "zt3"], writes=["pq0"])
            for k in range(8):
                p.op("tensor", lambda t, k=k, wt=wt: t.matmul(
                    q1[:, 0:glen], wt[:, k, :],
                    t3T[:, k, t0:t0 + glen], start=(k == 0), stop=(k == 7)),
                    reads=[f"wut{hc % 4}", "zt3"], writes=["pq1"])
            p.op("scalar", lambda s: s.activation(sstg[:, 0:glen], q0[:, 0:glen],
                                                  AF.Silu, scale=p.cst(S127)),
                 reads=["pq0"], writes=["sstg"])
            p.op("scalar", lambda s: s.copy(ustg[:, 0:glen], q1[:, 0:glen]),
                 reads=["pq1"], writes=["ustg"])
            p.op("vector", lambda v, hc=hc: v.tensor_mul(
                Hg[:, hc, 0:glen], sstg[:, 0:glen], ustg[:, 0:glen]),
                reads=["sstg", "ustg"], writes=["zHg"])
        banks = [(y0, "py0"), (y1, "py1")]
        for hc in range(24):
            wt = wdt[hc % 4]
            p.dma("sp", wt[:], io["wd_bf"][hc * P:(hc + 1) * P, :],
                  reads=["d_wbf"], writes=[f"wdt{hc % 4}"])
            for mi, tc in enumerate(grp):
                m0 = (tc - grp[0]) * P
                if mi < 2:
                    ps, pres = banks[mi]
                    for h2 in range(2):
                        p.op("tensor", lambda t, hc=hc, m0=m0, h2=h2, ps=ps, wt=wt:
                             t.matmul(ps[:, h2 * 512:(h2 + 1) * 512],
                                      Hg[:, hc, m0:m0 + P],
                                      wt[:, h2 * 512:(h2 + 1) * 512],
                                      start=(hc == 0), stop=(hc == 23)),
                             reads=["zHg", f"wdt{hc % 4}"], writes=[pres])
                else:
                    for h2, (psh, presh) in enumerate(((q0, "pq0"), (q1, "pq1"))):
                        p.op("tensor", lambda t, hc=hc, m0=m0, h2=h2, psh=psh, wt=wt:
                             t.matmul(psh[:],
                                      Hg[:, hc, m0:m0 + P],
                                      wt[:, h2 * 512:(h2 + 1) * 512],
                                      start=(hc == 0), stop=(hc == 23)),
                             reads=["zHg", f"wdt{hc % 4}"], writes=[presh])
        items = []
        for mi, tc in enumerate(grp):
            p.dma("sp", scr["xold"][tc % 4][:],
                  io["X3buf"][b, tc * P:(tc + 1) * P, :], reads=["d_X3"],
                  writes=[f"xold{tc % 4}"])
            if mi < 2:
                ps, pres = banks[mi]
                items.append((tc, [(ps[:], pres, 0, D)]))
            else:
                items.append((tc, [(q0[:], "pq0", 0, 512),
                                   (q1[:], "pq1", 512, D)]))
        cfg = dict(io["bcfg"])
        cfg["alpha"] = ("const", S2_FFN, S2_FFN / D, (S2_FFN * S2_FFN) / D)
        cfg["xnew_dram"] = lambda b_, tc_: io["OUT"][b_, tc_ * P:(tc_ + 1) * P, :]
        cfg["xnew_res"] = "d_OUT"
        cfg["next"] = None
        emit_boundary_group(p, cfg, b, items)


def emit_boundary_chunk_2part(p, cfg, b, tc, parts):
    scr = cfg["scr"]
    u = scr["u"]
    (a0, r0), (a1, r1) = parts
    p.op("vector", lambda v: v.tensor_copy(u[:, 0:512], a0), reads=[r0],
         writes=["bscr_u"])
    p.op("vector", lambda v: v.tensor_copy(u[:, 512:1024], a1), reads=[r1],
         writes=["bscr_u"])
    cfg = dict(cfg)
    cfg["pre_add"] = None
    cfg["_u_ready"] = True
    emit_boundary_chunk(p, cfg, b, tc, u[:], "bscr_u")



def emit_l0_attn(p, b, io):
    A = io["A1"]
    zA, cqT, ckvT = A["zA"], A["cqT"], A["ckvT"]
    wdq, wdkv, wtile = A["wdq"], A["wdkv"], A["wtile"]
    ropeC, ropeS = A["ropeC"], A["ropeS"]
    vaug, qTt = A["vaug"], A["qTt"]
    q0, q1, q2, y0, y1 = io["q0"], io["q1"], io["q2"], io["y0"], io["y1"]
    scr, st = io["scr"], io["st"]

    _takeover(p, ["zB1", "zB2", "cqT", "ckvT", "vaug", "qTt", "aoTt"],
              ["w1t0", "w1t1", "w1t2", "w1t3", "w2t0", "w2t1", "w2t2", "w2t3",
               "Hg", "ropeCS", "P01", "WbrT", "wtile", "woutT"])
    p.dma("pool", wdq[:], io["wdqT"].rearrange("(c p) m -> p c m", p=P),
          writes=["zB1"])
    p.dma("pool", wdkv[:], io["wdkvT"].rearrange("(c p) m -> p c m", p=P),
          writes=["zB2"])
    for wres, wsb, dst, dres in (("zB1", wdq, cqT, "cqT"),
                                 ("zB2", wdkv, ckvT, "ckvT")):
        for oc in range(4):
            for tt in range(4):
                ps, pres = ((q0, "pq0") if (oc * 4 + tt) % 2 == 0 else (q1, "pq1"))
                for k in range(8):
                    p.op("tensor", lambda t, k=k, oc=oc, tt=tt, ps=ps, wsb=wsb:
                         t.matmul(ps[:], wsb[:, k, oc * P:(oc + 1) * P],
                                  zA[:, k, tt * 512:(tt + 1) * 512],
                                  start=(k == 0), stop=(k == 7)),
                         reads=[wres, "zA"], writes=[pres])
                p.op("scalar", lambda s, oc=oc, tt=tt, ps=ps, dst=dst:
                     s.copy(dst[:, oc, tt * 512:(tt + 1) * 512], ps[:]),
                     reads=[pres], writes=[dres])

    _takeover(p, ["wtile"], ["zB1"])
    _takeover(p, ["ropeCS", "P01"], ["zB2"])
    p.op("pool", lambda g: g.memset(vaug[:, :, :, 64:65], 1.0), writes=["vaug"])
    p.dma("pool", wtile[:], io["wupvT"].rearrange("(c p) m -> p c m", p=P),
          writes=["wtile"])
    for tc in range(NB):
        for k in range(4):
            for h2 in range(2):
                p.op("tensor", lambda t, k=k, tc=tc, h2=h2:
                     t.matmul(y0[:, h2 * 512:(h2 + 1) * 512],
                              ckvT[:, k, tc * P:(tc + 1) * P],
                              wtile[:, k, h2 * 512:(h2 + 1) * 512],
                              start=(k == 0), stop=(k == 3)),
                     reads=["ckvT", "wtile"], writes=["py0"])
        p.op("scalar", lambda sE, tc=tc: sE.copy(
            vaug[:, tc, :, 0:64],
            y0[:].rearrange("p (h d) -> p h d", h=16)),
            reads=["py0"], writes=["vaug"])

    p.dma("sp", ropeC[:], io["ropeC"], writes=["ropeCS"])
    p.dma("sp", ropeS[:], io["ropeS"], writes=["ropeCS"])

    def qk_produce(cT, cres, dst, dres, tcs, qtile_mode):
        for tc in tcs:
            yb, ybres = y0, "py0"
            for k in range(4):
                for h2 in range(2):
                    p.op("tensor", lambda t, k=k, tc=tc, h2=h2, cT=cT, yb=yb:
                         t.matmul(yb[:, h2 * 512:(h2 + 1) * 512],
                                  cT[:, k, tc * P:(tc + 1) * P],
                                  wtile[:, k, h2 * 512:(h2 + 1) * 512],
                                  start=(k == 0), stop=(k == 3)),
                         reads=[cres, "wtile"], writes=[ybres])
            yv = yb[:].rearrange("p (h e i) -> p h e i", h=16, e=2)
            E, O = yv[:, :, 0, :], yv[:, :, 1, :]
            cb = ropeC[:, tc, :]
            sb_ = ropeS[:, tc, :]
            Cb = bass.AP(cb.tensor, cb.offset, [list(cb.ap[0]), [0, 16], [1, 32]])
            Sb = bass.AP(sb_.tensor, sb_.offset, [list(sb_.ap[0]), [0, 16], [1, 32]])
            r1v = scr["rope1"][:].rearrange("p (h i) -> p h i", h=16)
            r2v = scr["rope2"][:].rearrange("p (h i) -> p h i", h=16)
            stg = scr["qkstage"]
            sv = stg[:].rearrange("p (h e i) -> p h e i", h=16, e=2)
            p.op("vector", lambda v: v.tensor_mul(r1v, E, Cb),
                 reads=["py0", "ropeCS"], writes=["rope1"])
            p.op("vector", lambda v: v.tensor_mul(r2v, O, Sb),
                 reads=["py0", "ropeCS"], writes=["rope2"])
            p.op("vector", lambda v: v.tensor_sub(sv[:, :, 0, :], r1v, r2v),
                 reads=["rope1", "rope2"], writes=["qkstage"])
            p.op("vector", lambda v: v.tensor_mul(r1v, E, Sb),
                 reads=["py0", "ropeCS"], writes=["rope1"])
            p.op("vector", lambda v: v.tensor_mul(r2v, O, Cb),
                 reads=["py0", "ropeCS"], writes=["rope2"])
            p.op("vector", lambda v: v.tensor_add(sv[:, :, 1, :], r1v, r2v),
                 reads=["rope1", "rope2"], writes=["qkstage"])
            for dc in range(8):
                p.op("tensor", lambda t, dc=dc: t.transpose(
                    io["tr"][:, dc * P:(dc + 1) * P],
                    stg[:, dc * P:(dc + 1) * P], scr["ident"][:]),
                    reads=["qkstage", "c_ident"], writes=["ptr"])
            ti = (tc % 4) if qtile_mode else tc
            p.op("vector", lambda v, ti=ti: v.tensor_copy(
                dst[:, :, ti * P:(ti + 1) * P],
                io["tr"][:].rearrange("p (c n) -> p c n", c=8)),
                reads=["ptr"], writes=[dres])

    p.dma("pool", wtile[:], io["wupkT"].rearrange("(c p) m -> p c m", p=P),
          writes=["wtile"])
    qk_produce(ckvT, "ckvT", zA, "zA", range(NB), False)

    p.dma("pool", wtile[:], io["wupqT"].rearrange("(c p) m -> p c m", p=P),
          writes=["wtile"])
    ones64 = scr["ones64"]
    for qt in range(4):
        qk_produce(cqT, "cqT", qTt, "qTt", range(4 * qt, 4 * qt + 4), True)
        for hp in range(8):
            P01 = scr["P01"]
            for kc2 in range(NB // 2):
                for sub in range(2):
                    kc = 2 * kc2 + sub
                    Sp = y0 if sub == 0 else y1
                    Sres = "py0" if sub == 0 else "py1"
                    for hloc in range(2):
                        base = 64 * hloc
                        p.op("tensor", lambda t, kc=kc, hp=hp, base=base,
                             Sp=Sp, hloc=hloc: t.matmul(
                                 Sp[:, hloc * 512:(hloc + 1) * 512],
                                 zA[base:base + 64, hp, kc * P:(kc + 1) * P],
                                 qTt[base:base + 64, hp, :],
                                 start=True, stop=True),
                             reads=["zA", "qTt"], writes=[Sres])
                p.op("scalar", lambda sE: sE.activation(
                    P01[:], io["yS"][:], AF.Exp, scale=p.cst(0.125)),
                    reads=["py0", "py1"], writes=["P01"])
                for sub in range(2):
                    kc = 2 * kc2 + sub
                    for hloc, (avp, avres) in enumerate(((q0, "pq0"),
                                                         (q1, "pq1"))):
                        p.op("tensor", lambda t, kc=kc, hp=hp, hloc=hloc,
                             sub=sub, avp=avp: t.matmul(
                                 avp[0:65, :], vaug[:, kc, 2 * hp + hloc, :],
                                 P01[:, sub * D + hloc * 512:
                                     sub * D + (hloc + 1) * 512],
                                 start=(kc == 0), stop=(kc == NB - 1)),
                             reads=["vaug", "P01"], writes=[avres])
            for hloc, (avp, avres) in enumerate(((q0, "pq0"), (q1, "pq1"))):
                srow, rrow, rb = scr["srow"], scr["rrow"], scr["rbsb"]
                aostg = scr["aostg"]
                p.op("vector", lambda v, avp=avp: v.tensor_copy(
                    srow[:], avp[64:65, :]),
                    reads=[avres], writes=["srow"])
                p.op("vector", lambda v: v.reciprocal(rrow[:], srow[:]),
                     reads=["srow"], writes=["rrow"])
                p.op("tensor", lambda t: t.matmul(q2[0:64, :], ones64[:],
                                                  rrow[:], start=True, stop=True),
                     reads=["rrow", "ones64"], writes=["pq2"])
                p.op("vector", lambda v: v.tensor_copy(rb[:], q2[0:64, :]),
                     reads=["pq2"], writes=["rbsb"])
                p.op("vector", lambda v, avp=avp: v.tensor_mul(
                    aostg[:], avp[0:64, :], rb[:]),
                    reads=[avres, "rbsb"], writes=["aostg"])
                p.dma("pool", io["aoTbuf"][hloc * 64:(hloc + 1) * 64, hp,
                                         qt * 512:(qt + 1) * 512],
                      aostg[:], reads=["aostg"], writes=["d_ao"])
    _takeover(p, ["woutT"], ["wtile", "ropeCS", "P01"])
    woutT = A["woutT"]
    aotile = scr["aotile"]
    p.dma("pool", woutT[:], io["woutT"].rearrange("(c p) m -> p c m", p=P),
          writes=["woutT"])
    for tcg in range(0, NB, 2):
        items = []
        for tc in (tcg, tcg + 1):
            aot = aotile[tc % 2]
            p.dma("sp", aot[:], io["aoTbuf"][:, :, tc * P:(tc + 1) * P],
                  reads=["d_ao"], writes=[f"aotile{tc % 2}"])
            p.dma("sp", scr["xold"][tc % 4][:],
                  io["x_in"][b, tc * P:(tc + 1) * P, :],
                  writes=[f"xold{tc % 4}"])
            yb, ybres = (y0, "py0") if tc % 2 == 0 else (y1, "py1")
            for k in range(8):
                for h2 in range(2):
                    p.op("tensor", lambda t, k=k, h2=h2, yb=yb, aot=aot:
                         t.matmul(
                        yb[:, h2 * 512:(h2 + 1) * 512], aot[:, k, :],
                        woutT[:, k, h2 * 512:(h2 + 1) * 512],
                        start=(k == 0), stop=(k == 7)),
                        reads=[f"aotile{tc % 2}", "woutT"], writes=[ybres])
            items.append((tc, [(yb[:], ybres, 0, D)]))
        cfg = dict(io["bcfg"])
        cfg["trec"] = (io["R0"], io["NB0"])
        cfg["trec_res"] = "R0"
        cfg["xnew_dram"] = lambda b_, tc_: io["X1buf"][b_, tc_ * P:(tc_ + 1) * P, :]
        cfg["xnew_res"] = "d_X1"
        cfg["next"] = (io["R1"], io["NB1"], "R1",
                       ("dram", lambda b_, tc_: io["tTbuf"][:, :, tc_ * P:(tc_ + 1) * P]))
        emit_boundary_group(p, cfg, b, items)


def emit_l0_ffn(p, b, io):
    A = io["A2"]
    zA, Hg = A["zA"], A["Hg"]
    w1t, w2t, WrT = A["w1t"], A["w2t"], A["WrT"]
    q0, q1, q2, y0, y1 = io["q0"], io["q1"], io["q2"], io["y0"], io["y1"]
    scr, st = io["scr"], io["st"]
    _takeover(p, ["w1t0", "w1t1", "w1t2", "w1t3"], ["zB1", "woutT"])
    _takeover(p, ["w2t0", "w2t1", "w2t2", "w2t3", "Hg"],
              ["zB2", "cqT", "ckvT", "vaug", "P01", "ropeCS",
               "woutT", "qTt", "aoTt", "wtile"])
    p.dma("sp", zA[:], io["tTbuf"], reads=["d_tT"], writes=["zA"])
    if b == 0:
        p.dma("pool", WrT[:], io["WrT"].rearrange("(c p) m -> p c m", p=P),
              writes=["WrT"])
        for rc in range(32):
            cg = scr["qkstage"] if rc % 2 == 0 else scr["tT_stage"]
            cgr = "qkstage" if rc % 2 == 0 else "bscr_tstage"
            p.dma("pool", cg[:], io["W2"][rc * P:(rc + 1) * P, :], writes=[cgr])
            p.dma("pool", io["W2bf"][rc * P:(rc + 1) * P, :], cg[:],
                  reads=[cgr], writes=["d_W2bf"])
    p7 = io["p7"]
    for tc in range(NB):
        for k in range(8):
            p.op("tensor", lambda t, k=k, tc=tc: t.matmul(
                q2[:, 0:8], zA[:, k, tc * P:(tc + 1) * P], WrT[:, k, :],
                start=(k == 0), stop=(k == 7)),
                reads=["zA", "WrT"], writes=["pq2"])
        nmx, e8, ssum, rs = st.col(2), scr["e8"], st.col(3), st.col(4)
        p.op("vector", lambda v: v.reduce_max(nmx, q2[:, 0:8], axis=AX.X,
                                              negate=True),
             reads=["pq2"], writes=["st_m"])
        p.op("scalar", lambda s: s.activation(e8[:], q2[:, 0:8], AF.Exp, bias=nmx),
             reads=["pq2", "st_m"], writes=["e8"])
        p.op("vector", lambda v: v.reduce_sum(ssum, e8[:], axis=AX.X),
             reads=["e8"], writes=["st_e2"])
        p.op("vector", lambda v: v.reciprocal(rs, ssum),
             reads=["st_e2"], writes=["st_var"])
        p.op("vector", lambda v, tc=tc: v.tensor_mul(p7[:, tc:tc + 1],
                                                     e8[:, 7:8], rs),
             reads=["e8", "st_var"], writes=["p7"])
    groups = [list(range(g * 3, min(g * 3 + 3, NB))) for g in range(6)]
    for grp in groups:
        glen = len(grp) * P
        t0 = grp[0] * P
        for hc in range(32):
            wt = w1t[hc % 4]
            gb, gres = (q0, "pq0") if hc % 2 == 0 else (q1, "pq1")
            p.dma("pool", wt[:], io["W1"][:, hc * P:(hc + 1) * P]
                  .rearrange("(c p) m -> p c m", p=P), writes=[f"w1t{hc % 4}"])
            for k in range(8):
                p.op("tensor", lambda t, k=k, wt=wt, gb=gb: t.matmul(
                    gb[:, 0:glen], wt[:, k, :], zA[:, k, t0:t0 + glen],
                    start=(k == 0), stop=(k == 7)),
                    reads=[f"w1t{hc % 4}", "zA"], writes=[gres])
            p.op("scalar", lambda s, hc=hc, gb=gb: s.activation(
                Hg[:, hc, 0:glen], gb[:, 0:glen], AF.Gelu,
                bias=io["b1c_sb"][:, hc:hc + 1]),
                reads=[gres, "c_b1c"], writes=["Hg"])
        banks = [(y0, "py0"), (y1, "py1")]
        for hc in range(32):
            wt = w2t[hc % 4]
            p.dma("sp", wt[:], io["W2bf"][hc * P:(hc + 1) * P, :],
                  reads=["d_W2bf"], writes=[f"w2t{hc % 4}"])
            for mi, tc in enumerate(grp):
                m0 = (tc - grp[0]) * P
                if mi < 2:
                    ps, pres = banks[mi]
                    for h2 in range(2):
                        p.op("tensor", lambda t, hc=hc, m0=m0, h2=h2, ps=ps, wt=wt:
                             t.matmul(ps[:, h2 * 512:(h2 + 1) * 512],
                                      Hg[:, hc, m0:m0 + P],
                                      wt[:, h2 * 512:(h2 + 1) * 512],
                                      start=(hc == 0), stop=(hc == 31)),
                             reads=["Hg", f"w2t{hc % 4}"], writes=[pres])
                else:
                    for h2, (psh, presh) in enumerate(((q0, "pq0"), (q1, "pq1"))):
                        p.op("tensor", lambda t, hc=hc, m0=m0, psh=psh, wt=wt,
                             h2=h2: t.matmul(psh[:], Hg[:, hc, m0:m0 + P],
                                             wt[:, h2 * 512:(h2 + 1) * 512],
                                             start=(hc == 0), stop=(hc == 31)),
                             reads=["Hg", f"w2t{hc % 4}"], writes=[presh])
        items = []
        for mi, tc in enumerate(grp):
            p.dma("sp", scr["xold"][tc % 4][:],
                  io["X1buf"][b, tc * P:(tc + 1) * P, :], reads=["d_X1"],
                  writes=[f"xold{tc % 4}"])
            if mi < 2:
                ps, pres = banks[mi]
                items.append((tc, [(ps[:], pres, 0, D)]))
            else:
                items.append((tc, [(q0[:], "pq0", 0, 512),
                                   (q1[:], "pq1", 512, D)]))
        cfg = dict(io["bcfg"])
        cfg["pre_add"] = (io["b2B_sb"][:], "c_b2B")
        cfg["alpha"] = ("col", p7[:, grp[0]:grp[0] + len(grp)], "p7")
        cfg["xnew_dram"] = lambda b_, tc_: io["X2out"][b_, tc_ * P:(tc_ + 1) * P, :]
        cfg["xnew_res"] = "d_X2o"
        cfg["next"] = (io["R2"], io["NB2"], "R2",
                       ("dram", lambda b_, tc_: io["tTbuf"][:, :, tc_ * P:(tc_ + 1) * P]))
        emit_boundary_group(p, cfg, b, items)


def emit_l1_branches(p, b, io):
    A = io["A3"]
    zA, WbrT = A["zA"], A["WbrT"]
    q0, q1 = io["q0"], io["q1"]
    scr, st = io["scr"], io["st"]
    statsA = io["statsA"]
    hstage = scr["hstage"]
    _takeover(p, ["WbrT"], ["w1t0", "w1t1", "w1t2", "w1t3", "w2t0", "w2t1", "w2t2", "w2t3"])
    p.dma("sp", zA[:], io["tTbuf"], reads=["d_tT"], writes=["zA"])
    for ibr, wname in enumerate(("WqT", "WkT", "WvT")):
        p.dma("pool", WbrT[:], io[wname].rearrange("(c p) m -> p c m", p=P),
              writes=["WbrT"])
        for ec in range(8):
            for tt in range(4):
                ps, pres = ((q0, "pq0") if tt % 2 == 0 else (q1, "pq1"))
                for k in range(8):
                    p.op("tensor", lambda t, k=k, ec=ec, tt=tt, ps=ps: t.matmul(
                        ps[:], WbrT[:, k, ec * P:(ec + 1) * P],
                        zA[:, k, tt * 512:(tt + 1) * 512],
                        start=(k == 0), stop=(k == 7)),
                        reads=["WbrT", "zA"], writes=[pres])
                su_t, sq_t = st.col(2), st.col(3)
                p.op("scalar", lambda s, ps=ps: s.activation(
                    hstage[:], ps[:], AF.Identity, accum_out=su_t),
                    reads=[pres], writes=["hstage", "st_m"])
                p.op("scalar", lambda s, ps=ps: s.activation(
                    scr["sqj512"][:], ps[:], AF.Square, accum_out=sq_t),
                    reads=[pres], writes=["sqj512", "st_e2"])
                p.op("vector", lambda v, ibr=ibr, ec=ec: v.tensor_add(
                    statsA[:, ibr, ec, 0:1], statsA[:, ibr, ec, 0:1], su_t),
                    reads=["st_m"], writes=["statsA"])
                p.op("vector", lambda v, ibr=ibr, ec=ec: v.tensor_add(
                    statsA[:, ibr, ec, 1:2], statsA[:, ibr, ec, 1:2], sq_t),
                    reads=["st_e2"], writes=["statsA"])
                p.dma("pool", io["h_out"][ibr, b, ec, :, tt * 512:(tt + 1) * 512],
                      hstage[:], reads=["hstage"])



def _mk_scr_common(p):
    scr = {}
    scr["xold"] = [p.sbuf(f"xold{i}", (P, D), F32) for i in range(4)]
    scr["u4"] = p.sbuf("scr_u4", (P, 3, D), F32)
    scr["xn4"] = p.sbuf("scr_xn4", (P, 3, D), BF16)
    scr["SU"] = p.sbuf("scr_SU", (P, NB), F32)
    scr["SQ"] = p.sbuf("scr_SQ", (P, NB), F32)
    scr["stG"] = p.sbuf("scr_stG", (P, 6, 4), F32)
    scr["u"] = p.sbuf("scr_u", (P, D), F32)
    scr["xnew"] = p.sbuf("scr_xnew", (P, D), F32)
    scr["tn"] = p.sbuf("scr_tn", (P, D), BF16)
    scr["sqj"] = p.sbuf("scr_sqj", (P, D), BF16)
    scr["tT_stage"] = p.sbuf("scr_tstage", (P, D), BF16)
    scr["ident"] = p.sbuf("identsb", (P, P), BF16)
    return scr


def _load_setup(p, scr, io):
    tok1 = p.dma("sp", p.consts_tile[:], io["constv"], writes=["c_const"])
    tok2 = p.dma("sp", scr["ident"][:], io["ident"], writes=["c_ident"])
    for e in ENGS:
        p._wait(("eng", e), tok1)
        p._wait(("eng", e), tok2)


def build_launchA(debug=False):
    nc = bass.Bass("TRN2", target_bir_lowering=False)
    io = {}
    di = nc.dram_tensor
    io["x_in"] = di("x", [NBATCH, S, D], F32, kind="ExternalInput").ap()
    for nm, shp, dt in (("wdqT", [D, 512], F32), ("wdkvT", [D, 512], F32),
                        ("wupqT", [512, D], F32), ("wupkT", [512, D], F32),
                        ("wupvT", [512, D], F32), ("woutT", [D, D], F32),
                        ("lnsB", [P, D], F32), ("lnbB", [P, D], F32),
                        ("ropeC", [P, NB, 32], F32), ("ropeS", [P, NB, 32], F32),
                        ("WrT", [D, 8], F32), ("W1", [D, 4096], F32),
                        ("b1c", [P, 32], F32), ("W2", [4096, D], F32),
                        ("b2B", [P, D], F32), ("WqT", [D, D], F32),
                        ("WkT", [D, D], F32), ("WvT", [D, D], F32),
                        ("ident", [P, P], BF16),
                        ("constv", [P, len(CONSTS)], F32)):
        io[nm] = di(nm, shp, dt, kind="ExternalInput").ap()
    io["X2out"] = di("X2out", [NBATCH, S, D], F32, kind="ExternalOutput").ap()
    io["h_out"] = di("h_out", [3, NBATCH, 8, P, S], BF16, kind="ExternalOutput").ap()
    io["stats_out"] = di("stats_out", [P, 48], F32, kind="ExternalOutput").ap()
    io["X1buf"] = di("X1buf", [NBATCH, S, D], F32, kind="Internal").ap()
    if debug:
        io["dbg_X1"] = di("dbg_X1", [NBATCH, S, D], F32, kind="ExternalOutput").ap()
        io["dbg_ao"] = di("dbg_ao", [P, 8, S], BF16, kind="ExternalOutput").ap()
        io["dbg_t1"] = di("dbg_t1", [P, 8, S], BF16, kind="ExternalOutput").ap()
        io["dbg_p7"] = di("dbg_p7", [P, NB], F32, kind="ExternalOutput").ap()
        io["dbg_W2bf"] = di("dbg_W2bf", [4096, D], BF16, kind="ExternalOutput").ap()
    io["tTbuf"] = di("tTbuf", [P, 8, S], BF16, kind="Internal").ap()
    io["aoTbuf"] = di("aoTbuf", [P, 8, S], BF16, kind="Internal").ap()
    io["W2bf"] = di("W2bf", [4096, D], BF16, kind="Internal").ap()

    p = Prog(nc)
    p.setup_consts()
    scr = _mk_scr_common(p)
    scr["trec"] = p.sbuf("scr_trec", (P, D), F32)
    scr["qkstage"] = p.sbuf("scr_qkst", (P, D), BF16)
    scr["rope1"] = p.sbuf("scr_rope1", (P, 512), F32)
    scr["rope2"] = p.sbuf("scr_rope2", (P, 512), F32)
    scr["e8"] = p.sbuf("scr_e8", (P, 8), F32)
    scr["srow"] = p.sbuf("scr_srow", (1, 512), F32)
    scr["rrow"] = p.sbuf("scr_rrow", (1, 512), F32)
    scr["rbsb"] = p.sbuf("scr_rbsb", (64, 512), F32)
    scr["ones64"] = p.sbuf("scr_ones64", (1, 64), F32)
    scr["aotile"] = [p.sbuf("scr_aotile0", (P, 8, P), BF16),
                     p.sbuf("scr_aotile1", (P, 8, P), BF16)]
    scr["aostg"] = p.sbuf("scr_aostg", (64, 512), BF16)
    scr["hstage"] = p.sbuf("scr_hstage", (P, 512), BF16)
    scr["sqj512"] = p.sbuf("scr_sqj512", (P, 512), BF16)
    io["lnsB_sb"] = p.sbuf("lnsB_sb", (P, D), F32)
    io["lnbB_sb"] = p.sbuf("lnbB_sb", (P, D), F32)
    io["b1c_sb"] = p.sbuf("b1c_sb", (P, 32), F32)
    io["b2B_sb"] = p.sbuf("b2B_sb", (P, D), F32)
    st = TinyStats(p, "tinyst", 8)
    for nm in ("R0", "NB0", "R1", "NB1", "R2", "NB2"):
        io[nm] = p.sbuf(nm, (P, NB), F32)
    io["p7"] = p.sbuf("p7arr", (P, NB), F32)
    io["statsA"] = p.sbuf("statsA", (P, 3, 8, 2), F32)

    AR = 123392
    arena = nc.alloc_sbuf_tensor("arena", [P, AR], mybir.dt.uint8)
    base = nc.lookup_mloc(arena).addr
    at = lambda nm, off, shp, dt: nc.alloc_sbuf_tensor_at(nm, list(shp), dt,
                                                          offset=base + off)
    A1 = {}
    A1["zA"] = at("zA_t", 0, (P, 8, S), BF16)
    A1["wdq"] = at("wdq_t", 32768, (P, 8, 512), BF16)
    A1["woutT"] = at("woutT_t", 32768, (P, 8, D), BF16)
    A1["wtile"] = at("wtile_t", 32768, (P, 4, D), BF16)
    A1["wdkv"] = at("wdkv_t", 40960, (P, 8, 512), BF16)
    A1["ropeC"] = at("ropeC_t", 40960, (P, NB, 32), F32)
    A1["ropeS"] = at("ropeS_t", 43008, (P, NB, 32), F32)
    scr["P01"] = at("P01_t", 45056, (P, 2 * D), BF16)
    A1["cqT"] = at("cqT_t", 49152, (P, 4, S), BF16)
    A1["ckvT"] = at("ckvT_t", 65536, (P, 4, S), BF16)
    A1["vaug"] = at("vaug_t", 81920, (P, NB, 16, 65), BF16)
    A1["qTt"] = at("qTt_t", 115200, (P, 8, 512), BF16)
    A2 = {"zA": A1["zA"]}
    A2["w1t"] = [at(f"w1t{i}_t", 32768 + 2048 * i, (P, 8, P), BF16)
                 for i in range(4)]
    A2["w2t"] = [at(f"w2t{i}_t", 40960 + 2048 * i, (P, D), BF16)
                 for i in range(4)]
    A2["Hg"] = at("Hg_t", 49152, (P, 32, 384), BF16)
    A2["WrT"] = p.sbuf("WrT_sb", (P, 8, 8), BF16)
    A3 = {"zA": A1["zA"], "WbrT": at("WbrT_t", 32768, (P, 8, D), BF16)}
    io["A1"], io["A2"], io["A3"] = A1, A2, A3

    yS = p.psum("yS", (P, 2 * D))
    y0 = TView(yS, 0); y1 = TView(yS, D)
    q0 = p.psum("q0", (P, 512)); q1 = p.psum("q1", (P, 512))
    q2 = p.psum("q2", (P, 512))
    tr = p.psum("trp", (P, D), BF16)
    io.update(q0=q0, q1=q1, q2=q2, y0=y0, y1=y1, yS=yS, tr=tr)
    io["scr"], io["st"] = scr, st
    io["bcfg"] = {"scr": scr, "st": st, "out_q": "pool", "tr_psum": tr,
                  "stG": scr["stG"]}

    with nc.Block():
        _load_setup(p, scr, io)
        p.dma("sp", io["lnsB_sb"][:], io["lnsB"], writes=["c_lnsB"])
        p.dma("sp", io["lnbB_sb"][:], io["lnbB"], writes=["c_lnbB"])
        p.dma("sp", io["b1c_sb"][:], io["b1c"], writes=["c_b1c"])
        p.dma("sp", io["b2B_sb"][:], io["b2B"], writes=["c_b2B"])
        p.op("pool", lambda g: g.memset(scr["ones64"][:], 1.0), writes=["ones64"])
        p.op("pool", lambda g: g.memset(io["statsA"][:], 0.0), writes=["statsA"])
        for b in range(NBATCH):
            cfg_pre = {"scr": scr, "st": st, "lnsB": io["lnsB_sb"],
                       "lnbB": io["lnbB_sb"], "tr_psum": tr}
            emit_pre_ln_from_dram(
                p, cfg_pre, b,
                lambda b_, tc: io["x_in"][b_, tc * P:(tc + 1) * P, :],
                io["R0"], io["NB0"], "R0", ("sbuf", A1["zA"], "zA"),
                inner=(io["lnsB_sb"], io["lnbB_sb"]))
            emit_l0_attn(p, b, io)
            if debug and b == 0:
                p.dma("sp", io["dbg_ao"], io["aoTbuf"], reads=["d_ao"])
                p.dma("sp", io["dbg_t1"], io["tTbuf"], reads=["d_tT"])
                p.dma("sp", io["dbg_X1"][0], io["X1buf"][0], reads=["d_X1"])
            emit_l0_ffn(p, b, io)
            if debug and b == 0:
                p.dma("sp", io["dbg_p7"], io["p7"][:], reads=["p7"])
                p.dma("sp", io["dbg_W2bf"], io["W2bf"], reads=["d_W2bf"])
            emit_l1_branches(p, b, io)
        p.dma("sp", io["stats_out"],
              io["statsA"][:].rearrange("p a b c -> p (a b c)"),
              reads=["statsA"])
        p.finish()
    return nc, p


def build_launchB(debug=False):
    nc = bass.Bass("TRN2", target_bir_lowering=False)
    io = {}
    di = nc.dram_tensor
    io["X2in"] = di("X2in", [NBATCH, S, D], F32, kind="ExternalInput").ap()
    io["h_in"] = di("h_in", [3, NBATCH, 8, P, S], BF16, kind="ExternalInput").ap()
    io["allstats"] = di("allstats", [P, 48, 8], F32, kind="ExternalInput").ap()
    for nm, shp, dt in (("bn_gam", [P, 3, 8], F32), ("bn_bet", [P, 3, 8], F32),
                        ("WpT", [D, D], F32), ("bpB", [P, D], F32),
                        ("wgT", [D, 3072], I32), ("wuT", [D, 3072], I32),
                        ("wdT", [3072, D], I32), ("ident", [P, P], BF16),
                        ("constv", [P, len(CONSTS)], F32)):
        io[nm] = di(nm, shp, dt, kind="ExternalInput").ap()
    io["OUT"] = di("OUT", [NBATCH, S, D], F32, kind="ExternalOutput").ap()
    if debug:
        io["dbg_sc"] = di("dbg_sc", [P, 48], F32, kind="ExternalOutput").ap()
        io["dbg_X3"] = di("dbg_X3", [NBATCH, S, D], F32, kind="ExternalOutput").ap()
        io["dbg_oT"] = di("dbg_oT", [P, 8, S], BF16, kind="ExternalOutput").ap()
        io["dbg_t3"] = di("dbg_t3", [P, 8, S], BF16, kind="ExternalOutput").ap()
    io["X3buf"] = di("X3buf", [NBATCH, S, D], F32, kind="Internal").ap()
    io["tTbuf"] = di("tTbufB", [P, 8, S], BF16, kind="Internal").ap()
    io["outTbuf"] = di("outTbuf", [P, 8, S], BF16, kind="Internal").ap()
    io["wg_bf"] = di("wg_bf", [D, 3072], BF16, kind="Internal").ap()
    io["wu_bf"] = di("wu_bf", [D, 3072], BF16, kind="Internal").ap()
    io["wd_bf"] = di("wd_bf", [3072, D], BF16, kind="Internal").ap()

    p = Prog(nc)
    p.setup_consts()
    scr = _mk_scr_common(p)
    scr["stg2"] = p.sbuf("scr_stg2", (P, S), BF16)
    scr["sstg"] = p.sbuf("scr_sstg", (P, 512), BF16)
    scr["ustg"] = p.sbuf("scr_ustg", (P, 512), BF16)
    scr["vv_sb"] = p.sbuf("scr_vv", (P, 8, 64), BF16)
    scr["aostg"] = p.sbuf("scr_aostgB", (64, 512), BF16)
    scr["otile"] = [p.sbuf("scr_otile0", (P, 8, P), BF16),
                    p.sbuf("scr_otile1", (P, 8, P), BF16)]
    scr["SSin"] = p.sbuf("scr_ssin", (P, 48, 8), F32)
    scr["SS"] = p.sbuf("scr_ss", (P, 48), F32)
    for nm in ("mu24", "e224", "var24", "r24", "gr24", "gamB", "betB"):
        scr[nm] = p.sbuf("scr_" + nm, (P, 24), F32)
    io["scB"] = p.sbuf("scB", (P, 24), F32)
    io["biB"] = p.sbuf("biB", (P, 24), F32)
    st = TinyStats(p, "tinystB", 8)
    for nm in ("R3", "NB3"):
        io[nm] = p.sbuf(nm, (P, NB), F32)
    io["bpB_sb"] = p.sbuf("bpB_sb", (P, D), F32)

    AR = 131072
    arena = nc.alloc_sbuf_tensor("arenaB", [P, AR], mybir.dt.uint8)
    base = nc.lookup_mloc(arena).addr
    at = lambda nm, off, shp, dt: nc.alloc_sbuf_tensor_at(nm, list(shp), dt,
                                                          offset=base + off)
    A4 = {}
    A4["hq"] = at("hq_t", 0, (P, 8, S), BF16)
    A4["bq"] = at("bq_t", 32768, (P, 8, S), BF16)
    A4["outT"] = A4["bq"]
    A4["hv"] = at("hv_t", 65536, (P, 8, S), BF16)
    A4["WpT"] = at("WpT_t", 65536, (P, 8, D), BF16)
    A4["vn"] = at("vn_t", 98304, (P, NB, D), BF16)
    A5 = {}
    A5["wg"] = at("wg_t", 0, (P, 8, 3072), BF16)
    A5["t3T"] = at("t3T_t", 49152, (P, 8, S), BF16)
    A5["Hg"] = at("HgB_t", 81920, (P, 24, 384), BF16)
    A5["wut"] = [at(f"wut{i}_t", 100352 + 2048 * i, (P, 8, P), BF16)
                 for i in range(4)]
    A5["wdt"] = [at(f"wdt{i}_t", 108544 + 2048 * i, (P, D), BF16)
                 for i in range(4)]
    scr["castg"] = at("castg_t", 116736, (P, 2, 3072), BF16)
    io["A4"], io["A5"] = A4, A5

    yS = p.psum("yS", (P, 2 * D))
    y0 = TView(yS, 0); y1 = TView(yS, D)
    q0 = p.psum("q0", (P, 512)); q1 = p.psum("q1", (P, 512))
    q2 = p.psum("q2", (P, 512))
    tr = p.psum("trp", (P, D), BF16)
    io.update(q0=q0, q1=q1, q2=q2, y0=y0, y1=y1, yS=yS, tr=tr)
    io["scr"], io["st"] = scr, st
    io["bcfg"] = {"scr": scr, "st": st, "out_q": "pool", "tr_psum": tr,
                  "stG": scr["stG"]}

    with nc.Block():
        _load_setup(p, scr, io)
        p.dma("sp", io["bpB_sb"][:], io["bpB"], writes=["c_bpB"])
        emit_bn_finalize(p, io)
        if debug:
            p.dma("sp", io["dbg_sc"][:, 0:24], io["scB"][:], reads=["scB"])
            p.dma("sp", io["dbg_sc"][:, 24:48], io["biB"][:], reads=["biB"])
        for b in range(NBATCH):
            emit_l1_post(p, b, io)
            if debug and b == 0:
                p.dma("sp", io["dbg_oT"], io["outTbuf"], reads=["d_oT"])
                p.dma("sp", io["dbg_t3"], io["tTbuf"], reads=["d_tT"])
                p.dma("sp", io["dbg_X3"][0], io["X3buf"][0], reads=["d_X3"])
            emit_l1_ffn(p, b, io)
        if debug:
            p.dma("sp", io["dbg_X3"][1], io["X3buf"][1], reads=["d_X3"])
        p.finish()
    return nc, p


_CACHE = {}


def kernel(**inputs):
    g = prep_shared(inputs)
    x = np.asarray(inputs["L0_attn_x"], np.float32)
    if "A" not in _CACHE:
        _CACHE["A"] = build_launchA()
        _CACHE["B"] = build_launchB()
    ncA, _ = _CACHE["A"]
    ncB, _ = _CACHE["B"]
    xsh = x.reshape(8, NBATCH, S, D)
    keysA = ["wdqT", "wdkvT", "wupqT", "wupkT", "wupvT", "woutT", "lnsB",
             "lnbB", "ropeC", "ropeS", "WrT", "W1", "b1c", "W2", "b2B",
             "WqT", "WkT", "WvT", "ident", "constv"]
    in_mapsA = [dict({k: g[k] for k in keysA}, x=np.ascontiguousarray(xsh[c]))
                for c in range(8)]
    resA = run_bass_kernel_spmd(ncA, in_mapsA, core_ids=list(range(8)))
    allstats = np.ascontiguousarray(
        np.stack([resA.results[c]["stats_out"] for c in range(8)], axis=-1))
    keysB = ["bn_gam", "bn_bet", "WpT", "bpB", "wgT", "wuT", "wdT", "ident",
             "constv"]
    in_mapsB = [dict({k: g[k] for k in keysB},
                     X2in=resA.results[c]["X2out"],
                     h_in=resA.results[c]["h_out"],
                     allstats=allstats)
                for c in range(8)]
    resB = run_bass_kernel_spmd(ncB, in_mapsB, core_ids=list(range(8)))
    out = np.concatenate([resB.results[c]["OUT"] for c in range(8)], axis=0)
    return np.ascontiguousarray(out.reshape(16, S, D).astype(np.float32))


def emit_boundary_group(p, cfg, b, items):
    scr = cfg["scr"]
    u4, xn4 = scr["u4"], scr["xn4"]
    SU, SQ = scr["SU"], scr["SQ"]
    stG = cfg["stG"]
    G = len(items)
    g0 = items[0][0]
    cD = p.cst(1.0 / D)
    alpha = cfg.get("alpha")
    for j, (tc, parts) in enumerate(items):
        uref = f"u4_{j}"
        xo_res = f"xold{tc % 4}"
        if cfg.get("trec") is not None:
            r1, nb1 = cfg["trec"]
            tr = scr["trec"]
            p.op("scalar", lambda sE, tc=tc: sE.activation(
                tr[:], scr["xold"][tc % 4][:], AF.Identity,
                scale=r1[:, tc:tc + 1], bias=nb1[:, tc:tc + 1]),
                reads=[xo_res, cfg["trec_res"]], writes=["bscr_trec"])
            for ap, res, c0, c1 in parts:
                p.op("vector", lambda v, ap=ap, c0=c0, c1=c1, j=j: v.tensor_add(
                    u4[:, j, c0:c1], ap, tr[:, c0:c1]),
                    reads=[res, "bscr_trec"], writes=[uref])
        elif cfg.get("pre_add") is not None:
            pap, pres = cfg["pre_add"]
            for ap, res, c0, c1 in parts:
                p.op("vector", lambda v, ap=ap, c0=c0, c1=c1, j=j: v.tensor_add(
                    u4[:, j, c0:c1], ap, pap[:, c0:c1]),
                    reads=[res, pres], writes=[uref])
        else:
            for ap, res, c0, c1 in parts:
                p.op("vector", lambda v, ap=ap, c0=c0, c1=c1, j=j: v.tensor_copy(
                    u4[:, j, c0:c1], ap),
                    reads=[res], writes=[uref])
        p.op("vector", lambda v, j=j, tc=tc: v.reduce_sum(
            SU[:, tc:tc + 1], u4[:, j, :], axis=AX.X),
            reads=[uref], writes=["SU"])
        p.op("scalar", lambda sE, j=j, tc=tc: sE.activation(
            scr["sqj"][:], u4[:, j, :], AF.Square, accum_out=SQ[:, tc:tc + 1]),
            reads=[uref], writes=["SQ", "bscr_sqj"])
    m_y, e2_y, var, r = (stG[:, 0, 0:G], stG[:, 1, 0:G], stG[:, 2, 0:G],
                         stG[:, 3, 0:G])
    scA, nb_ = stG[:, 4, 0:G], stG[:, 5, 0:G]
    suv, sqv = SU[:, g0:g0 + G], SQ[:, g0:g0 + G]
    if alpha is None:
        p.op("vector", lambda v: v.tensor_scalar(m_y, suv, cD, None, ALU.mult),
             reads=["SU"], writes=["stG"])
        p.op("vector", lambda v: v.tensor_scalar(e2_y, sqv, cD, None, ALU.mult),
             reads=["SQ"], writes=["stG"])
    elif isinstance(alpha, tuple) and alpha[0] == "col":
        acol, ares = alpha[1], alpha[2]
        p.op("vector", lambda v: v.tensor_scalar(m_y, suv, cD, None, ALU.mult),
             reads=["SU"], writes=["stG"])
        p.op("vector", lambda v: v.tensor_mul(m_y, m_y, acol),
             reads=["stG", ares], writes=["stG"])
        p.op("vector", lambda v: v.tensor_scalar(e2_y, sqv, cD, None, ALU.mult),
             reads=["SQ"], writes=["stG"])
        p.op("vector", lambda v: v.tensor_mul(e2_y, e2_y, acol),
             reads=["stG", ares], writes=["stG"])
        p.op("vector", lambda v: v.tensor_mul(e2_y, e2_y, acol),
             reads=["stG", ares], writes=["stG"])
    else:
        _, ca, ca_m, ca_e = alpha
        p.op("vector", lambda v: v.tensor_scalar(m_y, suv, p.cst(ca_m), None,
                                                 ALU.mult),
             reads=["SU"], writes=["stG"])
        p.op("vector", lambda v: v.tensor_scalar(e2_y, sqv, p.cst(ca_e), None,
                                                 ALU.mult),
             reads=["SQ"], writes=["stG"])
    p.op("vector", lambda v: v.tensor_mul(var, m_y, m_y), reads=["stG"],
         writes=["stG"])
    p.op("vector", lambda v: v.tensor_sub(var, e2_y, var), reads=["stG"],
         writes=["stG"])
    p.op("scalar", lambda sE: sE.activation(r, var, AF.Ln, bias=p.cst(EPS)),
         reads=["stG"], writes=["stG"])
    p.op("scalar", lambda sE: sE.activation(r, r, AF.Exp, scale=p.cst(-0.5)),
         reads=["stG"], writes=["stG"])
    if alpha is None:
        p.op("vector", lambda v: v.tensor_copy(scA, r), reads=["stG"],
             writes=["stG"])
    elif isinstance(alpha, tuple) and alpha[0] == "col":
        p.op("vector", lambda v: v.tensor_mul(scA, r, alpha[1]),
             reads=["stG", alpha[2]], writes=["stG"])
    else:
        p.op("vector", lambda v: v.tensor_scalar(scA, r, p.cst(alpha[1]), None,
                                                 ALU.mult),
             reads=["stG"], writes=["stG"])
    p.op("vector", lambda v: v.tensor_mul(nb_, m_y, r), reads=["stG"],
         writes=["stG"])
    p.op("vector", lambda v: v.tensor_scalar(nb_, nb_, p.cst(-1.0), None,
                                             ALU.mult),
         reads=["stG"], writes=["stG"])
    nxt = cfg.get("next")
    for j, (tc, parts) in enumerate(items):
        uref = f"u4_{j}"
        xo_res = f"xold{tc % 4}"
        xnew = scr["xnew"]
        p.op("vector", lambda v, j=j: v.tensor_scalar(
            xnew[:], u4[:, j, :], stG[:, 4, j:j + 1], stG[:, 5, j:j + 1],
            ALU.mult, ALU.add),
            reads=[uref, "stG"], writes=["bscr_xnew"])
        p.op("vector", lambda v, tc=tc: v.tensor_add(
            xnew[:], xnew[:], scr["xold"][tc % 4][:]),
            reads=["bscr_xnew", xo_res], writes=["bscr_xnew"])
        p.dma(cfg.get("out_q", "pool"), cfg["xnew_dram"](b, tc), xnew[:],
              reads=["bscr_xnew"], writes=[cfg.get("xnew_res", "d_xnew")])
        if nxt is not None:
            p.op("vector", lambda v, tc=tc: v.reduce_sum(
                SU[:, tc:tc + 1], xnew[:], axis=AX.X),
                reads=["bscr_xnew"], writes=["SU"])
            p.op("scalar", lambda sE, tc=tc: sE.activation(
                scr["sqj"][:], xnew[:], AF.Square,
                accum_out=SQ[:, tc:tc + 1]),
                reads=["bscr_xnew"], writes=["SQ", "bscr_sqj"])
            p.op("vector", lambda v, j=j: v.tensor_copy(xn4[:, j, :], xnew[:]),
                 reads=["bscr_xnew"], writes=["xn4"])
    if nxt is None:
        return
    Rarr, NBarr, Rres, tT_store = nxt
    suv, sqv = SU[:, g0:g0 + G], SQ[:, g0:g0 + G]
    m2, v2 = stG[:, 0, 0:G], stG[:, 2, 0:G]
    rr, nbb = Rarr[:, g0:g0 + G], NBarr[:, g0:g0 + G]
    p.op("vector", lambda v: v.tensor_scalar(m2, suv, cD, None, ALU.mult),
         reads=["SU"], writes=["stG"])
    p.op("vector", lambda v: v.tensor_scalar(v2, sqv, cD, None, ALU.mult),
         reads=["SQ"], writes=["stG"])
    p.op("vector", lambda v: v.tensor_mul(stG[:, 1, 0:G], m2, m2),
         reads=["stG"], writes=["stG"])
    p.op("vector", lambda v: v.tensor_sub(v2, v2, stG[:, 1, 0:G]),
         reads=["stG"], writes=["stG"])
    p.op("scalar", lambda sE: sE.activation(rr, v2, AF.Ln, bias=p.cst(EPS)),
         reads=["stG"], writes=[Rres])
    p.op("scalar", lambda sE: sE.activation(rr, rr, AF.Exp, scale=p.cst(-0.5)),
         reads=[Rres], writes=[Rres])
    p.op("vector", lambda v: v.tensor_mul(nbb, m2, rr),
         reads=["stG", Rres], writes=[Rres])
    p.op("vector", lambda v: v.tensor_scalar(nbb, nbb, p.cst(-1.0), None,
                                             ALU.mult),
         reads=[Rres], writes=[Rres])
    for j, (tc, parts) in enumerate(items):
        tn = scr["tn"]
        p.op("scalar", lambda sE, j=j, tc=tc: sE.activation(
            tn[:], xn4[:, j, :], AF.Identity,
            scale=Rarr[:, tc:tc + 1], bias=NBarr[:, tc:tc + 1]),
            reads=["xn4", Rres], writes=["bscr_tn"])
        tpp = cfg["tr_psum"]
        for dc in range(8):
            p.op("tensor", lambda t, dc=dc: t.transpose(
                tpp[:, dc * P:(dc + 1) * P], tn[:, dc * P:(dc + 1) * P],
                scr["ident"][:]),
                reads=["bscr_tn", "c_ident"], writes=["ptr"])
        kind = tT_store[0]
        if kind == "sbuf":
            tT = tT_store[1]
            p.op("vector", lambda v, tc=tc: v.tensor_copy(
                tT[:, :, tc * P:(tc + 1) * P],
                tpp[:].rearrange("p (c n) -> p c n", c=8)),
                reads=["ptr"], writes=[tT_store[2]])
        else:
            stg = scr["tT_stage"]
            p.op("vector", lambda v: v.tensor_copy(stg[:], tpp[:]),
                 reads=["ptr"], writes=["bscr_tstage"])
            p.dma("pool", tT_store[1](b, tc),
                  stg[:].rearrange("p (c n) -> p c n", c=8),
                  reads=["bscr_tstage"], writes=["d_tT"])
